# version 01
# baseline (speedup 1.0000x reference)
import numpy as np
from contextlib import ExitStack
import concourse.bass as bass
import concourse.mybir as mybir
from concourse.bass_utils import run_bass_kernel_spmd

F32 = mybir.dt.float32
AF = mybir.ActivationFunctionType
OP = mybir.AluOpType
AXX = mybir.AxisListType.X

NCORE = 8
T = 2048
DM = 2048
NCH = 16
NTOK = 1024
TTW = 512
QSTARTS = [[0, 1536], [512, 1024]]
NKB = [2, 4]
KBSRC = [(0, 0), (1, 0), (1, 512), (0, 512)]
SLOTS = [(0, 0), (0, 1), (1, 0), (1, 1), (1, 2), (1, 3)]
NH = 16
ARENA = 26624

DBG = [None]

CST_ID = 0
CST_PERMH = 128
CST_PERMQ = 256
CST_PERMI = 384
CST_ROWSEL = 512
CST_ONES = 768
CST_FOLDF = 776
CST_FOLDH = 792
CST_SELC = 800
CST_ESEL = 928
CST_VECS = 1824
CST_REPR = 1920
CST_END = 2048

VGAM, VBET = 0, 1
VL0W1, VL0W2 = 2, 18
VBQ1, VBK1, VBV1, VBO1 = 34, 50, 54, 58
VEB = 74
VALP, VRB0, VRB1 = 90, 91, 92
VNVEC = 96

ATT_QTAB = 0
ATT_QB = 2048
ATT_OB = 10240
ATT_EB = 18432
ATT_MB = 22528
ATT_KV = 24576

MTS16 = [(i * 128, 128) for i in range(16)]
MTS8 = [(i * 128, 128) for i in range(8)]
MTS4 = [(i * 128, 128) for i in range(4)]


def rr(ap, s, **kw):
    return ap.rearrange(s, **kw)


class MK:
    def __init__(self, nc, block, tensors, sems):
        self.nc = nc
        self.block = block
        self.t = tensors
        self.S = sems
        self.n = {k: 0 for k in sems}
        self.nbar = 0
        self.wb_last = [0, 0]
        self.ps01_last = [None, None]

    def sec(self, eng, fn):
        getattr(self.block, eng)(fn)

    def barrier(self):
        self.nbar += 1
        tgt = self.nbar
        engs = ("tensor", "vector", "scalar", "gpsimd", "sync")
        sync_sems = ("dwa", "dwb", "dka", "dkb", "dva", "dvb", "dm", "dx", "dk")
        for eng in engs:
            def f(e, eng=eng):
                if eng == "sync":
                    for sem in sync_sems:
                        if self.n[sem]:
                            e.wait_ge(self.S[sem], self.n[sem])
                    e.nop().then_inc(self.S["bar_sync"], 1)
                elif eng == "gpsimd":
                    if self.n["dg"]:
                        e.wait_ge(self.S["dg"], self.n["dg"])
                    e.drain().then_inc(self.S["bar_gpsimd"], 1)
                else:
                    e.drain().then_inc(self.S["bar_" + eng], 1)
            self.sec(eng, f)
        for eng in engs:
            def f2(e, tgt=tgt):
                for other in engs:
                    e.wait_ge(self.S["bar_" + other], tgt)
            self.sec(eng, f2)

    def A(self, off, cols, p0=0, pn=128):
        return self.t["arena"][p0:p0 + pn, off:off + cols]

    def xap(self, kc, c0, w=TTW):
        return self.t["x"][:, kc * NTOK + c0: kc * NTOK + c0 + w]

    def cst(self, off, cols, p0=0, pn=128):
        return self.t["cst"][p0:p0 + pn, off:off + cols]

    def vec(self, col, pn=128):
        return self.t["cst"][0:pn, CST_VECS + col:CST_VECS + col + 1]

    def hsel(self, h):
        return self.t["sm1"][0:16, h * 128:(h + 1) * 128]

    def onesrow(self):
        return self.t["sm1"][0:1, 0:128]

    def rowsel(self, h):
        return self.cst(CST_ROWSEL + h * NH, NH)

    def wpanel_dst(self, slot, nkc, w):
        return self.t["wb"][:, slot * 2048: slot * 2048 + nkc * w]

    def wtile(self, slot, kc, w):
        return self.t["wb"][:, slot * 2048 + kc * w: slot * 2048 + (kc + 1) * w]

    def proj(self, W_ap, Kdim, mts, tts, rhs_fn, evac, eng="vector",
             dma_out=None, dma_per_tile=0):
        nkc = Kdim // 128
        ntl = len(mts) * len(tts)
        mmb = self.n["mm"]
        lane = ("dwa", "dwb")
        lb = [self.n["dwa"], self.n["dwb"]]
        evs = "ve" if eng == "vector" else "sc"
        evb = self.n[evs]
        dgb = self.n["dg"]
        self._gp_snapshot = self.n["gp"]
        S = self.S
        Wr = rr(W_ap, "(kc p) m -> p kc m", p=128)

        def fsync(e):
            for j, (c0, w) in enumerate(mts):
                if j >= 2:
                    e.wait_ge(S["mm"], mmb + (j - 1) * len(tts))
                else:
                    if self.wb_last[j % 2]:
                        e.wait_ge(S["mm"], self.wb_last[j % 2])
                e.dma_start(out=self.wpanel_dst(j % 2, nkc, w),
                            in_=Wr[:, :, c0:c0 + w]).then_inc(S[lane[j % 2]], 16)
        self.sec("sync", fsync)
        self.n["dwa"] += 16 * ((len(mts) + 1) // 2)
        self.n["dwb"] += 16 * (len(mts) // 2)
        for sslot in range(2):
            js = [j for j in range(len(mts)) if j % 2 == sslot]
            if js:
                self.wb_last[sslot] = mmb + (js[-1] + 1) * len(tts)

        def ftensor(te):
            i = 0
            for j, (c0, w) in enumerate(mts):
                te.wait_ge(S[lane[j % 2]], lb[j % 2] + 16 * (j // 2 + 1))
                for tt in tts:
                    if i >= 2:
                        te.wait_ge(S[evs], evb + i - 1)
                    elif self.ps01_last[i % 2] is not None:
                        te.wait_ge(S[self.ps01_last[i % 2][0]],
                                   self.ps01_last[i % 2][1])
                    ps = self.t["ps%d" % (i % 2)]
                    ins = None
                    for kc in range(nkc):
                        ins = te.matmul(ps[0:w, 0:TTW],
                                        self.wtile(j % 2, kc, w),
                                        rhs_fn(kc, tt),
                                        start=(kc == 0), stop=(kc == nkc - 1))
                    ins.then_inc(S["mm"], 1)
                    i += 1
        self.sec("tensor", ftensor)
        self.n["mm"] += ntl

        def fev(e):
            i = 0
            for j, (c0, w) in enumerate(mts):
                for tt in tts:
                    e.wait_ge(S["mm"], mmb + i + 1)
                    if dma_out is not None and i >= 2:
                        e.wait_ge(S["gp"], self._gp_snapshot + i - 1)
                    evac(e, i, j, tt, self.t["ps%d" % (i % 2)][0:w, 0:TTW])
                    e.drain().then_inc(S[evs], 1)
                    i += 1
        self.sec(eng, fev)
        self.n[evs] += ntl

        for par in range(2):
            iis = [ii2 for ii2 in range(ntl) if ii2 % 2 == par]
            if iis:
                self.ps01_last[par] = (evs, evb + iis[-1] + 1)
        if dma_out is not None:
            gpb = self.n["gp"]
            def fg(g):
                i = 0
                for j, (c0, w) in enumerate(mts):
                    for tt in tts:
                        g.wait_ge(S[evs], evb + i + 1)
                        dma_out(g, i, j, tt)
                        g.wait_ge(S["dg"], dgb + 16 * dma_per_tile * (i + 1))
                        g.nop().then_inc(S["gp"], 1)
                        i += 1
            self.sec("gpsimd", fg)
            self.n["dg"] += 16 * dma_per_tile * ntl
            self.n["gp"] += ntl
        return ntl



def build():
    nc = bass.Bass("TRN2", target_bir_lowering=False, debug=False)
    d = {}

    def din(name, shape):
        d[name] = nc.dram_tensor(name, list(shape), F32, kind="ExternalInput")

    din("xs", (NTOK, DM))
    din("wq0", (DM, DM)); din("wk0", (DM, DM)); din("wv0", (DM, DM)); din("wo0", (DM, DM))
    din("kvdw", (DM, 512)); din("qdw", (DM, 512))
    din("wqasm", (512, DM))
    din("kuw", (512, 1024)); din("ropekw", (DM, 1024)); din("vuw", (512, DM))
    din("ow", (DM, DM))
    din("shg", (DM, 1024)); din("shu", (DM, 1024)); din("shd", (1024, DM))
    din("rg", (7 * DM, 1024)); din("ru", (7 * DM, 1024)); din("rd", (7 * 1024, DM))
    din("rw0", (DM, 7))
    din("wq1", (DM, DM)); din("wk1", (DM, 512)); din("wv1", (DM, 512)); din("wo1", (DM, DM))
    din("rw1", (DM, NH)); din("ew", (DM, DM))
    din("ident", (128, 128))
    din("rowsel", (128, NH * NH))
    din("hsel", (NH, NH * 128))
    din("permh", (128, 128)); din("permq", (128, 128)); din("permi", (128, 128))
    din("l0cos", (128, NTOK)); din("l0sin", (128, NTOK))
    din("l0cosq", (128, NTOK)); din("l0sinq", (128, NTOK))
    din("l1cos", (128, NTOK)); din("l1sin", (128, NTOK))
    din("masks", (6 * 128, 4 * TTW))
    din("foldF", (128, NH)); din("foldH", (128, NCORE)); din("repR", (NH, 128))
    din("selc", (NCORE, 128))
    din("esel", (7, 7 * 128))
    din("vecs", (128, VNVEC))

    d["yout"] = nc.dram_tensor("yout", [NTOK, DM], F32, kind="ExternalOutput")
    d["stash"] = nc.dram_tensor("stash", [128, NCH * NTOK], F32)
    d["qdsp"] = nc.dram_tensor("qdsp", [128, 4 * NTOK], F32)
    for tag, kd in (("a1", DM), ("a2", DM), ("a3", 512)):
        d["agki_" + tag] = nc.dram_tensor("agki_" + tag, [kd, NTOK], F32)
        d["agko_" + tag] = nc.dram_tensor("agko_" + tag, [2 * kd, NTOK], F32)
        d["agvi_" + tag] = nc.dram_tensor("agvi_" + tag, [NTOK, kd], F32)
        d["agvo_" + tag] = nc.dram_tensor("agvo_" + tag, [2 * NTOK, kd], F32)
    d["agsi"] = nc.dram_tensor("agsi", [NH, NTOK], F32)
    d["agso"] = nc.dram_tensor("agso", [NCORE * NH, NTOK], F32)

    with ExitStack() as st:
        tn = {}
        def sb(name, cols):
            tn[name] = st.enter_context(nc.sbuf_tensor(name, [128, cols], F32))
        sb("x", NCH * NTOK)
        sb("arena", ARENA)
        sb("wb", 2 * 2048)
        sb("cst", CST_END)
        sb("sm0", 1536)
        sb("sm1", 2048)
        sb("sm2", 64)
        for i in range(8):
            tn["ps%d" % i] = st.enter_context(
                nc.psum_tensor("ps%d" % i, [128, 512], F32))
        sems = {}
        for s in ("dwa", "dwb", "dka", "dkb", "dva", "dvb", "dm", "dx",
                  "dk", "dg", "gp", "cc", "mm", "sc", "ve",
                  "bar_tensor", "bar_vector", "bar_scalar", "bar_gpsimd",
                  "bar_sync"):
            sems[s] = st.enter_context(nc.semaphore(s))
        with nc.Block() as block:
            tensors = dict(tn)
            tensors.update(d)
            mk = MK(nc, block, tensors, sems)
            emit_all(mk)
    return nc


def emit_all(mk):
    p0_load(mk)
    if DBG[0] == "p0":
        return emit_output(mk)
    p1_l0_attn(mk)
    if DBG[0] == "p1":
        return emit_output(mk)
    p2_l0_ffn(mk)
    if DBG[0] == "p2":
        return emit_output(mk)
    p3_l1_attn(mk)
    if DBG[0] == "p3":
        return emit_output(mk)
    p4_l1_ffn(mk)
    emit_output(mk)


def p0_load(mk):
    S, t = mk.S, mk.t
    dkb = mk.n["dk"]

    def fsync(e):
        for dst_off, src, pn, cols in (
                (CST_ID, "ident", 128, 128), (CST_PERMH, "permh", 128, 128),
                (CST_PERMQ, "permq", 128, 128), (CST_PERMI, "permi", 128, 128),
                (CST_ROWSEL, "rowsel", 128, 256), (CST_FOLDF, "foldF", 128, 16),
                (CST_FOLDH, "foldH", 128, 8), (CST_SELC, "selc", 8, 128),
                (CST_ESEL, "esel", 7, 896), (CST_VECS, "vecs", 128, VNVEC),
                (CST_REPR, "repR", 16, 128)):
            e.dma_start(out=t["cst"][0:pn, dst_off:dst_off + cols],
                        in_=t[src].ap()).then_inc(S["dk"], 16)
        e.dma_start(out=t["sm1"][0:16, 0:2048], in_=t["hsel"].ap()).then_inc(S["dk"], 16)
    mk.sec("sync", fsync)
    mk.n["dk"] += 16 * 12

    def fvec(e):
        e.wait_ge(S["dk"], dkb + 16 * 12)
        e.memset(t["cst"][:, CST_ONES:CST_ONES + 8], 1.0)
        e.drain().then_inc(S["ve"], 1)
    mk.sec("vector", fvec)
    mk.n["ve"] += 1

    stg = 0
    dkb2 = mk.n["dx"]
    mmb = mk.n["mm"]
    veb = mk.n["ve"]
    xr = rr(t["x"].ap(), "p (c t) -> p c t", c=NCH)

    def fsync2(e):
        for g in range(8):
            if g >= 1:
                e.wait_ge(S["mm"], mmb + g * NCH)
            e.dma_start(out=mk.A(stg, 2048),
                        in_=t["xs"].ap()[g * 128:(g + 1) * 128, :]).then_inc(S["dx"], 16)
    mk.sec("sync", fsync2)
    mk.n["dx"] += 16 * 8

    def ftens(e):
        e.wait_ge(S["ve"], veb)
        for g in range(8):
            e.wait_ge(S["dx"], dkb2 + 16 * (g + 1))
            if g >= 1:
                e.wait_ge(S["ve"], veb + 4 * g)
            for dc in range(NCH):
                bank = t["ps%d" % (dc // 4)]
                e.transpose(bank[:, (dc % 4) * 128:(dc % 4) * 128 + 128],
                            mk.A(stg + dc * 128, 128),
                            mk.cst(CST_ID, 128))
            e.drain().then_inc(S["mm"], NCH)
    mk.sec("tensor", ftens)
    mk.n["mm"] += 8 * NCH

    def fvec2(e):
        for g in range(8):
            e.wait_ge(S["mm"], mmb + (g + 1) * NCH)
            for bk in range(4):
                dst = xr[:, bk * 4:(bk + 1) * 4, g * 128:(g + 1) * 128]
                src = rr(t["ps%d" % bk].ap()[:, 0:512], "p (b r) -> p b r", b=4)
                e.tensor_copy(dst, src)
            e.drain().then_inc(S["ve"], 4)
    mk.sec("vector", fvec2)
    mk.n["ve"] += 8 * 4
    mk.barrier()


def emit_output(mk):
    mk.barrier()
    S, t = mk.S, mk.t
    stg = 0
    mmb, veb, dgb = mk.n["mm"], mk.n["ve"], mk.n["dg"]

    def ftens(e):
        for g in range(8):
            if g >= 1:
                e.wait_ge(S["ve"], veb + 4 * g)
            for dc in range(NCH):
                bank = t["ps%d" % (dc // 4)]
                e.transpose(bank[:, (dc % 4) * 128:(dc % 4) * 128 + 128],
                            mk.xap(dc, g * 128, 128),
                            mk.cst(CST_ID, 128))
            e.drain().then_inc(S["mm"], NCH)
    mk.sec("tensor", ftens)
    mk.n["mm"] += 8 * NCH

    gpb = mk.n["gp"]
    def fvec(e):
        for g in range(8):
            e.wait_ge(S["mm"], mmb + (g + 1) * NCH)
            if g >= 1:
                e.wait_ge(S["gp"], gpb + g)
            for bk in range(4):
                e.tensor_copy(mk.A(stg + bk * 512, 512), t["ps%d" % bk][:, 0:512])
            e.drain().then_inc(S["ve"], 4)
    mk.sec("vector", fvec)
    mk.n["ve"] += 8 * 4

    def fg(g_):
        for g in range(8):
            g_.wait_ge(S["ve"], veb + 4 * (g + 1))
            g_.dma_start(out=t["yout"].ap()[g * 128:(g + 1) * 128, :],
                         in_=mk.A(stg, 2048)).then_inc(S["dg"], 16)
            g_.wait_ge(S["dg"], dgb + 16 * (g + 1))
            g_.nop().then_inc(S["gp"], 1)
    mk.sec("gpsimd", fg)
    mk.n["dg"] += 16 * 8
    mk.n["gp"] += 8


def ln_full(mk, sq_off, rms_wcol=None, dst_off=None, add_off=None):
    S, t = mk.S, mk.t
    is_ln = rms_wcol is None
    ve_prev_apply = None
    for tt in range(2):
        if add_off is not None:
            veb0 = mk.n["ve"]
            def fadd(e, tt=tt):
                for kc in range(NCH):
                    dst = mk.xap(kc, tt * TTW)
                    e.tensor_add(dst, dst, mk.A(add_off + kc * NTOK + tt * TTW, TTW))
                e.drain().then_inc(S["ve"], 1)
            mk.sec("vector", fadd)
            mk.n["ve"] += 1

        mmb, scb, veb = mk.n["mm"], mk.n["sc"], mk.n["ve"]

        def fsc(e, tt=tt, veb=veb):
            if add_off is not None:
                e.wait_ge(S["ve"], veb)
            for kc in range(NCH):
                if kc >= 1:
                    e.wait_ge(S["mm"], mmb + kc)
                e.activation(mk.A(sq_off, 512), mk.xap(kc, tt * TTW), AF.Square)
                e.drain().then_inc(S["sc"], 1)
        mk.sec("scalar", fsc)
        mk.n["sc"] += NCH

        def fte(e, tt=tt, ve_prev_apply=ve_prev_apply):
            if ve_prev_apply is not None:
                e.wait_ge(S["ve"], ve_prev_apply)
            for kc in range(NCH):
                e.wait_ge(S["sc"], scb + kc + 1)
                if is_ln:
                    e.matmul(t["ps6"][0:1, 0:TTW], mk.cst(CST_ONES, 1),
                             mk.xap(kc, tt * TTW),
                             start=(kc == 0), stop=(kc == NCH - 1))
                e.matmul(t["ps7"][0:1, 0:TTW], mk.cst(CST_ONES, 1),
                         mk.A(sq_off, 512),
                         start=(kc == 0), stop=(kc == NCH - 1))
                e.drain().then_inc(S["mm"], 1)
        mk.sec("tensor", fte)
        mk.n["mm"] += NCH

        mmb2 = mk.n["mm"]
        def fve(e, mmb2=mmb2):
            e.wait_ge(S["mm"], mmb2)
            if is_ln:
                e.tensor_scalar_mul(t["sm0"][0:1, 0:512], t["ps6"][0:1, 0:TTW], 1.0 / DM)
            e.tensor_scalar_mul(t["sm0"][0:1, 512:1024], t["ps7"][0:1, 0:TTW], 1.0 / DM)
            e.drain()
            if is_ln:
                e.tensor_mul(t["sm0"][0:1, 1024:1536], t["sm0"][0:1, 0:512],
                             t["sm0"][0:1, 0:512])
                e.drain()
                e.tensor_sub(t["sm0"][0:1, 512:1024], t["sm0"][0:1, 512:1024],
                             t["sm0"][0:1, 1024:1536])
                e.drain()
            e.tensor_scalar_add(t["sm0"][0:1, 512:1024], t["sm0"][0:1, 512:1024], 1e-5)
            e.drain()
            e.reciprocal(t["sm0"][0:1, 512:1024], t["sm0"][0:1, 512:1024])
            e.drain().then_inc(S["ve"], 1)
        mk.sec("vector", fve)
        mk.n["ve"] += 1

        veb2 = mk.n["ve"]
        def fsc2(e, veb2=veb2):
            e.wait_ge(S["ve"], veb2)
            e.activation(t["sm0"][0:1, 512:1024], t["sm0"][0:1, 512:1024], AF.Sqrt)
            e.drain().then_inc(S["sc"], 1)
        mk.sec("scalar", fsc2)
        mk.n["sc"] += 1

        scb2 = mk.n["sc"]
        if is_ln:
            def fve2(e, scb2=scb2):
                e.wait_ge(S["sc"], scb2)
                e.tensor_mul(t["sm0"][0:1, 1024:1536], t["sm0"][0:1, 0:512],
                             t["sm0"][0:1, 512:1024])
                e.drain()
                e.tensor_scalar_mul(t["sm0"][0:1, 1024:1536],
                                    t["sm0"][0:1, 1024:1536], -1.0)
                e.drain().then_inc(S["ve"], 1)
            mk.sec("vector", fve2)
            mk.n["ve"] += 1
        veb3 = mk.n["ve"]

        def fte2(e, scb2=scb2, veb3=veb3):
            e.wait_ge(S["sc"], scb2)
            e.wait_ge(S["ve"], veb3)
            e.matmul(t["ps4"][:, 0:TTW], mk.onesrow(), t["sm0"][0:1, 512:1024],
                     start=True, stop=True)
            if is_ln:
                e.matmul(t["ps5"][:, 0:TTW], mk.onesrow(), t["sm0"][0:1, 1024:1536],
                         start=True, stop=True)
            e.drain().then_inc(S["mm"], 1)
        mk.sec("tensor", fte2)
        mk.n["mm"] += 1

        mmb4 = mk.n["mm"]
        def fve3(e, tt=tt, mmb4=mmb4):
            e.wait_ge(S["mm"], mmb4)
            for kc in range(NCH):
                src = mk.xap(kc, tt * TTW)
                dst = src if dst_off is None else mk.A(dst_off + kc * NTOK + tt * TTW, TTW)
                e.tensor_mul(dst, src, t["ps4"][:, 0:TTW])
                if is_ln:
                    e.drain()
                    e.tensor_add(dst, dst, t["ps5"][:, 0:TTW])
            e.drain().then_inc(S["ve"], 1)
        mk.sec("vector", fve3)
        mk.n["ve"] += 1
        ve_prev_apply = mk.n["ve"]

        if not is_ln:
            veb5 = mk.n["ve"]
            def fsc3(e, tt=tt, veb5=veb5):
                e.wait_ge(S["ve"], veb5)
                for kc in range(NCH):
                    dst = mk.A(dst_off + kc * NTOK + tt * TTW, TTW)
                    e.activation(dst, dst, AF.Identity, scale=mk.vec(rms_wcol + kc))
                e.drain().then_inc(S["sc"], 1)
            mk.sec("scalar", fsc3)
            mk.n["sc"] += 1
    mk.barrier()


def fm_ln(mk, buf_off, tts, sq_off, extra_scale=1.0, dma_to=None, ncols=NTOK):
    S, t = mk.S, mk.t
    ntts = len(tts)
    ve_apply = {}
    for h in range(NH):
        for tti, tt in enumerate(tts):
            i = h * ntts + tti
            src = mk.A(buf_off + h * ncols + tti * TTW, TTW)
            stat = [t["ps6"], t["ps7"]] if i % 2 == 0 else [t["ps2"], t["ps3"]]
            bcA = t["ps4"] if i % 2 == 0 else t["ps5"]
            bcB = stat[0]
            scb, mmb, veb = mk.n["sc"], mk.n["mm"], mk.n["ve"]

            def fsc(e, src=src, i=i, mmb=mmb):
                if i >= 1:
                    e.wait_ge(S["mm"], mmb)
                e.activation(mk.A(sq_off, 512), src, AF.Square)
                e.drain().then_inc(S["sc"], 1)
            mk.sec("scalar", fsc)
            mk.n["sc"] += 1

            def fte(e, src=src, stat=stat, scb=scb, i=i):
                e.wait_ge(S["sc"], scb + 1)
                if i >= 2 and (i - 2) in ve_apply:
                    e.wait_ge(S["ve"], ve_apply[i - 2])
                e.matmul(stat[0][0:1, 0:TTW], mk.cst(CST_ONES, 1), src,
                         start=True, stop=True)
                e.matmul(stat[1][0:1, 0:TTW], mk.cst(CST_ONES, 1), mk.A(sq_off, 512),
                         start=True, stop=True)
                e.drain().then_inc(S["mm"], 1)
            mk.sec("tensor", fte)
            mk.n["mm"] += 1

            mmb2 = mk.n["mm"]
            def fve(e, stat=stat, mmb2=mmb2):
                e.wait_ge(S["mm"], mmb2)
                e.tensor_scalar_mul(t["sm0"][0:1, 0:512], stat[0][0:1, 0:TTW], 1.0 / 128)
                e.tensor_scalar_mul(t["sm0"][0:1, 512:1024], stat[1][0:1, 0:TTW], 1.0 / 128)
                e.drain()
                e.tensor_mul(t["sm0"][0:1, 1024:1536], t["sm0"][0:1, 0:512],
                             t["sm0"][0:1, 0:512])
                e.drain()
                e.tensor_sub(t["sm0"][0:1, 512:1024], t["sm0"][0:1, 512:1024],
                             t["sm0"][0:1, 1024:1536])
                e.drain()
                e.tensor_scalar_add(t["sm0"][0:1, 512:1024], t["sm0"][0:1, 512:1024], 1e-5)
                e.drain()
                e.reciprocal(t["sm0"][0:1, 512:1024], t["sm0"][0:1, 512:1024])
                e.drain().then_inc(S["ve"], 1)
            mk.sec("vector", fve)
            mk.n["ve"] += 1

            veb2 = mk.n["ve"]
            def fsc2(e, veb2=veb2):
                e.wait_ge(S["ve"], veb2)
                e.activation(t["sm0"][0:1, 512:1024], t["sm0"][0:1, 512:1024], AF.Sqrt)
                e.drain().then_inc(S["sc"], 1)
            mk.sec("scalar", fsc2)
            mk.n["sc"] += 1

            scb2 = mk.n["sc"]
            def fve2(e, scb2=scb2):
                e.wait_ge(S["sc"], scb2)
                if extra_scale != 1.0:
                    e.tensor_scalar_mul(t["sm0"][0:1, 512:1024],
                                        t["sm0"][0:1, 512:1024], float(extra_scale))
                    e.drain()
                e.tensor_mul(t["sm0"][0:1, 1024:1536], t["sm0"][0:1, 0:512],
                             t["sm0"][0:1, 512:1024])
                e.drain()
                e.tensor_scalar_mul(t["sm0"][0:1, 1024:1536],
                                    t["sm0"][0:1, 1024:1536], -1.0)
                e.drain().then_inc(S["ve"], 1)
            mk.sec("vector", fve2)
            mk.n["ve"] += 1

            veb3 = mk.n["ve"]
            def fte2(e, bcA=bcA, bcB=bcB, veb3=veb3):
                e.wait_ge(S["ve"], veb3)
                e.matmul(bcA[:, 0:TTW], mk.onesrow(), t["sm0"][0:1, 512:1024],
                         start=True, stop=True)
                e.matmul(bcB[:, 0:TTW], mk.onesrow(), t["sm0"][0:1, 1024:1536],
                         start=True, stop=True)
                e.drain().then_inc(S["mm"], 1)
            mk.sec("tensor", fte2)
            mk.n["mm"] += 1

            mmb4 = mk.n["mm"]
            def fve3(e, src=src, bcA=bcA, bcB=bcB, mmb4=mmb4):
                e.wait_ge(S["mm"], mmb4)
                e.tensor_mul(src, src, bcA[:, 0:TTW])
                e.drain()
                e.tensor_add(src, src, bcB[:, 0:TTW])
                e.drain().then_inc(S["ve"], 1)
            mk.sec("vector", fve3)
            mk.n["ve"] += 1
            ve_apply[i] = mk.n["ve"]

            if dma_to is not None:
                veb4 = mk.n["ve"]
                def fgd(g, src=src, h=h, tt=tt, veb4=veb4):
                    g.wait_ge(S["ve"], veb4)
                    g.dma_start(out=dma_to.ap()[h * 128:(h + 1) * 128,
                                                tt * TTW:(tt + 1) * TTW],
                                in_=src).then_inc(S["dg"], 16)
                mk.sec("gpsimd", fgd)
                mk.n["dg"] += 16


def rope_buf(mk, buf_off, chunks, tts, perm_off, cos_off, sin_off, tmp_off,
             ncols=NTOK):
    S, t = mk.S, mk.t
    veb0 = mk.n["ve"]
    mmb0 = mk.n["mm"]
    units = [(ch, tti, tt) for ch in chunks for tti, tt in enumerate(tts)]

    def fte(e):
        for i, (ch, tti, tt) in enumerate(units):
            if i >= 2:
                e.wait_ge(S["ve"], veb0 + 2 * (i - 1))
            else:
                e.wait_ge(S["ve"], veb0)
            src = mk.A(buf_off + ch * ncols + tti * TTW, TTW)
            bank = t["ps%d" % (6 + i % 2)]
            e.matmul(bank[:, 0:TTW], mk.cst(perm_off, 128), src,
                     start=True, stop=True)
            e.drain().then_inc(S["mm"], 1)
    mk.sec("tensor", fte)
    mk.n["mm"] += len(units)

    def fve(e):
        for i, (ch, tti, tt) in enumerate(units):
            e.wait_ge(S["mm"], mmb0 + i + 1)
            src = mk.A(buf_off + ch * ncols + tti * TTW, TTW)
            bank = t["ps%d" % (6 + i % 2)]
            cosap = mk.A(cos_off + tt * TTW, TTW)
            sinap = mk.A(sin_off + tt * TTW, TTW)
            e.drain()
            e.tensor_mul(mk.A(tmp_off, 512), bank[:, 0:TTW], sinap)
            e.tensor_mul(src, src, cosap)
            e.drain()
            e.tensor_add(src, src, mk.A(tmp_off, 512))
            e.drain().then_inc(S["ve"], 2)
    mk.sec("vector", fve)
    mk.n["ve"] += 2 * len(units)


def attn_core(mk, qc, agko, agvo, kdim, score, krow_fn, vcol_fn):
    S, t = mk.S, mk.t
    nkb = NKB[qc]
    NU = nkb * NH
    qb, ob, eb, mb, kv = ATT_QB, ATT_OB, ATT_EB, ATT_MB, ATT_KV

    mmb, scb, veb = mk.n["mm"], mk.n["sc"], mk.n["ve"]
    klane = ("dka", "dkb")
    vlane = ("dva", "dvb")
    kcnt = {0: mk.n["dka"], 1: mk.n["dkb"]}
    vcnt = {0: mk.n["dva"], 1: mk.n["dvb"]}
    dmb = mk.n["dm"]
    rec = {"mask": {}, "k": {}, "v": {}}

    def fsync(e):
        for kbi in range(nkb):
            s = SLOTS.index((qc, kbi))
            if kbi >= 1:
                e.wait_ge(S["ve"], veb + 5 * NH * kbi)
            e.dma_start(out=mk.A(mb, 2048),
                        in_=t["masks"].ap()[s * 128:(s + 1) * 128, :]).then_inc(S["dm"], 16)
            rec["mask"][kbi] = dmb + 16 * (kbi + 1)
            blk, c0 = KBSRC[kbi]
            for h in range(NH):
                u = kbi * NH + h
                if u >= 2:
                    e.wait_ge(S["mm"], mmb + 5 * (u - 1))
                kr = krow_fn(h)
                kcnt[u % 2] += 16
                e.dma_start(out=mk.A(kv + (u % 2) * 512, 512),
                            in_=agko.ap()[blk * kdim + kr:blk * kdim + kr + 128,
                                          c0:c0 + 512]).then_inc(S[klane[u % 2]], 16)
                rec["k"][u] = (klane[u % 2], kcnt[u % 2])
                vc = vcol_fn(h)
                vsrc = rr(agvo.ap(), "(b tk p) d -> p b tk d", p=128, b=2)
                vcnt[u % 2] += 16
                e.dma_start(out=rr(mk.A(kv + 1024 + (u % 2) * 512, 512),
                                   "p (tk d) -> p tk d", tk=4),
                            in_=vsrc[:, blk, c0 // 128:c0 // 128 + 4,
                                     vc:vc + 128]).then_inc(S[vlane[u % 2]], 16)
                rec["v"][u] = (vlane[u % 2], vcnt[u % 2])
    mk.sec("sync", fsync)
    mk.n["dm"] = dmb + 16 * nkb
    mk.n["dka"] = kcnt[0]
    mk.n["dkb"] = kcnt[1]
    mk.n["dva"] = vcnt[0]
    mk.n["dvb"] = vcnt[1]

    def fte(e):
        for kbi in range(nkb):
            for h in range(NH):
                u = kbi * NH + h
                e.wait_ge(S[rec["k"][u][0]], rec["k"][u][1])
                for kc in range(4):
                    g = 4 * u + kc
                    if g >= 2:
                        e.wait_ge(S["sc"], scb + g - 1)
                    sbk = t["ps%d" % (2 + g % 2)]
                    e.matmul(sbk[:, 0:TTW],
                             mk.A(kv + (u % 2) * 512 + kc * 128, 128),
                             mk.A(qb + h * 512, TTW),
                             start=True, stop=True).then_inc(S["mm"], 1)
                e.wait_ge(S["ve"], veb + 5 * u + 4)
                e.wait_ge(S[rec["v"][u][0]], rec["v"][u][1])
                if u >= 2:
                    e.wait_ge(S["ve"], veb + 5 * (u - 1))
                obk = t["ps%d" % (4 + h % 2)]
                ins = None
                for kc in range(4):
                    e.matmul(t["ps6"][0:16, 0:TTW], mk.rowsel(h),
                             mk.A(eb + (u % 2) * 2048 + kc * 512, 512),
                             start=(u == 0 and kc == 0),
                             stop=(u == NU - 1 and kc == 3))
                    ins = e.matmul(obk[:, 0:TTW],
                                   mk.A(kv + 1024 + (u % 2) * 512 + kc * 128, 128),
                                   mk.A(eb + (u % 2) * 2048 + kc * 512, 512),
                                   start=(kc == 0), stop=(kc == 3))
                ins.then_inc(S["mm"], 1)
    mk.sec("tensor", fte)
    mk.n["mm"] += 5 * NU

    def fsc(e):
        for kbi in range(nkb):
            for h in range(NH):
                u = kbi * NH + h
                for kc in range(4):
                    g = 4 * u + kc
                    e.wait_ge(S["mm"], mmb + 5 * u + kc + 1)
                    if u >= 2:
                        e.wait_ge(S["mm"], mmb + 5 * (u - 1))
                    sbk = t["ps%d" % (2 + g % 2)]
                    dst = mk.A(eb + (u % 2) * 2048 + kc * 512, 512)
                    if score[0] == "square":
                        e.activation(dst, sbk[:, 0:TTW], AF.Square)
                    else:
                        e.activation(dst, sbk[:, 0:TTW], AF.Exp, scale=float(score[1]))
                    e.drain().then_inc(S["sc"], 1)
    mk.sec("scalar", fsc)
    mk.n["sc"] += 4 * NU

    def fve(e):
        for kbi in range(nkb):
            for h in range(NH):
                u = kbi * NH + h
                e.wait_ge(S["dm"], rec["mask"][kbi])
                for kc in range(4):
                    e.wait_ge(S["sc"], scb + 4 * u + kc + 1)
                    ap = mk.A(eb + (u % 2) * 2048 + kc * 512, 512)
                    e.tensor_mul(ap, ap, mk.A(mb + kc * 512, 512))
                    e.drain().then_inc(S["ve"], 1)
                e.wait_ge(S["mm"], mmb + 5 * u + 5)
                obk = t["ps%d" % (4 + h % 2)]
                dst = mk.A(ob + h * 512, 512)
                e.drain()
                if kbi == 0:
                    e.tensor_copy(dst, obk[:, 0:TTW])
                else:
                    e.tensor_add(dst, dst, obk[:, 0:TTW])
                e.drain().then_inc(S["ve"], 1)
    mk.sec("vector", fve)
    mk.n["ve"] += 5 * NU


def znorm_apply(mk, z_eps):
    S, t = mk.S, mk.t
    ob = ATT_OB
    mmb = mk.n["mm"]

    def fve(e):
        e.wait_ge(S["mm"], mmb)
        src = t["ps6"][0:16, 0:TTW]
        if z_eps:
            e.tensor_scalar_add(t["sm0"][0:16, 0:512], src, float(z_eps))
            e.drain()
            e.reciprocal(t["sm0"][0:16, 0:512], t["sm0"][0:16, 0:512])
        else:
            e.reciprocal(t["sm0"][0:16, 0:512], src)
        e.drain().then_inc(S["ve"], 1)
    mk.sec("vector", fve)
    mk.n["ve"] += 1

    veb2 = mk.n["ve"]
    mmb2 = mk.n["mm"]
    def fte(e):
        e.wait_ge(S["ve"], veb2)
        for h in range(NH):
            if h >= 2:
                e.wait_ge(S["ve"], veb2 + h - 1)
            bank = t["ps%d" % (2 + h % 2)]
            e.matmul(bank[:, 0:TTW], mk.hsel(h), t["sm0"][0:16, 0:512],
                     start=True, stop=True)
            e.drain().then_inc(S["mm"], 1)
    mk.sec("tensor", fte)
    mk.n["mm"] += NH

    def fve2(e):
        e.drain()
        for h in range(NH):
            e.wait_ge(S["mm"], mmb2 + h + 1)
            bank = t["ps%d" % (2 + h % 2)]
            dst = mk.A(ob + h * 512, 512)
            e.tensor_mul(dst, dst, bank[:, 0:TTW])
            e.drain().then_inc(S["ve"], 1)
    mk.sec("vector", fve2)
    mk.n["ve"] += NH


def proj_vT(mk, W_ap, Kdim, mts, rhs_fn, agvi, vt_off, bias_col=None):
    S, t = mk.S, mk.t
    vt = vt_off
    vs = vt_off + 1024
    nkc = Kdim // 128
    mmb, scb, veb, dgb = (mk.n["mm"], mk.n["sc"], mk.n["ve"], mk.n["dg"])
    lane = ("dwa", "dwb")
    lb = [mk.n["dwa"], mk.n["dwb"]]
    Wr = rr(W_ap, "(kc p) m -> p kc m", p=128)
    ntl = len(mts) * 2

    def fsync(e):
        for j, (c0, w) in enumerate(mts):
            if j >= 2:
                e.wait_ge(S["mm"], mmb + 2 * ((j - 1) * 2))
            else:
                if mk.wb_last[j % 2]:
                    e.wait_ge(S["mm"], mk.wb_last[j % 2])
            e.dma_start(out=mk.wpanel_dst(j % 2, nkc, w),
                        in_=Wr[:, :, c0:c0 + w]).then_inc(S[lane[j % 2]], 16)
    mk.sec("sync", fsync)
    mk.n["dwa"] += 16 * ((len(mts) + 1) // 2)
    mk.n["dwb"] += 16 * (len(mts) // 2)
    for sslot in range(2):
        js = [j for j in range(len(mts)) if j % 2 == sslot]
        if js:
            mk.wb_last[sslot] = mmb + 2 * ntl

    recp = {}
    rect = {}

    def fte(e):
        cnt = mmb
        for j, (c0, w) in enumerate(mts):
            e.wait_ge(S[lane[j % 2]], lb[j % 2] + 16 * (j // 2 + 1))
            for tt in range(2):
                i = j * 2 + tt
                if i >= 2:
                    e.wait_ge(S["sc"], scb + i - 1)
                elif mk.ps01_last[i % 2] is not None:
                    e.wait_ge(S[mk.ps01_last[i % 2][0]], mk.ps01_last[i % 2][1])
                ps = t["ps%d" % (i % 2)]
                ins = None
                for kc in range(nkc):
                    ins = e.matmul(ps[0:w, 0:TTW], mk.wtile(j % 2, kc, w),
                                   rhs_fn(kc, tt), start=(kc == 0),
                                   stop=(kc == nkc - 1))
                ins.then_inc(S["mm"], 1)
                cnt += 1
                recp[i] = cnt
                if i >= 1:
                    it = i - 1
                    e.wait_ge(S["sc"], scb + it + 1)
                    if it >= 2:
                        e.wait_ge(S["ve"], veb + it - 1)
                    bank = t["ps%d" % (2 + it % 2)]
                    for b in range(4):
                        e.transpose(bank[:, b * 128:(b + 1) * 128],
                                    mk.A(vt + (it % 2) * 512 + b * 128, 128),
                                    mk.cst(CST_ID, 128))
                    e.drain().then_inc(S["mm"], 1)
                    cnt += 1
                    rect[it] = cnt
        it = ntl - 1
        e.wait_ge(S["sc"], scb + it + 1)
        e.wait_ge(S["ve"], veb + it - 1)
        bank = t["ps%d" % (2 + it % 2)]
        for b in range(4):
            e.transpose(bank[:, b * 128:(b + 1) * 128],
                        mk.A(vt + (it % 2) * 512 + b * 128, 128),
                        mk.cst(CST_ID, 128))
        e.drain().then_inc(S["mm"], 1)
        cnt += 1
        rect[it] = cnt
    mk.sec("tensor", fte)
    mk.n["mm"] += 2 * ntl
    for par in range(2):
        iis = [ii2 for ii2 in range(ntl) if ii2 % 2 == par]
        if iis:
            mk.ps01_last[par] = ("sc", mk.n["sc"] + iis[-1] + 1)

    def fsc(e):
        for i in range(ntl):
            e.wait_ge(S["mm"], recp[i])
            if i >= 2:
                e.wait_ge(S["mm"], rect[i - 2])
            ps = t["ps%d" % (i % 2)]
            j = i // 2
            if bias_col is not None:
                e.activation(mk.A(vt + (i % 2) * 512, 512), ps[:, 0:TTW],
                             AF.Identity, bias=mk.vec(bias_col + j))
            else:
                e.activation(mk.A(vt + (i % 2) * 512, 512), ps[:, 0:TTW], AF.Identity)
            e.drain().then_inc(S["sc"], 1)
    mk.sec("scalar", fsc)
    mk.n["sc"] += ntl

    gpb = mk.n["gp"]
    def fve(e):
        for i in range(ntl):
            e.wait_ge(S["mm"], rect[i])
            if i >= 2:
                e.wait_ge(S["gp"], gpb + i - 1)
            bank = t["ps%d" % (2 + i % 2)]
            e.tensor_copy(mk.A(vs + (i % 2) * 512, 512), bank[:, 0:512])
            e.drain().then_inc(S["ve"], 1)
    mk.sec("vector", fve)
    mk.n["ve"] += ntl

    def fg(g):
        for i in range(ntl):
            g.wait_ge(S["ve"], veb + i + 1)
            j, tt = i // 2, i % 2
            c0 = mts[j][0]
            dst = rr(agvi.ap(), "(tk p) d -> p tk d", p=128)[
                :, tt * 4:(tt + 1) * 4, c0:c0 + 128]
            src = rr(mk.A(vs + (i % 2) * 512, 512), "p (b d) -> p b d", b=4)
            g.dma_start(out=dst, in_=src).then_inc(S["dg"], 16)
            g.wait_ge(S["dg"], dgb + 16 * (i + 1))
            g.nop().then_inc(S["gp"], 1)
    mk.sec("gpsimd", fg)
    mk.n["dg"] += 16 * ntl
    mk.n["gp"] += ntl


def allgather_pair(mk, pairs):
    S = mk.S
    dgt = mk.n["dg"]
    groups = [[0, 1], [2, 3], [4, 5], [6, 7]]
    def fg(g):
        g.wait_ge(S["dg"], dgt)
        for p in pairs:
            g.collective_compute(
                "AllGather", OP.bypass, replica_groups=groups,
                ins=[p[0].ap().opt()],
                outs=[p[1].ap().opt()],
            ).then_inc(S["cc"], 1)
    mk.sec("gpsimd", fg)
    mk.n["cc"] += len(pairs)


def wait_cc(mk, val):
    S = mk.S
    def f(e):
        e.wait_ge(S["cc"], val)
    mk.sec("sync", f)


def p1_l0_attn(mk):
    S, t = mk.S, mk.t
    kq = 0
    def ev(e, i, j, tt, ps):
        e.activation(mk.A(kq + j * NTOK + tt * TTW, TTW), ps, AF.Identity,
                     scale=mk.vec(VGAM), bias=mk.vec(VBET))
    mk.proj(t["wk0"].ap(), DM, MTS16, (0, 1),
            lambda kc, tt: mk.xap(kc, tt * TTW), ev, eng="scalar")
    mk.barrier()
    fm_ln(mk, kq, (0, 1), sq_off=16384, dma_to=t["agki_a1"])
    mk.barrier()

    proj_vT(mk, t["wv0"].ap(), DM, MTS16,
            lambda kc, tt: mk.xap(kc, tt * TTW), t["agvi_a1"], vt_off=0)
    allgather_pair(mk, [(t["agki_a1"], t["agko_a1"]), (t["agvi_a1"], t["agvo_a1"])])
    mk.barrier()

    ccq = mk.n["cc"]
    for qc in range(2):
        def evq(e, i, j, tt, ps):
            e.activation(mk.A(ATT_QB + j * 512, TTW), ps, AF.Identity,
                         scale=mk.vec(VGAM), bias=mk.vec(VBET))
        mk.proj(t["wq0"].ap(), DM, MTS16, (qc,),
                lambda kc, tt: mk.xap(kc, tt * TTW), evq, eng="scalar")
        mk.barrier()
        fm_ln(mk, ATT_QB, (qc,), sq_off=ATT_EB, extra_scale=128.0 ** -0.5,
              ncols=512)
        wait_cc(mk, ccq)
        mk.barrier()
        attn_core(mk, qc, t["agko_a1"], t["agvo_a1"], DM, ("square",),
                  lambda h: h * 128, lambda h: h * 128)
        znorm_apply(mk, 1e-5)
        mk.barrier()
        def evo(e, i, j, tt, ps):
            dst = mk.xap(j, tt * TTW)
            e.tensor_add(dst, dst, ps)
        mk.proj(t["wo0"].ap(), DM, MTS16, (qc,),
                lambda kc, tt: mk.A(ATT_OB + kc * 512, 512), evo, eng="vector")
        mk.barrier()
    ln_full(mk, sq_off=0)


def p2_l0_ffn(mk):
    S, t = mk.S, mk.t
    dgb = mk.n["dg"]
    def fg(g):
        g.dma_start(out=t["stash"].ap(), in_=t["x"][:, :]).then_inc(S["dg"], 16)
    mk.sec("gpsimd", fg)
    mk.n["dg"] += 16
    mk.barrier()

    xn = 0
    ln_full(mk, sq_off=16384, rms_wcol=VL0W1, dst_off=xn)

    kvd = 16384
    qd = 20480
    mk.proj(t["kvdw"].ap(), DM, MTS4, (0, 1),
            lambda kc, tt: mk.A(xn + kc * NTOK + tt * TTW, TTW),
            _ev_copy(mk, kvd), eng="vector")
    mk.proj(t["qdw"].ap(), DM, MTS4, (0, 1),
            lambda kc, tt: mk.A(xn + kc * NTOK + tt * TTW, TTW),
            _ev_copy(mk, qd), eng="vector")
    mk.barrier()
    def fgq(g):
        g.dma_start(out=t["qdsp"].ap(), in_=mk.A(qd, 4 * NTOK)).then_inc(S["dg"], 16)
        g.wait_ge(S["dg"], mk.n["dg"] + 16)
        g.nop().then_inc(S["gp"], 1)
    mk.sec("gpsimd", fgq)
    mk.n["dg"] += 16
    mk.n["gp"] += 1
    gp_spill = mk.n["gp"]

    kp = 24576
    def evkp(e, i, j, tt, ps):
        e.tensor_copy(mk.A(kp + (i % 2) * 512, 512), ps)
    def dmakp(g, i, j, tt):
        for half in range(2):
            h = 2 * j + half
            src = mk.A(kp + (i % 2) * 512, 512, p0=half * 64, pn=64)
            g.dma_start(out=t["agki_a2"].ap()[h * 128:h * 128 + 64,
                                              tt * TTW:(tt + 1) * TTW],
                        in_=src).then_inc(S["dg"], 16)
    mk.proj(t["kuw"].ap(), 512, MTS8, (0, 1),
            lambda kc, tt: mk.A(kvd + kc * NTOK + tt * TTW, TTW),
            evkp, eng="vector", dma_out=dmakp, dma_per_tile=2)
    mk.barrier()

    proj_vT(mk, t["vuw"].ap(), 512, MTS16,
            lambda kc, tt: mk.A(kvd + kc * NTOK + tt * TTW, TTW),
            t["agvi_a2"], vt_off=24576)
    mk.barrier()

    tabs = 24576
    dkb = mk.n["dx"]
    def fsy(e):
        e.dma_start(out=mk.A(tabs, NTOK), in_=t["l0cos"].ap()).then_inc(S["dx"], 16)
        e.dma_start(out=mk.A(tabs + NTOK, NTOK), in_=t["l0sin"].ap()).then_inc(S["dx"], 16)
    mk.sec("sync", fsy)
    mk.n["dx"] += 32
    krb = 16384
    for half in range(2):
        mts = MTS8[half * 4:(half + 1) * 4]
        def evkr(e, i, j, tt, ps):
            e.tensor_copy(mk.A(krb + j * NTOK + tt * TTW, TTW), ps)
        mk.proj(t["ropekw"].ap()[:, half * 512:(half + 1) * 512], DM,
                MTS4, (0, 1),
                lambda kc, tt: mk.A(xn + kc * NTOK + tt * TTW, TTW),
                evkr, eng="vector")
        if half == 0:
            def fw(e, dkb=dkb):
                e.wait_ge(S["dx"], dkb + 32)
            mk.sec("vector", fw)
        mk.barrier()
        rope_buf(mk, krb, list(range(4)), (0, 1),
                 CST_PERMH, tabs, tabs + NTOK, tmp_off=krb + 4 * NTOK)
        veb = mk.n["ve"]
        def fgkr(g, half=half, veb=veb):
            g.wait_ge(S["ve"], veb)
            for j in range(4):
                for hh in range(2):
                    h = 2 * (half * 4 + j) + hh
                    g.dma_start(out=t["agki_a2"].ap()[h * 128 + 64:h * 128 + 128, :],
                                in_=mk.A(krb + j * NTOK, NTOK, p0=hh * 64, pn=64)
                                ).then_inc(S["dg"], 16)
        mk.sec("gpsimd", fgkr)
        mk.n["dg"] += 16 * 8
        mk.barrier()
    allgather_pair(mk, [(t["agki_a2"], t["agko_a2"]), (t["agvi_a2"], t["agvo_a2"])])
    mk.barrier()

    ccq = mk.n["cc"]
    for qc in range(2):
        dkb2 = mk.n["dx"]
        def fsy2(e):
            e.dma_start(out=mk.A(ATT_QTAB, NTOK), in_=t["l0cosq"].ap()).then_inc(S["dx"], 16)
            e.dma_start(out=mk.A(ATT_QTAB + NTOK, NTOK), in_=t["l0sinq"].ap()).then_inc(S["dx"], 16)
        mk.sec("sync", fsy2)
        mk.n["dx"] += 32
        def fgz(g):
            g.wait_ge(S["gp"], gp_spill)
            g.dma_start(out=mk.A(ATT_EB, 4096), in_=t["qdsp"].ap()).then_inc(S["dg"], 16)
            g.wait_ge(S["dg"], mk.n["dg"] + 16)
            g.nop().then_inc(S["gp"], 1)
        mk.sec("gpsimd", fgz)
        mk.n["dg"] += 16
        mk.n["gp"] += 1
        gp3 = mk.n["gp"]
        def fwv(e):
            e.wait_ge(S["gp"], gp3)
            e.wait_ge(S["dx"], dkb2 + 32)
        mk.sec("tensor", fwv)
        mk.barrier()

        def evq(e, i, j, tt, ps):
            e.tensor_copy(mk.A(ATT_QB + j * 512, TTW), ps)
        mk.proj(t["wqasm"].ap(), 512, MTS16, (qc,),
                lambda kc, tt: mk.A(ATT_EB + kc * NTOK + tt * TTW, TTW),
                evq, eng="vector")
        mk.barrier()
        rope_buf(mk, ATT_QB, list(range(NH)), (qc,), CST_PERMQ,
                 ATT_QTAB, ATT_QTAB + NTOK, tmp_off=ATT_MB, ncols=512)
        wait_cc(mk, ccq)
        mk.barrier()
        attn_core(mk, qc, t["agko_a2"], t["agvo_a2"], DM, ("exp", 128.0 ** -0.5),
                  lambda h: h * 128, lambda h: h * 128)
        znorm_apply(mk, None)
        mk.barrier()
        def evo(e, i, j, tt, ps):
            dst = mk.xap(j, tt * TTW)
            e.tensor_add(dst, dst, ps)
        mk.proj(t["ow"].ap(), DM, MTS16, (qc,),
                lambda kc, tt: mk.A(ATT_OB + kc * 512, 512), evo, eng="vector")
        mk.barrier()

    xn2 = 0
    ln_full(mk, sq_off=16384, rms_wcol=VL0W2, dst_off=xn2)
    moe_router(mk, xn2)
    moe_experts(mk, xn2)

    def fg2(g):
        g.dma_start(out=mk.A(0, NCH * NTOK), in_=t["stash"].ap()).then_inc(S["dg"], 16)
        g.wait_ge(S["dg"], mk.n["dg"] + 16)
        g.nop().then_inc(S["gp"], 1)
    mk.sec("gpsimd", fg2)
    mk.n["dg"] += 16
    mk.n["gp"] += 1
    gp5 = mk.n["gp"]
    def fw3(e):
        e.wait_ge(S["gp"], gp5)
    mk.sec("vector", fw3)
    ln_full(mk, sq_off=16384, add_off=0)


def _ev_copy(mk, dst_off, ncols=NTOK):
    def ev(e, i, j, tt, ps):
        e.tensor_copy(mk.A(dst_off + j * ncols + tt * TTW, TTW), ps)
    return ev


def moe_router(mk, xn2):
    S, t = mk.S, mk.t
    def ev(e, i, j, tt, ps):
        e.activation(t["sm0"][0:7, tt * TTW:(tt + 1) * TTW], ps[0:7, 0:TTW],
                     AF.Sigmoid, bias=mk.vec(VRB0, pn=7))
    mk.proj(t["rw0"].ap(), DM, [(0, 7)], (0, 1),
            lambda kc, tt: mk.A(xn2 + kc * NTOK + tt * TTW, TTW), ev, eng="scalar")
    mk.barrier()

    ptm = 1024
    wtm = 1088
    mmb, veb = mk.n["mm"], mk.n["ve"]

    def fte(e):
        for g in range(8):
            if g >= 2:
                e.wait_ge(S["ve"], veb + g - 1)
            bank = t["ps%d" % (2 + g % 2)]
            e.transpose(bank[0:128, 0:7], t["sm0"][0:7, g * 128:(g + 1) * 128],
                        mk.cst(CST_ID, 7, pn=7))
            e.drain().then_inc(S["mm"], 1)
    mk.sec("tensor", fte)
    mk.n["mm"] += 8

    def fve(e):
        for g in range(8):
            e.wait_ge(S["mm"], mmb + g + 1)
            bank = t["ps%d" % (2 + g % 2)]
            e.tensor_copy(t["sm0"][:, ptm + g * 8: ptm + g * 8 + 7], bank[:, 0:7])
            e.memset(t["sm0"][:, ptm + g * 8 + 7: ptm + g * 8 + 8], -1.0)
            e.drain().then_inc(S["ve"], 1)
    mk.sec("vector", fve)
    mk.n["ve"] += 8

    veb2 = mk.n["ve"]
    def fve2(e):
        e.wait_ge(S["ve"], veb2)
        for g in range(8):
            pg = t["sm0"][:, ptm + g * 8: ptm + g * 8 + 8]
            wg = t["sm0"][:, wtm + g * 8: wtm + g * 8 + 8]
            e.drain()
            e.max(out=t["sm2"][:, 0:8], in_=pg)
            e.drain()
            e.tensor_copy(t["sm2"][:, 8:10], t["sm2"][:, 0:2])
            e.memset(t["sm2"][:, 10:16], -2.0)
            e.drain()
            e.match_replace(out=wg, in_to_replace=t["sm2"][:, 8:16],
                            in_values=pg, imm_value=0.0)
            e.tensor_add(t["sm2"][:, 16:17], t["sm2"][:, 0:1], t["sm2"][:, 1:2])
            e.drain()
            e.reciprocal(t["sm2"][:, 16:17], t["sm2"][:, 16:17])
            e.drain()
            e.tensor_sub(wg, pg, wg)
            e.drain()
            e.tensor_single_scalar(wg, wg, t["sm2"][:, 16:17], OP.mult)
        e.drain().then_inc(S["ve"], 1)
    mk.sec("vector", fve2)
    mk.n["ve"] += 1

    veb3 = mk.n["ve"]
    mmb3 = mk.n["mm"]
    def fte2(e):
        e.wait_ge(S["ve"], veb3)
        for g in range(8):
            bank = t["ps%d" % (2 + (g // 4) % 2)]
            e.transpose(bank[0:8, (g % 4) * 128:(g % 4) * 128 + 128],
                        t["sm0"][:, wtm + g * 8: wtm + g * 8 + 8], mk.cst(CST_ID, 128))
            if g % 4 == 3:
                e.drain().then_inc(S["mm"], 1)
    mk.sec("tensor", fte2)
    mk.n["mm"] += 2

    wfm = 16384
    def fve3(e):
        for half in range(2):
            e.wait_ge(S["mm"], mmb3 + half + 1)
            bank = t["ps%d" % (2 + half % 2)]
            e.tensor_copy(mk.A(wfm + half * 512, 512, pn=8), bank[0:8, 0:512])
        e.drain().then_inc(S["ve"], 1)
    mk.sec("vector", fve3)
    mk.n["ve"] += 1
    mk.barrier()
    return wfm


WFM_OFF = 16384


def moe_experts(mk, xn2):
    S, t = mk.S, mk.t
    hb = 17408
    wB = 25600
    for ei in range(8):
        routed = ei >= 1
        e_idx = ei - 1
        if routed:
            mmb = mk.n["mm"]
            veb = mk.n["ve"]
            def fteb(e, e_idx=e_idx, veb=veb):
                e.wait_ge(S["ve"], veb)
                for tt in range(2):
                    e.matmul(t["ps%d" % (6 + tt)][:, 0:TTW],
                             mk.cst(CST_ESEL + e_idx * 128, 128, pn=7),
                             mk.A(WFM_OFF + tt * TTW, TTW, pn=7),
                             start=True, stop=True)
                    e.drain().then_inc(S["mm"], 1)
            mk.sec("tensor", fteb)
            mk.n["mm"] += 2
            def fveb(e, mmb=mmb):
                for tt in range(2):
                    e.wait_ge(S["mm"], mmb + tt + 1)
                    e.tensor_scalar_mul(mk.A(wB + tt * TTW, TTW),
                                        t["ps%d" % (6 + tt)][:, 0:TTW], 1.0 / 8.0)
                e.drain().then_inc(S["ve"], 1)
            mk.sec("vector", fveb)
            mk.n["ve"] += 1
            mk.barrier()
            Wg = t["rg"].ap()[e_idx * DM:(e_idx + 1) * DM, :]
            Wu = t["ru"].ap()[e_idx * DM:(e_idx + 1) * DM, :]
            Wd = t["rd"].ap()[e_idx * 1024:(e_idx + 1) * 1024, :]
        else:
            Wg, Wu, Wd = t["shg"].ap(), t["shu"].ap(), t["shd"].ap()

        gate_up(mk, xn2, Wg, Wu, hb, wB if routed else None)
        mk.barrier()
        def evd(e, i, j, tt, ps):
            dst = mk.xap(j, tt * TTW)
            e.tensor_add(dst, dst, ps)
        mk.proj(Wd, 1024, MTS16, (0, 1),
                lambda kc, tt: mk.A(hb + kc * NTOK + tt * TTW, TTW), evd,
                eng="vector")
        mk.barrier()


def gate_up(mk, xn2, Wg, Wu, hb, wB):
    S, t = mk.S, mk.t
    nkc = NCH
    mmb, scb, veb = mk.n["mm"], mk.n["sc"], mk.n["ve"]
    ga, gb = mk.n["dwa"], mk.n["dwb"]
    Wgr = rr(Wg, "(kc p) m -> p kc m", p=128)
    Wur = rr(Wu, "(kc p) m -> p kc m", p=128)

    def fsync(e):
        for j in range(8):
            if j >= 1:
                e.wait_ge(S["mm"], mmb + (j - 1) * 4 + 4)
            else:
                for sslot in range(2):
                    if mk.wb_last[sslot]:
                        e.wait_ge(S["mm"], mk.wb_last[sslot])
            e.dma_start(out=mk.wpanel_dst(0, nkc, 128),
                        in_=Wgr[:, :, j * 128:(j + 1) * 128]).then_inc(S["dwa"], 16)
            e.dma_start(out=mk.wpanel_dst(1, nkc, 128),
                        in_=Wur[:, :, j * 128:(j + 1) * 128]).then_inc(S["dwb"], 16)
    mk.sec("sync", fsync)
    mk.n["dwa"] += 16 * 8
    mk.n["dwb"] += 16 * 8
    mk.wb_last[0] = mk.wb_last[1] = mk.n["mm"] + 8 * 2 * 2

    def fte(e):
        i = 0
        for j in range(8):
            e.wait_ge(S["dwa"], ga + 16 * (j + 1))
            e.wait_ge(S["dwb"], gb + 16 * (j + 1))
            for tt in range(2):
                if i >= 1:
                    e.wait_ge(S["ve"], veb + i)
                else:
                    for par in range(2):
                        if mk.ps01_last[par] is not None:
                            e.wait_ge(S[mk.ps01_last[par][0]],
                                      mk.ps01_last[par][1])
                for kc in range(nkc):
                    e.matmul(t["ps0"][:, 0:TTW], mk.wtile(0, kc, 128),
                             mk.A(xn2 + kc * NTOK + tt * TTW, TTW),
                             start=(kc == 0), stop=(kc == nkc - 1))
                e.drain().then_inc(S["mm"], 1)
                ins = None
                for kc in range(nkc):
                    ins = e.matmul(t["ps1"][:, 0:TTW], mk.wtile(1, kc, 128),
                                   mk.A(xn2 + kc * NTOK + tt * TTW, TTW),
                                   start=(kc == 0), stop=(kc == nkc - 1))
                ins.then_inc(S["mm"], 1)
                i += 1
    mk.sec("tensor", fte)
    mk.n["mm"] += 8 * 2 * 2

    def fsc(e):
        for i in range(16):
            j, tt = i // 2, i % 2
            e.wait_ge(S["mm"], mmb + 2 * i + 1)
            dst = mk.A(hb + j * NTOK + tt * TTW, TTW)
            e.activation(dst, t["ps0"][:, 0:TTW], AF.Silu)
            e.drain().then_inc(S["sc"], 1)
    mk.sec("scalar", fsc)
    mk.n["sc"] += 16

    def fve(e):
        for i in range(16):
            j, tt = i // 2, i % 2
            e.wait_ge(S["sc"], scb + i + 1)
            e.wait_ge(S["mm"], mmb + 2 * i + 2)
            dst = mk.A(hb + j * NTOK + tt * TTW, TTW)
            e.tensor_mul(dst, dst, t["ps1"][:, 0:TTW])
            e.drain()
            if wB is None:
                e.tensor_scalar_mul(dst, dst, 1.0 / 8.0)
            else:
                e.tensor_mul(dst, dst, mk.A(wB + tt * TTW, TTW))
            e.drain().then_inc(S["ve"], 1)
    mk.sec("vector", fve)
    mk.n["ve"] += 16
    mk.ps01_last[0] = mk.ps01_last[1] = ("ve", mk.n["ve"])


def p3_l1_attn(mk):
    S, t = mk.S, mk.t
    tabs = 0
    dkb = mk.n["dk"]
    def fsy(e):
        e.dma_start(out=mk.A(tabs, NTOK), in_=t["l1cos"].ap()).then_inc(S["dx"], 16)
        e.dma_start(out=mk.A(tabs + NTOK, NTOK), in_=t["l1sin"].ap()).then_inc(S["dx"], 16)
    mk.sec("sync", fsy)
    mk.n["dx"] += 32

    kb = 2048
    def evk(e, i, j, tt, ps):
        e.activation(mk.A(kb + j * NTOK + tt * TTW, TTW), ps, AF.Identity,
                     bias=mk.vec(VBK1 + j))
    mk.proj(t["wk1"].ap(), DM, MTS4, (0, 1),
            lambda kc, tt: mk.xap(kc, tt * TTW), evk, eng="scalar")
    mk.barrier()
    nrm_factors(mk, kb, 4, None, sq_off=6144)
    def fwk(e):
        e.wait_ge(S["dx"], dkb + 32)
    mk.sec("vector", fwk)
    rope_buf(mk, kb, list(range(4)), (0, 1), CST_PERMI, tabs, tabs + NTOK,
             tmp_off=6144)
    apply_fac(mk, kb, 4)
    veb = mk.n["ve"]
    def fgk(g):
        g.wait_ge(S["ve"], veb)
        for j in range(4):
            g.dma_start(out=t["agki_a3"].ap()[j * 128:(j + 1) * 128, :],
                        in_=mk.A(kb + j * NTOK, NTOK)).then_inc(S["dg"], 16)
    mk.sec("gpsimd", fgk)
    mk.n["dg"] += 16 * 4
    mk.barrier()

    proj_vT(mk, t["wv1"].ap(), DM, MTS4,
            lambda kc, tt: mk.xap(kc, tt * TTW), t["agvi_a3"],
            vt_off=6656, bias_col=VBV1)
    allgather_pair(mk, [(t["agki_a3"], t["agko_a3"]), (t["agvi_a3"], t["agvo_a3"])])
    mk.barrier()

    ccq = mk.n["cc"]
    for qc in range(2):
        def evq(e, i, j, tt, ps):
            e.activation(mk.A(ATT_QB + j * 512, TTW), ps, AF.Identity,
                         bias=mk.vec(VBQ1 + j))
        mk.proj(t["wq1"].ap(), DM, MTS16, (qc,),
                lambda kc, tt: mk.xap(kc, tt * TTW), evq, eng="scalar")
        mk.barrier()
        nrm_factors(mk, ATT_QB, NH, VALP, tts=(qc,), ncols=512, sq_off=ATT_EB)
        rope_buf(mk, ATT_QB, list(range(NH)), (qc,), CST_PERMI,
                 ATT_QTAB, ATT_QTAB + NTOK, tmp_off=ATT_MB, ncols=512)
        apply_fac(mk, ATT_QB, NH, tts=(0,), ncols=512)
        wait_cc(mk, ccq)
        mk.barrier()
        attn_core(mk, qc, t["agko_a3"], t["agvo_a3"], 512, ("exp", 1.0),
                  lambda h: (h // 4) * 128, lambda h: (h // 4) * 128)
        znorm_apply(mk, None)
        mk.barrier()
        def evo(e, i, j, tt, ps):
            dst = mk.xap(j, tt * TTW)
            e.scalar_tensor_tensor(dst, ps, mk.vec(VBO1 + j), dst, OP.add, OP.add)
        mk.proj(t["wo1"].ap(), DM, MTS16, (qc,),
                lambda kc, tt: mk.A(ATT_OB + kc * 512, 512), evo, eng="vector")
        mk.barrier()
    ln_full(mk, sq_off=0)


def nrm_factors(mk, buf, nch, alpha_col, sq_off, tts=(0, 1), ncols=NTOK):
    S, t = mk.S, mk.t
    for tti, tt in enumerate(tts):
        scb, mmb = mk.n["sc"], mk.n["mm"]
        def fsc(e, tti=tti, mmb=mmb):
            for ch in range(nch):
                if ch >= 1:
                    e.wait_ge(S["mm"], mmb + ch)
                e.activation(mk.A(sq_off, 512),
                             mk.A(buf + ch * ncols + tti * TTW, TTW), AF.Square)
                e.drain().then_inc(S["sc"], 1)
        mk.sec("scalar", fsc)
        mk.n["sc"] += nch

        def fte(e, tti=tti, scb=scb):
            for ch in range(nch):
                e.wait_ge(S["sc"], scb + ch + 1)
                e.matmul(t["ps6"][0:16, 0:TTW], mk.rowsel(ch), mk.A(sq_off, 512),
                         start=(ch == 0), stop=(ch == nch - 1))
                e.drain().then_inc(S["mm"], 1)
        mk.sec("tensor", fte)
        mk.n["mm"] += nch

        mmb2 = mk.n["mm"]
        def fve(e, tti=tti, mmb2=mmb2):
            e.wait_ge(S["mm"], mmb2)
            e.tensor_copy(t["sm0"][0:16, tti * 512:(tti + 1) * 512],
                          t["ps6"][0:16, 0:TTW])
            e.drain().then_inc(S["ve"], 1)
        mk.sec("vector", fve)
        mk.n["ve"] += 1

        veb2 = mk.n["ve"]
        def fsc2(e, tti=tti, veb2=veb2):
            e.wait_ge(S["ve"], veb2)
            e.activation(t["sm0"][0:16, tti * 512:(tti + 1) * 512],
                         t["sm0"][0:16, tti * 512:(tti + 1) * 512], AF.Sqrt)
            e.drain().then_inc(S["sc"], 1)
        mk.sec("scalar", fsc2)
        mk.n["sc"] += 1

        scb2 = mk.n["sc"]
        def fve2(e, tti=tti, scb2=scb2):
            e.wait_ge(S["sc"], scb2)
            ap = t["sm0"][0:16, tti * 512:(tti + 1) * 512]
            e.tensor_scalar_add(ap, ap, 1e-6)
            e.drain()
            e.reciprocal(ap, ap)
            e.drain()
            if alpha_col is not None:
                e.tensor_single_scalar(ap, ap, mk.vec(alpha_col, pn=16), OP.mult)
            e.drain().then_inc(S["ve"], 1)
        mk.sec("vector", fve2)
        mk.n["ve"] += 1


def apply_fac(mk, buf, nch, tts=(0, 1), ncols=NTOK):
    S, t = mk.S, mk.t
    for tti in range(len(tts)):
        veb, mmb = mk.n["ve"], mk.n["mm"]
        def fte(e, tti=tti, veb=veb):
            e.wait_ge(S["ve"], veb)
            for ch in range(nch):
                if ch >= 2:
                    e.wait_ge(S["ve"], veb + ch - 1)
                bank = t["ps%d" % (2 + ch % 2)]
                e.matmul(bank[:, 0:TTW], mk.hsel(ch),
                         t["sm0"][0:16, tti * 512:(tti + 1) * 512],
                         start=True, stop=True)
                e.drain().then_inc(S["mm"], 1)
        mk.sec("tensor", fte)
        mk.n["mm"] += nch

        def fve(e, tti=tti, mmb=mmb):
            for ch in range(nch):
                e.wait_ge(S["mm"], mmb + ch + 1)
                bank = t["ps%d" % (2 + ch % 2)]
                ap = mk.A(buf + ch * ncols + tti * TTW, TTW)
                e.tensor_mul(ap, ap, bank[:, 0:TTW])
                e.drain().then_inc(S["ve"], 1)
        mk.sec("vector", fve)
        mk.n["ve"] += nch


def p4_l1_ffn(mk):
    S, t = mk.S, mk.t
    sfm = 0
    stm = 1024
    yb = 2176
    srep = 18560
    sel = 19584
    cB = 20608

    def ev(e, i, j, tt, ps):
        e.activation(t["arena"][0:16, sfm + tt * TTW:sfm + (tt + 1) * TTW],
                     ps[0:16, 0:TTW], AF.Identity, bias=mk.vec(VRB1, pn=16))
    mk.proj(t["rw1"].ap(), DM, [(0, 16)], (0, 1),
            lambda kc, tt: mk.xap(kc, tt * TTW), ev, eng="scalar")
    mk.barrier()

    mmb, veb = mk.n["mm"], mk.n["ve"]
    def fte(e):
        for g in range(8):
            if g >= 2:
                e.wait_ge(S["ve"], veb + g - 1)
            bank = t["ps%d" % (2 + g % 2)]
            e.transpose(bank[0:128, 0:16],
                        t["arena"][0:16, sfm + g * 128:sfm + (g + 1) * 128],
                        mk.cst(CST_ID, 16, pn=16))
            e.drain().then_inc(S["mm"], 1)
    mk.sec("tensor", fte)
    mk.n["mm"] += 8

    def fve(e):
        for g in range(8):
            e.wait_ge(S["mm"], mmb + g + 1)
            bank = t["ps%d" % (2 + g % 2)]
            e.tensor_copy(mk.A(stm + g * 16, 16), bank[:, 0:16])
            e.drain().then_inc(S["ve"], 1)
    mk.sec("vector", fve)
    mk.n["ve"] += 8

    for g in range(8):
        veb_g = mk.n["ve"]
        scb_pre = mk.n["sc"]
        def fva(e, g=g, scb_pre=scb_pre):
            if g >= 1:
                e.wait_ge(S["sc"], scb_pre)
            e.drain()
            ap = mk.A(stm + g * 16, 16)
            e.reduce_max(t["sm2"][:, 0:1], ap, axis=AXX)
            e.drain()
            e.tensor_scalar_mul(t["sm2"][:, 1:2], t["sm2"][:, 0:1], -1.0)
            e.drain().then_inc(S["ve"], 1)
        mk.sec("vector", fva)
        mk.n["ve"] += 1
        veb_g2 = mk.n["ve"]
        scb_g = mk.n["sc"]
        def fsa(e, g=g, veb_g2=veb_g2):
            e.wait_ge(S["ve"], veb_g2)
            ap = mk.A(stm + g * 16, 16)
            e.activation(ap, ap, AF.Exp, bias=t["sm2"][:, 1:2],
                         accum_out=t["sm2"][:, 2:3])
            e.drain().then_inc(S["sc"], 1)
        mk.sec("scalar", fsa)
        mk.n["sc"] += 1
        def fvb(e, g=g, scb_g=scb_g):
            e.wait_ge(S["sc"], scb_g + 1)
            e.drain()
            ap = mk.A(stm + g * 16, 16)
            e.reciprocal(t["sm2"][:, 3:4], t["sm2"][:, 2:3])
            e.drain()
            e.tensor_single_scalar(ap, ap, t["sm2"][:, 3:4], OP.mult)
            e.drain().then_inc(S["ve"], 1)
        mk.sec("vector", fvb)
        mk.n["ve"] += 1

    sfm2 = stm + 128
    veb4 = mk.n["ve"]
    mmb4 = mk.n["mm"]
    def fteb(e):
        e.wait_ge(S["ve"], veb4)
        for g in range(8):
            if g >= 2:
                e.wait_ge(S["ve"], veb4 + g - 1)
            bank = t["ps%d" % (6 + g % 2)]
            e.transpose(bank[0:16, 0:128], mk.A(stm + g * 16, 16),
                        mk.cst(CST_ID, 128))
            e.drain().then_inc(S["mm"], 1)
    mk.sec("tensor", fteb)
    mk.n["mm"] += 8

    def fveb(e):
        for g in range(8):
            e.wait_ge(S["mm"], mmb4 + g + 1)
            bank = t["ps%d" % (6 + g % 2)]
            e.tensor_copy(mk.A(sfm2 + g * 128, 128, pn=16), bank[0:16, 0:128])
            e.drain().then_inc(S["ve"], 1)
    mk.sec("vector", fveb)
    mk.n["ve"] += 8

    veb5b = mk.n["ve"]
    def fgs(g_):
        g_.wait_ge(S["ve"], veb5b)
        g_.dma_start(out=t["agsi"].ap(), in_=mk.A(sfm2, NTOK, pn=16)).then_inc(S["dg"], 16)
    mk.sec("gpsimd", fgs)
    mk.n["dg"] += 16
    dgt = mk.n["dg"]
    def fca(g_):
        g_.wait_ge(S["dg"], dgt)
        g_.collective_compute("AllGather", OP.bypass,
                              replica_groups=[list(range(NCORE))],
                              ins=[t["agsi"].ap().opt()],
                              outs=[t["agso"].ap().opt()]).then_inc(S["cc"], 1)
    mk.sec("gpsimd", fca)
    mk.n["cc"] += 1
    ccs = mk.n["cc"]

    def evy(e, i, j, tt, ps):
        e.activation(mk.A(yb + j * NTOK + tt * TTW, TTW), ps, AF.Identity,
                     bias=mk.vec(VEB + j))
    mk.proj(t["ew"].ap(), DM, MTS16, (0, 1),
            lambda kc, tt: mk.xap(kc, tt * TTW), evy, eng="scalar")
    mk.barrier()

    dkb = mk.n["dx"]
    def fsr(e):
        e.wait_ge(S["cc"], ccs)
        e.dma_start(out=mk.A(srep, NTOK), in_=t["agso"].ap()).then_inc(S["dx"], 16)
    mk.sec("sync", fsr)
    mk.n["dx"] += 16

    def fvi(e):
        e.wait_ge(S["dx"], dkb + 16)
        e.memset(t["sm2"][0:16, 20:21], 0.0)
        e.memset(t["sm2"][0:16, 21:22], 1.0)
        e.memset(t["sm2"][0:16, 22:23], 0.5)
        e.drain().then_inc(S["ve"], 1)
    mk.sec("vector", fvi)
    mk.n["ve"] += 1

    for it in range(38):
        veb_i = mk.n["ve"]
        def fti(e, veb_i=veb_i):
            e.wait_ge(S["ve"], veb_i)
            e.matmul(t["ps7"][:, 0:1], mk.cst(CST_REPR, 128, pn=16),
                     t["sm2"][0:16, 22:23], start=True, stop=True)
            e.drain().then_inc(S["mm"], 1)
        mk.sec("tensor", fti)
        mk.n["mm"] += 1
        mmb_i = mk.n["mm"]
        def fvj(e, mmb_i=mmb_i):
            e.wait_ge(S["mm"], mmb_i)
            e.drain()
            e.tensor_copy(t["sm2"][:, 24:25], t["ps7"][:, 0:1])
            e.drain()
            e.tensor_single_scalar(mk.A(sel, NTOK), mk.A(srep, NTOK),
                                   t["sm2"][:, 24:25], OP.subtract)
            e.drain()
            e.tensor_scalar_max(mk.A(sel, NTOK), mk.A(sel, NTOK), 0.0)
            e.drain()
            e.tensor_scalar_mul(mk.A(sel, NTOK), mk.A(sel, NTOK), 1e30)
            e.drain()
            e.tensor_scalar(mk.A(sel, NTOK), mk.A(sel, NTOK), 1.0, None,
                            OP.min, OP.add, accum_out=t["sm2"][:, 25:26])
            e.drain().then_inc(S["ve"], 1)
        mk.sec("vector", fvj)
        mk.n["ve"] += 1
        veb_j = mk.n["ve"]
        def ftj(e, veb_j=veb_j):
            e.wait_ge(S["ve"], veb_j)
            e.matmul(t["ps6"][0:16, 0:1], mk.cst(CST_FOLDF, 16),
                     t["sm2"][:, 25:26], start=True, stop=True)
            e.drain().then_inc(S["mm"], 1)
        mk.sec("tensor", ftj)
        mk.n["mm"] += 1
        mmb_j = mk.n["mm"]
        def fvk(e, mmb_j=mmb_j):
            e.wait_ge(S["mm"], mmb_j)
            sm = t["sm2"]
            e.drain()
            e.tensor_scalar_sub(sm[0:16, 26:27], t["ps6"][0:16, 0:1], 511.0)
            e.drain()
            e.tensor_scalar_max(sm[0:16, 26:27], sm[0:16, 26:27], 0.0)
            e.drain()
            e.tensor_scalar_min(sm[0:16, 26:27], sm[0:16, 26:27], 1.0)
            e.drain()
            e.tensor_sub(sm[0:16, 27:28], sm[0:16, 22:23], sm[0:16, 20:21])
            e.drain()
            e.tensor_mul(sm[0:16, 27:28], sm[0:16, 27:28], sm[0:16, 26:27])
            e.drain()
            e.tensor_add(sm[0:16, 20:21], sm[0:16, 20:21], sm[0:16, 27:28])
            e.drain()
            e.tensor_sub(sm[0:16, 28:29], sm[0:16, 21:22], sm[0:16, 22:23])
            e.drain()
            e.tensor_mul(sm[0:16, 28:29], sm[0:16, 28:29], sm[0:16, 26:27])
            e.drain()
            e.tensor_add(sm[0:16, 21:22], sm[0:16, 22:23], sm[0:16, 28:29])
            e.drain()
            e.tensor_add(sm[0:16, 22:23], sm[0:16, 20:21], sm[0:16, 21:22])
            e.drain()
            e.tensor_scalar_mul(sm[0:16, 22:23], sm[0:16, 22:23], 0.5)
            e.drain().then_inc(S["ve"], 1)
        mk.sec("vector", fvk)
        mk.n["ve"] += 1

    veb6 = mk.n["ve"]
    def ftf(e):
        e.wait_ge(S["ve"], veb6)
        e.matmul(t["ps7"][:, 0:1], mk.cst(CST_REPR, 128, pn=16),
                 t["sm2"][0:16, 20:21], start=True, stop=True)
        e.drain().then_inc(S["mm"], 1)
    mk.sec("tensor", ftf)
    mk.n["mm"] += 1
    mmb_f = mk.n["mm"]
    def fvf(e):
        e.wait_ge(S["mm"], mmb_f)
        e.drain()
        e.tensor_copy(t["sm2"][:, 24:25], t["ps7"][:, 0:1])
        e.drain()
        e.tensor_single_scalar(mk.A(sel, NTOK), mk.A(srep, NTOK),
                               t["sm2"][:, 24:25], OP.subtract)
        e.drain()
        e.tensor_scalar_max(mk.A(sel, NTOK), mk.A(sel, NTOK), 0.0)
        e.drain()
        e.tensor_scalar_mul(mk.A(sel, NTOK), mk.A(sel, NTOK), 1e30)
        e.drain()
        e.tensor_scalar_min(mk.A(sel, NTOK), mk.A(sel, NTOK), 1.0)
        e.drain()
        e.tensor_mul(mk.A(sel, NTOK), mk.A(sel, NTOK), mk.A(srep, NTOK))
        e.drain().then_inc(S["ve"], 1)
    mk.sec("vector", fvf)
    mk.n["ve"] += 1

    veb7 = mk.n["ve"]
    def ftg(e):
        e.wait_ge(S["ve"], veb7)
        for tt in range(2):
            e.matmul(t["ps%d" % (6 + tt)][0:8, 0:TTW], mk.cst(CST_FOLDH, 8),
                     mk.A(sel + tt * TTW, TTW), start=True, stop=True)
            e.drain().then_inc(S["mm"], 1)
    mk.sec("tensor", ftg)
    mk.n["mm"] += 2
    mmb_g = mk.n["mm"]
    def fvg(e):
        for tt in range(2):
            e.wait_ge(S["mm"], mmb_g - 2 + tt + 1)
            e.tensor_copy(t["sm0"][0:8, tt * 512:(tt + 1) * 512],
                          t["ps%d" % (6 + tt)][0:8, 0:TTW])
        e.drain().then_inc(S["ve"], 1)
    mk.sec("vector", fvg)
    mk.n["ve"] += 1
    veb8 = mk.n["ve"]
    def fth(e):
        e.wait_ge(S["ve"], veb8)
        for tt in range(2):
            e.matmul(t["ps%d" % (2 + tt)][:, 0:TTW], mk.cst(CST_SELC, 128, pn=8),
                     t["sm0"][0:8, tt * 512:(tt + 1) * 512], start=True, stop=True)
            e.drain().then_inc(S["mm"], 1)
    mk.sec("tensor", fth)
    mk.n["mm"] += 2
    mmb_h = mk.n["mm"]
    def fvh(e):
        for tt in range(2):
            e.wait_ge(S["mm"], mmb_h - 2 + tt + 1)
            e.tensor_copy(mk.A(cB + tt * TTW, TTW), t["ps%d" % (2 + tt)][:, 0:TTW])
        e.drain().then_inc(S["ve"], 1)
    mk.sec("vector", fvh)
    mk.n["ve"] += 1

    veb9 = mk.n["ve"]
    def fvz(e):
        e.wait_ge(S["ve"], veb9)
        for kc in range(NCH):
            for tt in range(2):
                dst = mk.xap(kc, tt * TTW)
                tmp = mk.A(sel + tt * TTW, TTW)
                e.drain()
                e.tensor_mul(tmp, mk.A(yb + kc * NTOK + tt * TTW, TTW),
                             mk.A(cB + tt * TTW, TTW))
                e.drain()
                e.tensor_add(dst, dst, tmp)
        e.drain().then_inc(S["ve"], 1)
    mk.sec("vector", fvz)
    mk.n["ve"] += 1
    mk.barrier()
    ln_full(mk, sq_off=sel)



def _host_constants():
    c = {}
    c["ident"] = np.eye(128, dtype=np.float32)
    rs = np.zeros((128, NH * NH), np.float32)
    for h in range(NH):
        rs[:, h * NH + h] = 1.0
    c["rowsel"] = rs
    hs = np.zeros((NH, NH * 128), np.float32)
    for h in range(NH):
        hs[h, h * 128:(h + 1) * 128] = 1.0
    c["hsel"] = hs
    ph = np.zeros((128, 128), np.float32)
    for blk in (0, 64):
        for m in range(32):
            ph[blk + m + 32, blk + m] = -1.0
        for m in range(32, 64):
            ph[blk + m - 32, blk + m] = 1.0
    c["permh"] = ph
    pq = np.zeros((128, 128), np.float32)
    for m in range(64, 96):
        pq[m + 32, m] = -1.0
    for m in range(96, 128):
        pq[m - 32, m] = 1.0
    c["permq"] = pq
    pi = np.zeros((128, 128), np.float32)
    for m in range(0, 128, 2):
        pi[m + 1, m] = -1.0
        pi[m, m + 1] = 1.0
    c["permi"] = pi
    ff = np.zeros((128, NH), np.float32)
    fh = np.zeros((128, NCORE), np.float32)
    for p in range(128):
        ff[p, p % 16] = 1.0
        fh[p, p // 16] = 1.0
    c["foldF"] = ff
    c["foldH"] = fh
    rp = np.zeros((NH, 128), np.float32)
    for m in range(128):
        rp[m % 16, m] = 1.0
    c["repR"] = rp
    es = np.zeros((7, 7 * 128), np.float32)
    for e in range(7):
        es[e, e * 128:(e + 1) * 128] = 1.0
    c["esel"] = es
    return c


def _rope_tables(hf):
    tl = np.arange(NTOK)
    tabs = np.array([QSTARTS[hf][i // 512] + (i % 512) for i in tl], np.float64)
    out = {}
    invf0 = 1.0 / (10000.0 ** (np.arange(0, 64, 2, dtype=np.float64) / 64.0))
    ang0 = np.zeros((128, NTOK), np.float64)
    for r in range(128):
        ang0[r] = tabs * invf0[(r % 64) % 32]
    out["l0cos"] = np.cos(ang0).astype(np.float32)
    out["l0sin"] = np.sin(ang0).astype(np.float32)
    cq = np.ones((128, NTOK), np.float64)
    sq = np.zeros((128, NTOK), np.float64)
    for r in range(64, 128):
        dd = r - 64
        cq[r] = np.cos(tabs * invf0[dd % 32])
        sq[r] = np.sin(tabs * invf0[dd % 32])
    out["l0cosq"] = cq.astype(np.float32)
    out["l0sinq"] = sq.astype(np.float32)
    invf1 = 1.0 / (10000.0 ** (np.arange(0, 128, 2, dtype=np.float64) / 128.0))
    ang1 = np.zeros((128, NTOK), np.float64)
    for r in range(128):
        ang1[r] = tabs * invf1[r // 2]
    out["l1cos"] = np.cos(ang1).astype(np.float32)
    out["l1sin"] = np.sin(ang1).astype(np.float32)
    return out


def _masks(hf):
    m = np.zeros((6, 128, 4, TTW), np.float32)
    for s, (qc, kb) in enumerate(SLOTS):
        q0 = QSTARTS[hf][qc]
        for kc in range(4):
            k0 = kb * 512 + kc * 128
            kk = k0 + np.arange(128)[:, None]
            qq = q0 + np.arange(TTW)[None, :]
            m[s, :, kc, :] = (qq >= kk).astype(np.float32)
    return m.reshape(6 * 128, 4 * TTW)


def _vec_pd(v, nch):
    return np.ascontiguousarray(np.asarray(v).reshape(nch, 128).T.astype(np.float32))


def kernel(**inputs):
    ii = {k: np.asarray(v) for k, v in inputs.items()}
    x = ii["L0_attn_hidden_states"]

    shared = {}
    shared["wq0"] = np.ascontiguousarray(ii["L0_attn_Wq"].T)
    shared["wk0"] = np.ascontiguousarray(ii["L0_attn_Wk"].T)
    shared["wv0"] = np.ascontiguousarray(ii["L0_attn_Wv"].T)
    shared["wo0"] = np.ascontiguousarray(ii["L0_attn_Wo"].T)
    shared["kvdw"] = ii["L0_ffn_kv_d_w"]
    shared["qdw"] = ii["L0_ffn_q_d_w"]
    qu = ii["L0_ffn_q_u_w"]; rq = ii["L0_ffn_rope_q_w"]
    wqasm = np.zeros((512, DM), np.float32)
    for h in range(NH):
        wqasm[:, h * 128:h * 128 + 64] = qu[:, h * 64:(h + 1) * 64]
        wqasm[:, h * 128 + 64:h * 128 + 128] = rq[:, h * 64:(h + 1) * 64]
    shared["wqasm"] = wqasm
    shared["kuw"] = ii["L0_ffn_k_u_w"]
    shared["ropekw"] = ii["L0_ffn_rope_k_w"]
    shared["vuw"] = ii["L0_ffn_v_u_w"]
    shared["ow"] = ii["L0_ffn_o_w"]
    shared["shg"] = ii["L0_ffn_sh_gate"]
    shared["shu"] = ii["L0_ffn_sh_up"]
    shared["shd"] = ii["L0_ffn_sh_down"]
    shared["rg"] = ii["L0_ffn_r_gate"].reshape(7 * DM, 1024)
    shared["ru"] = ii["L0_ffn_r_up"].reshape(7 * DM, 1024)
    shared["rd"] = ii["L0_ffn_r_down"].reshape(7 * 1024, DM)
    shared["rw0"] = ii["L0_ffn_router_w"]
    shared["wq1"] = ii["L1_attn_Wq"]
    shared["wk1"] = ii["L1_attn_Wk"]
    shared["wv1"] = ii["L1_attn_Wv"]
    shared["wo1"] = ii["L1_attn_Wo"]
    shared["rw1"] = ii["L1_ffn_router_w"]
    shared["ew"] = ii["L1_ffn_expert_w"]
    vecs = np.zeros((128, VNVEC), np.float32)
    vecs[:, VGAM] = ii["L0_attn_gamma"]
    vecs[:, VBET] = ii["L0_attn_beta"]
    vecs[:, VL0W1:VL0W1 + NCH] = _vec_pd(ii["L0_ffn_ln1_w"], NCH)
    vecs[:, VL0W2:VL0W2 + NCH] = _vec_pd(ii["L0_ffn_ln2_w"], NCH)
    vecs[:, VBQ1:VBQ1 + NCH] = _vec_pd(ii["L1_attn_bq"], NCH)
    vecs[:, VBK1:VBK1 + 4] = _vec_pd(ii["L1_attn_bk"], 4)
    vecs[:, VBV1:VBV1 + 4] = _vec_pd(ii["L1_attn_bv"], 4)
    vecs[:, VBO1:VBO1 + NCH] = _vec_pd(ii["L1_attn_bo"], NCH)
    vecs[:, VEB:VEB + NCH] = _vec_pd(ii["L1_ffn_expert_b"], NCH)
    vecs[0:NH, VALP] = ii["L1_attn_alpha"]
    vecs[0:7, VRB0] = ii["L0_ffn_router_b"]
    vecs[0:NH, VRB1] = ii["L1_ffn_router_b"]
    shared["vecs"] = vecs
    shared.update(_host_constants())
    shared = {k: np.ascontiguousarray(v, dtype=np.float32) for k, v in shared.items()}

    in_maps = []
    for c in range(NCORE):
        b, hf = c // 2, c % 2
        im = dict(shared)
        im["xs"] = np.ascontiguousarray(
            np.concatenate([x[b, s:s + 512] for s in QSTARTS[hf]], axis=0),
            dtype=np.float32)
        im.update({k: np.ascontiguousarray(v) for k, v in _rope_tables(hf).items()})
        im["masks"] = _masks(hf)
        sc = np.zeros((NCORE, 128), np.float32)
        sc[c, :] = 1.0
        im["selc"] = sc
        in_maps.append(im)

    nc = build()
    res = run_bass_kernel_spmd(nc, in_maps, core_ids=list(range(NCORE)))
    out = np.zeros((4, T, DM), np.float32)
    for c in range(NCORE):
        b, hf = c // 2, c % 2
        y = np.asarray(res.results[c]["yout"])
        for j, s in enumerate(QSTARTS[hf]):
            out[b, s:s + 512] = y[j * 512:(j + 1) * 512]
    return out


# version 02
# speedup vs baseline: 1.0064x; 1.0064x over previous
import numpy as np
from contextlib import ExitStack
import concourse.bass as bass
import concourse.mybir as mybir
from concourse.bass_utils import run_bass_kernel_spmd

F32 = mybir.dt.float32
AF = mybir.ActivationFunctionType
OP = mybir.AluOpType
AXX = mybir.AxisListType.X

NCORE = 8
T = 2048
DM = 2048
NCH = 16
NTOK = 1024
TTW = 512
QSTARTS = [[0, 1536], [512, 1024]]
NKB = [2, 4]
KBSRC = [(0, 0), (1, 0), (1, 512), (0, 512)]
SLOTS = [(0, 0), (0, 1), (1, 0), (1, 1), (1, 2), (1, 3)]
NH = 16
ARENA = 26624

DBG = [None]

CST_ID = 0
CST_PERMH = 128
CST_PERMQ = 256
CST_PERMI = 384
CST_ROWSEL = 512
CST_ONES = 768
CST_FOLDF = 776
CST_FOLDH = 792
CST_SELC = 800
CST_ESEL = 928
CST_VECS = 1824
CST_REPR = 1920
CST_END = 2048

VGAM, VBET = 0, 1
VL0W1, VL0W2 = 2, 18
VBQ1, VBK1, VBV1, VBO1 = 34, 50, 54, 58
VEB = 74
VALP, VRB0, VRB1 = 90, 91, 92
VNVEC = 96

ATT_QTAB = 0
ATT_QB = 2048
ATT_OB = 10240
ATT_EB = 18432
ATT_MB = 22528
ATT_KV = 24576

MTS16 = [(i * 128, 128) for i in range(16)]
MTS8 = [(i * 128, 128) for i in range(8)]
MTS4 = [(i * 128, 128) for i in range(4)]


def rr(ap, s, **kw):
    return ap.rearrange(s, **kw)


class _EPState:
    __slots__ = ("e", "last", "pend", "ninc")

    def __init__(self):
        self.e = None
        self.last = None
        self.pend = False
        self.ninc = 0


class _InsWrap:
    def __init__(self, ins, st):
        self._ins = ins
        self._st = st

    def then_inc(self, sem, n):
        self._st.ninc += 1
        return self._ins.then_inc(sem, n)

    def __getattr__(self, name):
        return getattr(self._ins, name)


class _DrainShim:
    def __init__(self, st):
        self._st = st

    def then_inc(self, sem, n):
        st = self._st
        if st.pend:
            st.pend = False
            if st.last is not None and st.ninc == 0:
                st.ninc = 1
                return st.last.then_inc(sem, n)
            ins = st.e.drain()
            st.last = ins
            st.ninc = 1
            return ins.then_inc(sem, n)
        return st.e.drain().then_inc(sem, n)


class EW:
    def __init__(self, st):
        object.__setattr__(self, "_st", st)

    def _flush(self):
        st = self._st
        if st.pend:
            st.pend = False
            st.e.drain()

    def wait_ge(self, *a, **k):
        self._flush()
        return self._st.e.wait_ge(*a, **k)

    def drain(self):
        self._flush()
        self._st.pend = True
        return _DrainShim(self._st)

    def __getattr__(self, name):
        st = self._st
        f = getattr(st.e, name)
        if not callable(f):
            return f

        def g(*a, **k):
            self._flush()
            ins = f(*a, **k)
            if ins is not None and hasattr(ins, "then_inc"):
                st.last = ins
                st.ninc = 0
                return _InsWrap(ins, st)
            return ins
        return g


class MK:
    def __init__(self, nc, block, tensors, sems):
        self.nc = nc
        self.block = block
        self.t = tensors
        self.S = sems
        self.n = {k: 0 for k in sems}
        self.nbar = 0
        self.wb_last = [0, 0]
        self.ps01_last = [None, None]
        self._ep = {}

    def sec(self, eng, fn):
        st = self._ep.setdefault(eng, _EPState())

        def wrapped(e, st=st, fn=fn):
            st.e = e
            fn(EW(st))
        getattr(self.block, eng)(wrapped)

    def barrier(self):
        self.nbar += 1
        tgt = self.nbar
        engs = ("tensor", "vector", "scalar", "gpsimd", "sync")
        sync_sems = ("dwa", "dwb", "dka", "dkb", "dva", "dvb", "dm", "dx", "dk")
        for eng in engs:
            def f(e, eng=eng):
                if eng == "sync":
                    for sem in sync_sems:
                        if self.n[sem]:
                            e.wait_ge(self.S[sem], self.n[sem])
                    e.nop().then_inc(self.S["bar_sync"], 1)
                elif eng == "gpsimd":
                    if self.n["dg"]:
                        e.wait_ge(self.S["dg"], self.n["dg"])
                    e.drain().then_inc(self.S["bar_gpsimd"], 1)
                else:
                    e.drain().then_inc(self.S["bar_" + eng], 1)
            self.sec(eng, f)
        for eng in engs:
            def f2(e, tgt=tgt):
                for other in engs:
                    e.wait_ge(self.S["bar_" + other], tgt)
            self.sec(eng, f2)

    def A(self, off, cols, p0=0, pn=128):
        return self.t["arena"][p0:p0 + pn, off:off + cols]

    def xap(self, kc, c0, w=TTW):
        return self.t["x"][:, kc * NTOK + c0: kc * NTOK + c0 + w]

    def cst(self, off, cols, p0=0, pn=128):
        return self.t["cst"][p0:p0 + pn, off:off + cols]

    def vec(self, col, pn=128):
        return self.t["cst"][0:pn, CST_VECS + col:CST_VECS + col + 1]

    def hsel(self, h):
        return self.t["sm1"][0:16, h * 128:(h + 1) * 128]

    def onesrow(self):
        return self.t["sm1"][0:1, 0:128]

    def rowsel(self, h):
        return self.cst(CST_ROWSEL + h * NH, NH)

    def wpanel_dst(self, slot, nkc, w):
        return self.t["wb"][:, slot * 2048: slot * 2048 + nkc * w]

    def wtile(self, slot, kc, w):
        return self.t["wb"][:, slot * 2048 + kc * w: slot * 2048 + (kc + 1) * w]

    def proj(self, W_ap, Kdim, mts, tts, rhs_fn, evac, eng="vector",
             dma_out=None, dma_per_tile=0):
        nkc = Kdim // 128
        ntl = len(mts) * len(tts)
        mmb = self.n["mm"]
        lane = ("dwa", "dwb")
        lb = [self.n["dwa"], self.n["dwb"]]
        evs = "ve" if eng == "vector" else "sc"
        evb = self.n[evs]
        dgb = self.n["dg"]
        self._gp_snapshot = self.n["gp"]
        S = self.S
        Wr = rr(W_ap, "(kc p) m -> p kc m", p=128)

        def fsync(e):
            for j, (c0, w) in enumerate(mts):
                if j >= 2:
                    e.wait_ge(S["mm"], mmb + (j - 1) * len(tts))
                else:
                    if self.wb_last[j % 2]:
                        e.wait_ge(S["mm"], self.wb_last[j % 2])
                e.dma_start(out=self.wpanel_dst(j % 2, nkc, w),
                            in_=Wr[:, :, c0:c0 + w]).then_inc(S[lane[j % 2]], 16)
        self.sec("sync", fsync)
        self.n["dwa"] += 16 * ((len(mts) + 1) // 2)
        self.n["dwb"] += 16 * (len(mts) // 2)
        for sslot in range(2):
            js = [j for j in range(len(mts)) if j % 2 == sslot]
            if js:
                self.wb_last[sslot] = mmb + (js[-1] + 1) * len(tts)

        def ftensor(te):
            i = 0
            for j, (c0, w) in enumerate(mts):
                te.wait_ge(S[lane[j % 2]], lb[j % 2] + 16 * (j // 2 + 1))
                for tt in tts:
                    if i >= 2:
                        te.wait_ge(S[evs], evb + i - 1)
                    elif self.ps01_last[i % 2] is not None:
                        te.wait_ge(S[self.ps01_last[i % 2][0]],
                                   self.ps01_last[i % 2][1])
                    ps = self.t["ps%d" % (i % 2)]
                    ins = None
                    for kc in range(nkc):
                        ins = te.matmul(ps[0:w, 0:TTW],
                                        self.wtile(j % 2, kc, w),
                                        rhs_fn(kc, tt),
                                        start=(kc == 0), stop=(kc == nkc - 1))
                    ins.then_inc(S["mm"], 1)
                    i += 1
        self.sec("tensor", ftensor)
        self.n["mm"] += ntl

        def fev(e):
            i = 0
            for j, (c0, w) in enumerate(mts):
                for tt in tts:
                    e.wait_ge(S["mm"], mmb + i + 1)
                    if dma_out is not None and i >= 2:
                        e.wait_ge(S["gp"], self._gp_snapshot + i - 1)
                    evac(e, i, j, tt, self.t["ps%d" % (i % 2)][0:w, 0:TTW])
                    e.drain().then_inc(S[evs], 1)
                    i += 1
        self.sec(eng, fev)
        self.n[evs] += ntl

        for par in range(2):
            iis = [ii2 for ii2 in range(ntl) if ii2 % 2 == par]
            if iis:
                self.ps01_last[par] = (evs, evb + iis[-1] + 1)
        if dma_out is not None:
            gpb = self.n["gp"]
            def fg(g):
                i = 0
                for j, (c0, w) in enumerate(mts):
                    for tt in tts:
                        g.wait_ge(S[evs], evb + i + 1)
                        dma_out(g, i, j, tt)
                        g.wait_ge(S["dg"], dgb + 16 * dma_per_tile * (i + 1))
                        g.nop().then_inc(S["gp"], 1)
                        i += 1
            self.sec("gpsimd", fg)
            self.n["dg"] += 16 * dma_per_tile * ntl
            self.n["gp"] += ntl
        return ntl



def build():
    nc = bass.Bass("TRN2", target_bir_lowering=False, debug=False)
    d = {}

    def din(name, shape):
        d[name] = nc.dram_tensor(name, list(shape), F32, kind="ExternalInput")

    din("xs", (NTOK, DM))
    din("wq0", (DM, DM)); din("wk0", (DM, DM)); din("wv0", (DM, DM)); din("wo0", (DM, DM))
    din("kvdw", (DM, 512)); din("qdw", (DM, 512))
    din("wqasm", (512, DM))
    din("kuw", (512, 1024)); din("ropekw", (DM, 1024)); din("vuw", (512, DM))
    din("ow", (DM, DM))
    din("shg", (DM, 1024)); din("shu", (DM, 1024)); din("shd", (1024, DM))
    din("rg", (7 * DM, 1024)); din("ru", (7 * DM, 1024)); din("rd", (7 * 1024, DM))
    din("rw0", (DM, 7))
    din("wq1", (DM, DM)); din("wk1", (DM, 512)); din("wv1", (DM, 512)); din("wo1", (DM, DM))
    din("rw1", (DM, NH)); din("ew", (DM, DM))
    din("ident", (128, 128))
    din("rowsel", (128, NH * NH))
    din("hsel", (NH, NH * 128))
    din("permh", (128, 128)); din("permq", (128, 128)); din("permi", (128, 128))
    din("l0cos", (128, NTOK)); din("l0sin", (128, NTOK))
    din("l0cosq", (128, NTOK)); din("l0sinq", (128, NTOK))
    din("l1cos", (128, NTOK)); din("l1sin", (128, NTOK))
    din("masks", (6 * 128, 4 * TTW))
    din("foldF", (128, NH)); din("foldH", (128, NCORE)); din("repR", (NH, 128))
    din("selc", (NCORE, 128))
    din("esel", (7, 7 * 128))
    din("vecs", (128, VNVEC))

    d["yout"] = nc.dram_tensor("yout", [NTOK, DM], F32, kind="ExternalOutput")
    d["stash"] = nc.dram_tensor("stash", [128, NCH * NTOK], F32)
    d["qdsp"] = nc.dram_tensor("qdsp", [128, 4 * NTOK], F32)
    for tag, kd in (("a1", DM), ("a2", DM), ("a3", 512)):
        d["agki_" + tag] = nc.dram_tensor("agki_" + tag, [kd, NTOK], F32)
        d["agko_" + tag] = nc.dram_tensor("agko_" + tag, [2 * kd, NTOK], F32)
        d["agvi_" + tag] = nc.dram_tensor("agvi_" + tag, [NTOK, kd], F32)
        d["agvo_" + tag] = nc.dram_tensor("agvo_" + tag, [2 * NTOK, kd], F32)
    d["agsi"] = nc.dram_tensor("agsi", [NH, NTOK], F32)
    d["agso"] = nc.dram_tensor("agso", [NCORE * NH, NTOK], F32)

    with ExitStack() as st:
        tn = {}
        def sb(name, cols):
            tn[name] = st.enter_context(nc.sbuf_tensor(name, [128, cols], F32))
        sb("x", NCH * NTOK)
        sb("arena", ARENA)
        sb("wb", 2 * 2048)
        sb("cst", CST_END)
        sb("sm0", 1536)
        sb("sm1", 2048)
        sb("sm2", 64)
        for i in range(8):
            tn["ps%d" % i] = st.enter_context(
                nc.psum_tensor("ps%d" % i, [128, 512], F32))
        sems = {}
        for s in ("dwa", "dwb", "dka", "dkb", "dva", "dvb", "dm", "dx",
                  "dk", "dg", "gp", "cc", "mm", "sc", "ve",
                  "bar_tensor", "bar_vector", "bar_scalar", "bar_gpsimd",
                  "bar_sync"):
            sems[s] = st.enter_context(nc.semaphore(s))
        with nc.Block() as block:
            tensors = dict(tn)
            tensors.update(d)
            mk = MK(nc, block, tensors, sems)
            emit_all(mk)
    return nc


def emit_all(mk):
    p0_load(mk)
    if DBG[0] == "p0":
        return emit_output(mk)
    p1_l0_attn(mk)
    if DBG[0] == "p1":
        return emit_output(mk)
    p2_l0_ffn(mk)
    if DBG[0] == "p2":
        return emit_output(mk)
    p3_l1_attn(mk)
    if DBG[0] == "p3":
        return emit_output(mk)
    p4_l1_ffn(mk)
    emit_output(mk)


def p0_load(mk):
    S, t = mk.S, mk.t
    dkb = mk.n["dk"]

    def fsync(e):
        for dst_off, src, pn, cols in (
                (CST_ID, "ident", 128, 128), (CST_PERMH, "permh", 128, 128),
                (CST_PERMQ, "permq", 128, 128), (CST_PERMI, "permi", 128, 128),
                (CST_ROWSEL, "rowsel", 128, 256), (CST_FOLDF, "foldF", 128, 16),
                (CST_FOLDH, "foldH", 128, 8), (CST_SELC, "selc", 8, 128),
                (CST_ESEL, "esel", 7, 896), (CST_VECS, "vecs", 128, VNVEC),
                (CST_REPR, "repR", 16, 128)):
            e.dma_start(out=t["cst"][0:pn, dst_off:dst_off + cols],
                        in_=t[src].ap()).then_inc(S["dk"], 16)
        e.dma_start(out=t["sm1"][0:16, 0:2048], in_=t["hsel"].ap()).then_inc(S["dk"], 16)
    mk.sec("sync", fsync)
    mk.n["dk"] += 16 * 12

    def fvec(e):
        e.wait_ge(S["dk"], dkb + 16 * 12)
        e.memset(t["cst"][:, CST_ONES:CST_ONES + 8], 1.0)
        e.drain().then_inc(S["ve"], 1)
    mk.sec("vector", fvec)
    mk.n["ve"] += 1

    stg = 0
    dkb2 = mk.n["dx"]
    mmb = mk.n["mm"]
    veb = mk.n["ve"]
    xr = rr(t["x"].ap(), "p (c t) -> p c t", c=NCH)

    def fsync2(e):
        for g in range(8):
            if g >= 1:
                e.wait_ge(S["mm"], mmb + g * NCH)
            e.dma_start(out=mk.A(stg, 2048),
                        in_=t["xs"].ap()[g * 128:(g + 1) * 128, :]).then_inc(S["dx"], 16)
    mk.sec("sync", fsync2)
    mk.n["dx"] += 16 * 8

    def ftens(e):
        e.wait_ge(S["ve"], veb)
        for g in range(8):
            e.wait_ge(S["dx"], dkb2 + 16 * (g + 1))
            if g >= 1:
                e.wait_ge(S["ve"], veb + 4 * g)
            for dc in range(NCH):
                bank = t["ps%d" % (dc // 4)]
                e.transpose(bank[:, (dc % 4) * 128:(dc % 4) * 128 + 128],
                            mk.A(stg + dc * 128, 128),
                            mk.cst(CST_ID, 128))
            e.drain().then_inc(S["mm"], NCH)
    mk.sec("tensor", ftens)
    mk.n["mm"] += 8 * NCH

    def fvec2(e):
        for g in range(8):
            e.wait_ge(S["mm"], mmb + (g + 1) * NCH)
            for bk in range(4):
                dst = xr[:, bk * 4:(bk + 1) * 4, g * 128:(g + 1) * 128]
                src = rr(t["ps%d" % bk].ap()[:, 0:512], "p (b r) -> p b r", b=4)
                e.tensor_copy(dst, src)
            e.drain().then_inc(S["ve"], 4)
    mk.sec("vector", fvec2)
    mk.n["ve"] += 8 * 4
    mk.barrier()


def emit_output(mk):
    mk.barrier()
    S, t = mk.S, mk.t
    stg = 0
    mmb, veb, dgb = mk.n["mm"], mk.n["ve"], mk.n["dg"]

    def ftens(e):
        for g in range(8):
            if g >= 1:
                e.wait_ge(S["ve"], veb + 4 * g)
            for dc in range(NCH):
                bank = t["ps%d" % (dc // 4)]
                e.transpose(bank[:, (dc % 4) * 128:(dc % 4) * 128 + 128],
                            mk.xap(dc, g * 128, 128),
                            mk.cst(CST_ID, 128))
            e.drain().then_inc(S["mm"], NCH)
    mk.sec("tensor", ftens)
    mk.n["mm"] += 8 * NCH

    gpb = mk.n["gp"]
    def fvec(e):
        for g in range(8):
            e.wait_ge(S["mm"], mmb + (g + 1) * NCH)
            if g >= 1:
                e.wait_ge(S["gp"], gpb + g)
            for bk in range(4):
                e.tensor_copy(mk.A(stg + bk * 512, 512), t["ps%d" % bk][:, 0:512])
            e.drain().then_inc(S["ve"], 4)
    mk.sec("vector", fvec)
    mk.n["ve"] += 8 * 4

    def fg(g_):
        for g in range(8):
            g_.wait_ge(S["ve"], veb + 4 * (g + 1))
            g_.dma_start(out=t["yout"].ap()[g * 128:(g + 1) * 128, :],
                         in_=mk.A(stg, 2048)).then_inc(S["dg"], 16)
            g_.wait_ge(S["dg"], dgb + 16 * (g + 1))
            g_.nop().then_inc(S["gp"], 1)
    mk.sec("gpsimd", fg)
    mk.n["dg"] += 16 * 8
    mk.n["gp"] += 8


def ln_full(mk, sq_off, rms_wcol=None, dst_off=None, add_off=None):
    S, t = mk.S, mk.t
    is_ln = rms_wcol is None
    ve_prev_apply = None
    for tt in range(2):
        if add_off is not None:
            veb0 = mk.n["ve"]
            def fadd(e, tt=tt):
                for kc in range(NCH):
                    dst = mk.xap(kc, tt * TTW)
                    e.tensor_add(dst, dst, mk.A(add_off + kc * NTOK + tt * TTW, TTW))
                e.drain().then_inc(S["ve"], 1)
            mk.sec("vector", fadd)
            mk.n["ve"] += 1

        mmb, scb, veb = mk.n["mm"], mk.n["sc"], mk.n["ve"]

        def fsc(e, tt=tt, veb=veb):
            if add_off is not None:
                e.wait_ge(S["ve"], veb)
            for kc in range(NCH):
                if kc >= 1:
                    e.wait_ge(S["mm"], mmb + kc)
                e.activation(mk.A(sq_off, 512), mk.xap(kc, tt * TTW), AF.Square)
                e.drain().then_inc(S["sc"], 1)
        mk.sec("scalar", fsc)
        mk.n["sc"] += NCH

        def fte(e, tt=tt, ve_prev_apply=ve_prev_apply):
            if ve_prev_apply is not None:
                e.wait_ge(S["ve"], ve_prev_apply)
            for kc in range(NCH):
                e.wait_ge(S["sc"], scb + kc + 1)
                if is_ln:
                    e.matmul(t["ps6"][0:1, 0:TTW], mk.cst(CST_ONES, 1),
                             mk.xap(kc, tt * TTW),
                             start=(kc == 0), stop=(kc == NCH - 1))
                e.matmul(t["ps7"][0:1, 0:TTW], mk.cst(CST_ONES, 1),
                         mk.A(sq_off, 512),
                         start=(kc == 0), stop=(kc == NCH - 1))
                e.drain().then_inc(S["mm"], 1)
        mk.sec("tensor", fte)
        mk.n["mm"] += NCH

        mmb2 = mk.n["mm"]
        def fve(e, mmb2=mmb2):
            e.wait_ge(S["mm"], mmb2)
            if is_ln:
                e.tensor_scalar_mul(t["sm0"][0:1, 0:512], t["ps6"][0:1, 0:TTW], 1.0 / DM)
            e.tensor_scalar_mul(t["sm0"][0:1, 512:1024], t["ps7"][0:1, 0:TTW], 1.0 / DM)
            e.drain()
            if is_ln:
                e.tensor_mul(t["sm0"][0:1, 1024:1536], t["sm0"][0:1, 0:512],
                             t["sm0"][0:1, 0:512])
                e.drain()
                e.tensor_sub(t["sm0"][0:1, 512:1024], t["sm0"][0:1, 512:1024],
                             t["sm0"][0:1, 1024:1536])
                e.drain()
            e.tensor_scalar_add(t["sm0"][0:1, 512:1024], t["sm0"][0:1, 512:1024], 1e-5)
            e.drain()
            e.reciprocal(t["sm0"][0:1, 512:1024], t["sm0"][0:1, 512:1024])
            e.drain().then_inc(S["ve"], 1)
        mk.sec("vector", fve)
        mk.n["ve"] += 1

        veb2 = mk.n["ve"]
        def fsc2(e, veb2=veb2):
            e.wait_ge(S["ve"], veb2)
            e.activation(t["sm0"][0:1, 512:1024], t["sm0"][0:1, 512:1024], AF.Sqrt)
            e.drain().then_inc(S["sc"], 1)
        mk.sec("scalar", fsc2)
        mk.n["sc"] += 1

        scb2 = mk.n["sc"]
        if is_ln:
            def fve2(e, scb2=scb2):
                e.wait_ge(S["sc"], scb2)
                e.tensor_mul(t["sm0"][0:1, 1024:1536], t["sm0"][0:1, 0:512],
                             t["sm0"][0:1, 512:1024])
                e.drain()
                e.tensor_scalar_mul(t["sm0"][0:1, 1024:1536],
                                    t["sm0"][0:1, 1024:1536], -1.0)
                e.drain().then_inc(S["ve"], 1)
            mk.sec("vector", fve2)
            mk.n["ve"] += 1
        veb3 = mk.n["ve"]

        def fte2(e, scb2=scb2, veb3=veb3):
            e.wait_ge(S["sc"], scb2)
            e.wait_ge(S["ve"], veb3)
            e.matmul(t["ps4"][:, 0:TTW], mk.onesrow(), t["sm0"][0:1, 512:1024],
                     start=True, stop=True)
            if is_ln:
                e.matmul(t["ps5"][:, 0:TTW], mk.onesrow(), t["sm0"][0:1, 1024:1536],
                         start=True, stop=True)
            e.drain().then_inc(S["mm"], 1)
        mk.sec("tensor", fte2)
        mk.n["mm"] += 1

        mmb4 = mk.n["mm"]
        def fve3(e, tt=tt, mmb4=mmb4):
            e.wait_ge(S["mm"], mmb4)
            for kc in range(NCH):
                src = mk.xap(kc, tt * TTW)
                dst = src if dst_off is None else mk.A(dst_off + kc * NTOK + tt * TTW, TTW)
                e.tensor_mul(dst, src, t["ps4"][:, 0:TTW])
                if is_ln:
                    e.drain()
                    e.tensor_add(dst, dst, t["ps5"][:, 0:TTW])
            e.drain().then_inc(S["ve"], 1)
        mk.sec("vector", fve3)
        mk.n["ve"] += 1
        ve_prev_apply = mk.n["ve"]

        if not is_ln:
            veb5 = mk.n["ve"]
            def fsc3(e, tt=tt, veb5=veb5):
                e.wait_ge(S["ve"], veb5)
                for kc in range(NCH):
                    dst = mk.A(dst_off + kc * NTOK + tt * TTW, TTW)
                    e.activation(dst, dst, AF.Identity, scale=mk.vec(rms_wcol + kc))
                e.drain().then_inc(S["sc"], 1)
            mk.sec("scalar", fsc3)
            mk.n["sc"] += 1
    mk.barrier()


def fm_ln(mk, buf_off, tts, sq_off, extra_scale=1.0, dma_to=None, ncols=NTOK):
    S, t = mk.S, mk.t
    ntts = len(tts)
    ve_apply = {}
    for h in range(NH):
        for tti, tt in enumerate(tts):
            i = h * ntts + tti
            src = mk.A(buf_off + h * ncols + tti * TTW, TTW)
            stat = [t["ps6"], t["ps7"]] if i % 2 == 0 else [t["ps2"], t["ps3"]]
            bcA = t["ps4"] if i % 2 == 0 else t["ps5"]
            bcB = stat[0]
            scb, mmb, veb = mk.n["sc"], mk.n["mm"], mk.n["ve"]

            def fsc(e, src=src, i=i, mmb=mmb):
                if i >= 1:
                    e.wait_ge(S["mm"], mmb)
                e.activation(mk.A(sq_off, 512), src, AF.Square)
                e.drain().then_inc(S["sc"], 1)
            mk.sec("scalar", fsc)
            mk.n["sc"] += 1

            def fte(e, src=src, stat=stat, scb=scb, i=i):
                e.wait_ge(S["sc"], scb + 1)
                if i >= 2 and (i - 2) in ve_apply:
                    e.wait_ge(S["ve"], ve_apply[i - 2])
                e.matmul(stat[0][0:1, 0:TTW], mk.cst(CST_ONES, 1), src,
                         start=True, stop=True)
                e.matmul(stat[1][0:1, 0:TTW], mk.cst(CST_ONES, 1), mk.A(sq_off, 512),
                         start=True, stop=True)
                e.drain().then_inc(S["mm"], 1)
            mk.sec("tensor", fte)
            mk.n["mm"] += 1

            mmb2 = mk.n["mm"]
            def fve(e, stat=stat, mmb2=mmb2):
                e.wait_ge(S["mm"], mmb2)
                e.tensor_scalar_mul(t["sm0"][0:1, 0:512], stat[0][0:1, 0:TTW], 1.0 / 128)
                e.tensor_scalar_mul(t["sm0"][0:1, 512:1024], stat[1][0:1, 0:TTW], 1.0 / 128)
                e.drain()
                e.tensor_mul(t["sm0"][0:1, 1024:1536], t["sm0"][0:1, 0:512],
                             t["sm0"][0:1, 0:512])
                e.drain()
                e.tensor_sub(t["sm0"][0:1, 512:1024], t["sm0"][0:1, 512:1024],
                             t["sm0"][0:1, 1024:1536])
                e.drain()
                e.tensor_scalar_add(t["sm0"][0:1, 512:1024], t["sm0"][0:1, 512:1024], 1e-5)
                e.drain()
                e.reciprocal(t["sm0"][0:1, 512:1024], t["sm0"][0:1, 512:1024])
                e.drain().then_inc(S["ve"], 1)
            mk.sec("vector", fve)
            mk.n["ve"] += 1

            veb2 = mk.n["ve"]
            def fsc2(e, veb2=veb2):
                e.wait_ge(S["ve"], veb2)
                e.activation(t["sm0"][0:1, 512:1024], t["sm0"][0:1, 512:1024], AF.Sqrt)
                e.drain().then_inc(S["sc"], 1)
            mk.sec("scalar", fsc2)
            mk.n["sc"] += 1

            scb2 = mk.n["sc"]
            def fve2(e, scb2=scb2):
                e.wait_ge(S["sc"], scb2)
                if extra_scale != 1.0:
                    e.tensor_scalar_mul(t["sm0"][0:1, 512:1024],
                                        t["sm0"][0:1, 512:1024], float(extra_scale))
                    e.drain()
                e.tensor_mul(t["sm0"][0:1, 1024:1536], t["sm0"][0:1, 0:512],
                             t["sm0"][0:1, 512:1024])
                e.drain()
                e.tensor_scalar_mul(t["sm0"][0:1, 1024:1536],
                                    t["sm0"][0:1, 1024:1536], -1.0)
                e.drain().then_inc(S["ve"], 1)
            mk.sec("vector", fve2)
            mk.n["ve"] += 1

            veb3 = mk.n["ve"]
            def fte2(e, bcA=bcA, bcB=bcB, veb3=veb3):
                e.wait_ge(S["ve"], veb3)
                e.matmul(bcA[:, 0:TTW], mk.onesrow(), t["sm0"][0:1, 512:1024],
                         start=True, stop=True)
                e.matmul(bcB[:, 0:TTW], mk.onesrow(), t["sm0"][0:1, 1024:1536],
                         start=True, stop=True)
                e.drain().then_inc(S["mm"], 1)
            mk.sec("tensor", fte2)
            mk.n["mm"] += 1

            mmb4 = mk.n["mm"]
            def fve3(e, src=src, bcA=bcA, bcB=bcB, mmb4=mmb4):
                e.wait_ge(S["mm"], mmb4)
                e.tensor_mul(src, src, bcA[:, 0:TTW])
                e.drain()
                e.tensor_add(src, src, bcB[:, 0:TTW])
                e.drain().then_inc(S["ve"], 1)
            mk.sec("vector", fve3)
            mk.n["ve"] += 1
            ve_apply[i] = mk.n["ve"]

            if dma_to is not None:
                veb4 = mk.n["ve"]
                def fgd(g, src=src, h=h, tt=tt, veb4=veb4):
                    g.wait_ge(S["ve"], veb4)
                    g.dma_start(out=dma_to.ap()[h * 128:(h + 1) * 128,
                                                tt * TTW:(tt + 1) * TTW],
                                in_=src).then_inc(S["dg"], 16)
                mk.sec("gpsimd", fgd)
                mk.n["dg"] += 16


def rope_buf(mk, buf_off, chunks, tts, perm_off, cos_off, sin_off, tmp_off,
             ncols=NTOK):
    S, t = mk.S, mk.t
    veb0 = mk.n["ve"]
    mmb0 = mk.n["mm"]
    units = [(ch, tti, tt) for ch in chunks for tti, tt in enumerate(tts)]

    def fte(e):
        for i, (ch, tti, tt) in enumerate(units):
            if i >= 2:
                e.wait_ge(S["ve"], veb0 + 2 * (i - 1))
            else:
                e.wait_ge(S["ve"], veb0)
            src = mk.A(buf_off + ch * ncols + tti * TTW, TTW)
            bank = t["ps%d" % (6 + i % 2)]
            e.matmul(bank[:, 0:TTW], mk.cst(perm_off, 128), src,
                     start=True, stop=True)
            e.drain().then_inc(S["mm"], 1)
    mk.sec("tensor", fte)
    mk.n["mm"] += len(units)

    def fve(e):
        for i, (ch, tti, tt) in enumerate(units):
            e.wait_ge(S["mm"], mmb0 + i + 1)
            src = mk.A(buf_off + ch * ncols + tti * TTW, TTW)
            bank = t["ps%d" % (6 + i % 2)]
            cosap = mk.A(cos_off + tt * TTW, TTW)
            sinap = mk.A(sin_off + tt * TTW, TTW)
            e.drain()
            e.tensor_mul(mk.A(tmp_off, 512), bank[:, 0:TTW], sinap)
            e.tensor_mul(src, src, cosap)
            e.drain()
            e.tensor_add(src, src, mk.A(tmp_off, 512))
            e.drain().then_inc(S["ve"], 2)
    mk.sec("vector", fve)
    mk.n["ve"] += 2 * len(units)


def attn_core(mk, qc, agko, agvo, kdim, score, krow_fn, vcol_fn):
    S, t = mk.S, mk.t
    nkb = NKB[qc]
    NU = nkb * NH
    qb, ob, eb, mb, kv = ATT_QB, ATT_OB, ATT_EB, ATT_MB, ATT_KV

    mmb, scb, veb = mk.n["mm"], mk.n["sc"], mk.n["ve"]
    klane = ("dka", "dkb")
    vlane = ("dva", "dvb")
    kcnt = {0: mk.n["dka"], 1: mk.n["dkb"]}
    vcnt = {0: mk.n["dva"], 1: mk.n["dvb"]}
    dmb = mk.n["dm"]
    rec = {"mask": {}, "k": {}, "v": {}}

    def fsync(e):
        for kbi in range(nkb):
            s = SLOTS.index((qc, kbi))
            if kbi >= 1:
                e.wait_ge(S["ve"], veb + 5 * NH * kbi)
            e.dma_start(out=mk.A(mb, 2048),
                        in_=t["masks"].ap()[s * 128:(s + 1) * 128, :]).then_inc(S["dm"], 16)
            rec["mask"][kbi] = dmb + 16 * (kbi + 1)
            blk, c0 = KBSRC[kbi]
            for h in range(NH):
                u = kbi * NH + h
                if u >= 2:
                    e.wait_ge(S["mm"], mmb + 5 * (u - 1))
                kr = krow_fn(h)
                kcnt[u % 2] += 16
                e.dma_start(out=mk.A(kv + (u % 2) * 512, 512),
                            in_=agko.ap()[blk * kdim + kr:blk * kdim + kr + 128,
                                          c0:c0 + 512]).then_inc(S[klane[u % 2]], 16)
                rec["k"][u] = (klane[u % 2], kcnt[u % 2])
                vc = vcol_fn(h)
                vsrc = rr(agvo.ap(), "(b tk p) d -> p b tk d", p=128, b=2)
                vcnt[u % 2] += 16
                e.dma_start(out=rr(mk.A(kv + 1024 + (u % 2) * 512, 512),
                                   "p (tk d) -> p tk d", tk=4),
                            in_=vsrc[:, blk, c0 // 128:c0 // 128 + 4,
                                     vc:vc + 128]).then_inc(S[vlane[u % 2]], 16)
                rec["v"][u] = (vlane[u % 2], vcnt[u % 2])
    mk.sec("sync", fsync)
    mk.n["dm"] = dmb + 16 * nkb
    mk.n["dka"] = kcnt[0]
    mk.n["dkb"] = kcnt[1]
    mk.n["dva"] = vcnt[0]
    mk.n["dvb"] = vcnt[1]

    def fte(e):
        for kbi in range(nkb):
            for h in range(NH):
                u = kbi * NH + h
                e.wait_ge(S[rec["k"][u][0]], rec["k"][u][1])
                for kc in range(4):
                    g = 4 * u + kc
                    if g >= 2:
                        e.wait_ge(S["sc"], scb + g - 1)
                    sbk = t["ps%d" % (2 + g % 2)]
                    e.matmul(sbk[:, 0:TTW],
                             mk.A(kv + (u % 2) * 512 + kc * 128, 128),
                             mk.A(qb + h * 512, TTW),
                             start=True, stop=True).then_inc(S["mm"], 1)
                e.wait_ge(S["ve"], veb + 5 * u + 4)
                e.wait_ge(S[rec["v"][u][0]], rec["v"][u][1])
                if u >= 2:
                    e.wait_ge(S["ve"], veb + 5 * (u - 1))
                obk = t["ps%d" % (4 + h % 2)]
                ins = None
                for kc in range(4):
                    e.matmul(t["ps6"][0:16, 0:TTW], mk.rowsel(h),
                             mk.A(eb + (u % 2) * 2048 + kc * 512, 512),
                             start=(u == 0 and kc == 0),
                             stop=(u == NU - 1 and kc == 3))
                    ins = e.matmul(obk[:, 0:TTW],
                                   mk.A(kv + 1024 + (u % 2) * 512 + kc * 128, 128),
                                   mk.A(eb + (u % 2) * 2048 + kc * 512, 512),
                                   start=(kc == 0), stop=(kc == 3))
                ins.then_inc(S["mm"], 1)
    mk.sec("tensor", fte)
    mk.n["mm"] += 5 * NU

    def fsc(e):
        for kbi in range(nkb):
            for h in range(NH):
                u = kbi * NH + h
                for kc in range(4):
                    g = 4 * u + kc
                    e.wait_ge(S["mm"], mmb + 5 * u + kc + 1)
                    if u >= 2:
                        e.wait_ge(S["mm"], mmb + 5 * (u - 1))
                    sbk = t["ps%d" % (2 + g % 2)]
                    dst = mk.A(eb + (u % 2) * 2048 + kc * 512, 512)
                    if score[0] == "square":
                        e.activation(dst, sbk[:, 0:TTW], AF.Square)
                    else:
                        e.activation(dst, sbk[:, 0:TTW], AF.Exp, scale=float(score[1]))
                    e.drain().then_inc(S["sc"], 1)
    mk.sec("scalar", fsc)
    mk.n["sc"] += 4 * NU

    def fve(e):
        for kbi in range(nkb):
            for h in range(NH):
                u = kbi * NH + h
                e.wait_ge(S["dm"], rec["mask"][kbi])
                for kc in range(4):
                    e.wait_ge(S["sc"], scb + 4 * u + kc + 1)
                    ap = mk.A(eb + (u % 2) * 2048 + kc * 512, 512)
                    e.tensor_mul(ap, ap, mk.A(mb + kc * 512, 512))
                    e.drain().then_inc(S["ve"], 1)
                e.wait_ge(S["mm"], mmb + 5 * u + 5)
                obk = t["ps%d" % (4 + h % 2)]
                dst = mk.A(ob + h * 512, 512)
                e.drain()
                if kbi == 0:
                    e.tensor_copy(dst, obk[:, 0:TTW])
                else:
                    e.tensor_add(dst, dst, obk[:, 0:TTW])
                e.drain().then_inc(S["ve"], 1)
    mk.sec("vector", fve)
    mk.n["ve"] += 5 * NU


def znorm_apply(mk, z_eps):
    S, t = mk.S, mk.t
    ob = ATT_OB
    mmb = mk.n["mm"]

    def fve(e):
        e.wait_ge(S["mm"], mmb)
        src = t["ps6"][0:16, 0:TTW]
        if z_eps:
            e.tensor_scalar_add(t["sm0"][0:16, 0:512], src, float(z_eps))
            e.drain()
            e.reciprocal(t["sm0"][0:16, 0:512], t["sm0"][0:16, 0:512])
        else:
            e.reciprocal(t["sm0"][0:16, 0:512], src)
        e.drain().then_inc(S["ve"], 1)
    mk.sec("vector", fve)
    mk.n["ve"] += 1

    veb2 = mk.n["ve"]
    mmb2 = mk.n["mm"]
    def fte(e):
        e.wait_ge(S["ve"], veb2)
        for h in range(NH):
            if h >= 2:
                e.wait_ge(S["ve"], veb2 + h - 1)
            bank = t["ps%d" % (2 + h % 2)]
            e.matmul(bank[:, 0:TTW], mk.hsel(h), t["sm0"][0:16, 0:512],
                     start=True, stop=True)
            e.drain().then_inc(S["mm"], 1)
    mk.sec("tensor", fte)
    mk.n["mm"] += NH

    def fve2(e):
        e.drain()
        for h in range(NH):
            e.wait_ge(S["mm"], mmb2 + h + 1)
            bank = t["ps%d" % (2 + h % 2)]
            dst = mk.A(ob + h * 512, 512)
            e.tensor_mul(dst, dst, bank[:, 0:TTW])
            e.drain().then_inc(S["ve"], 1)
    mk.sec("vector", fve2)
    mk.n["ve"] += NH


def proj_vT(mk, W_ap, Kdim, mts, rhs_fn, agvi, vt_off, bias_col=None):
    S, t = mk.S, mk.t
    vt = vt_off
    vs = vt_off + 1024
    nkc = Kdim // 128
    mmb, scb, veb, dgb = (mk.n["mm"], mk.n["sc"], mk.n["ve"], mk.n["dg"])
    lane = ("dwa", "dwb")
    lb = [mk.n["dwa"], mk.n["dwb"]]
    Wr = rr(W_ap, "(kc p) m -> p kc m", p=128)
    ntl = len(mts) * 2

    def fsync(e):
        for j, (c0, w) in enumerate(mts):
            if j >= 2:
                e.wait_ge(S["mm"], mmb + 2 * ((j - 1) * 2))
            else:
                if mk.wb_last[j % 2]:
                    e.wait_ge(S["mm"], mk.wb_last[j % 2])
            e.dma_start(out=mk.wpanel_dst(j % 2, nkc, w),
                        in_=Wr[:, :, c0:c0 + w]).then_inc(S[lane[j % 2]], 16)
    mk.sec("sync", fsync)
    mk.n["dwa"] += 16 * ((len(mts) + 1) // 2)
    mk.n["dwb"] += 16 * (len(mts) // 2)
    for sslot in range(2):
        js = [j for j in range(len(mts)) if j % 2 == sslot]
        if js:
            mk.wb_last[sslot] = mmb + 2 * ntl

    recp = {}
    rect = {}

    def fte(e):
        cnt = mmb
        for j, (c0, w) in enumerate(mts):
            e.wait_ge(S[lane[j % 2]], lb[j % 2] + 16 * (j // 2 + 1))
            for tt in range(2):
                i = j * 2 + tt
                if i >= 2:
                    e.wait_ge(S["sc"], scb + i - 1)
                elif mk.ps01_last[i % 2] is not None:
                    e.wait_ge(S[mk.ps01_last[i % 2][0]], mk.ps01_last[i % 2][1])
                ps = t["ps%d" % (i % 2)]
                ins = None
                for kc in range(nkc):
                    ins = e.matmul(ps[0:w, 0:TTW], mk.wtile(j % 2, kc, w),
                                   rhs_fn(kc, tt), start=(kc == 0),
                                   stop=(kc == nkc - 1))
                ins.then_inc(S["mm"], 1)
                cnt += 1
                recp[i] = cnt
                if i >= 1:
                    it = i - 1
                    e.wait_ge(S["sc"], scb + it + 1)
                    if it >= 2:
                        e.wait_ge(S["ve"], veb + it - 1)
                    bank = t["ps%d" % (2 + it % 2)]
                    for b in range(4):
                        e.transpose(bank[:, b * 128:(b + 1) * 128],
                                    mk.A(vt + (it % 2) * 512 + b * 128, 128),
                                    mk.cst(CST_ID, 128))
                    e.drain().then_inc(S["mm"], 1)
                    cnt += 1
                    rect[it] = cnt
        it = ntl - 1
        e.wait_ge(S["sc"], scb + it + 1)
        e.wait_ge(S["ve"], veb + it - 1)
        bank = t["ps%d" % (2 + it % 2)]
        for b in range(4):
            e.transpose(bank[:, b * 128:(b + 1) * 128],
                        mk.A(vt + (it % 2) * 512 + b * 128, 128),
                        mk.cst(CST_ID, 128))
        e.drain().then_inc(S["mm"], 1)
        cnt += 1
        rect[it] = cnt
    mk.sec("tensor", fte)
    mk.n["mm"] += 2 * ntl
    for par in range(2):
        iis = [ii2 for ii2 in range(ntl) if ii2 % 2 == par]
        if iis:
            mk.ps01_last[par] = ("sc", mk.n["sc"] + iis[-1] + 1)

    def fsc(e):
        for i in range(ntl):
            e.wait_ge(S["mm"], recp[i])
            if i >= 2:
                e.wait_ge(S["mm"], rect[i - 2])
            ps = t["ps%d" % (i % 2)]
            j = i // 2
            if bias_col is not None:
                e.activation(mk.A(vt + (i % 2) * 512, 512), ps[:, 0:TTW],
                             AF.Identity, bias=mk.vec(bias_col + j))
            else:
                e.activation(mk.A(vt + (i % 2) * 512, 512), ps[:, 0:TTW], AF.Identity)
            e.drain().then_inc(S["sc"], 1)
    mk.sec("scalar", fsc)
    mk.n["sc"] += ntl

    gpb = mk.n["gp"]
    def fve(e):
        for i in range(ntl):
            e.wait_ge(S["mm"], rect[i])
            if i >= 2:
                e.wait_ge(S["gp"], gpb + i - 1)
            bank = t["ps%d" % (2 + i % 2)]
            e.tensor_copy(mk.A(vs + (i % 2) * 512, 512), bank[:, 0:512])
            e.drain().then_inc(S["ve"], 1)
    mk.sec("vector", fve)
    mk.n["ve"] += ntl

    def fg(g):
        for i in range(ntl):
            g.wait_ge(S["ve"], veb + i + 1)
            j, tt = i // 2, i % 2
            c0 = mts[j][0]
            dst = rr(agvi.ap(), "(tk p) d -> p tk d", p=128)[
                :, tt * 4:(tt + 1) * 4, c0:c0 + 128]
            src = rr(mk.A(vs + (i % 2) * 512, 512), "p (b d) -> p b d", b=4)
            g.dma_start(out=dst, in_=src).then_inc(S["dg"], 16)
            g.wait_ge(S["dg"], dgb + 16 * (i + 1))
            g.nop().then_inc(S["gp"], 1)
    mk.sec("gpsimd", fg)
    mk.n["dg"] += 16 * ntl
    mk.n["gp"] += ntl


def allgather_pair(mk, pairs):
    S = mk.S
    dgt = mk.n["dg"]
    groups = [[0, 1], [2, 3], [4, 5], [6, 7]]
    def fg(g):
        g.wait_ge(S["dg"], dgt)
        for p in pairs:
            g.collective_compute(
                "AllGather", OP.bypass, replica_groups=groups,
                ins=[p[0].ap().opt()],
                outs=[p[1].ap().opt()],
            ).then_inc(S["cc"], 1)
    mk.sec("gpsimd", fg)
    mk.n["cc"] += len(pairs)


def wait_cc(mk, val):
    S = mk.S
    def f(e):
        e.wait_ge(S["cc"], val)
    mk.sec("sync", f)


def p1_l0_attn(mk):
    S, t = mk.S, mk.t
    kq = 0
    def ev(e, i, j, tt, ps):
        e.activation(mk.A(kq + j * NTOK + tt * TTW, TTW), ps, AF.Identity,
                     scale=mk.vec(VGAM), bias=mk.vec(VBET))
    mk.proj(t["wk0"].ap(), DM, MTS16, (0, 1),
            lambda kc, tt: mk.xap(kc, tt * TTW), ev, eng="scalar")
    mk.barrier()
    fm_ln(mk, kq, (0, 1), sq_off=16384, dma_to=t["agki_a1"])
    mk.barrier()

    proj_vT(mk, t["wv0"].ap(), DM, MTS16,
            lambda kc, tt: mk.xap(kc, tt * TTW), t["agvi_a1"], vt_off=0)
    allgather_pair(mk, [(t["agki_a1"], t["agko_a1"]), (t["agvi_a1"], t["agvo_a1"])])
    mk.barrier()

    ccq = mk.n["cc"]
    for qc in range(2):
        def evq(e, i, j, tt, ps):
            e.activation(mk.A(ATT_QB + j * 512, TTW), ps, AF.Identity,
                         scale=mk.vec(VGAM), bias=mk.vec(VBET))
        mk.proj(t["wq0"].ap(), DM, MTS16, (qc,),
                lambda kc, tt: mk.xap(kc, tt * TTW), evq, eng="scalar")
        mk.barrier()
        fm_ln(mk, ATT_QB, (qc,), sq_off=ATT_EB, extra_scale=128.0 ** -0.5,
              ncols=512)
        wait_cc(mk, ccq)
        mk.barrier()
        attn_core(mk, qc, t["agko_a1"], t["agvo_a1"], DM, ("square",),
                  lambda h: h * 128, lambda h: h * 128)
        znorm_apply(mk, 1e-5)
        mk.barrier()
        def evo(e, i, j, tt, ps):
            dst = mk.xap(j, tt * TTW)
            e.tensor_add(dst, dst, ps)
        mk.proj(t["wo0"].ap(), DM, MTS16, (qc,),
                lambda kc, tt: mk.A(ATT_OB + kc * 512, 512), evo, eng="vector")
        mk.barrier()
    ln_full(mk, sq_off=0)


def p2_l0_ffn(mk):
    S, t = mk.S, mk.t
    dgb = mk.n["dg"]
    def fg(g):
        g.dma_start(out=t["stash"].ap(), in_=t["x"][:, :]).then_inc(S["dg"], 16)
    mk.sec("gpsimd", fg)
    mk.n["dg"] += 16
    mk.barrier()

    xn = 0
    ln_full(mk, sq_off=16384, rms_wcol=VL0W1, dst_off=xn)

    kvd = 16384
    qd = 20480
    mk.proj(t["kvdw"].ap(), DM, MTS4, (0, 1),
            lambda kc, tt: mk.A(xn + kc * NTOK + tt * TTW, TTW),
            _ev_copy(mk, kvd), eng="vector")
    mk.proj(t["qdw"].ap(), DM, MTS4, (0, 1),
            lambda kc, tt: mk.A(xn + kc * NTOK + tt * TTW, TTW),
            _ev_copy(mk, qd), eng="vector")
    mk.barrier()
    def fgq(g):
        g.dma_start(out=t["qdsp"].ap(), in_=mk.A(qd, 4 * NTOK)).then_inc(S["dg"], 16)
        g.wait_ge(S["dg"], mk.n["dg"] + 16)
        g.nop().then_inc(S["gp"], 1)
    mk.sec("gpsimd", fgq)
    mk.n["dg"] += 16
    mk.n["gp"] += 1
    gp_spill = mk.n["gp"]

    kp = 24576
    def evkp(e, i, j, tt, ps):
        e.tensor_copy(mk.A(kp + (i % 2) * 512, 512), ps)
    def dmakp(g, i, j, tt):
        for half in range(2):
            h = 2 * j + half
            src = mk.A(kp + (i % 2) * 512, 512, p0=half * 64, pn=64)
            g.dma_start(out=t["agki_a2"].ap()[h * 128:h * 128 + 64,
                                              tt * TTW:(tt + 1) * TTW],
                        in_=src).then_inc(S["dg"], 16)
    mk.proj(t["kuw"].ap(), 512, MTS8, (0, 1),
            lambda kc, tt: mk.A(kvd + kc * NTOK + tt * TTW, TTW),
            evkp, eng="vector", dma_out=dmakp, dma_per_tile=2)
    mk.barrier()

    proj_vT(mk, t["vuw"].ap(), 512, MTS16,
            lambda kc, tt: mk.A(kvd + kc * NTOK + tt * TTW, TTW),
            t["agvi_a2"], vt_off=24576)
    mk.barrier()

    tabs = 24576
    dkb = mk.n["dx"]
    def fsy(e):
        e.dma_start(out=mk.A(tabs, NTOK), in_=t["l0cos"].ap()).then_inc(S["dx"], 16)
        e.dma_start(out=mk.A(tabs + NTOK, NTOK), in_=t["l0sin"].ap()).then_inc(S["dx"], 16)
    mk.sec("sync", fsy)
    mk.n["dx"] += 32
    krb = 16384
    for half in range(2):
        mts = MTS8[half * 4:(half + 1) * 4]
        def evkr(e, i, j, tt, ps):
            e.tensor_copy(mk.A(krb + j * NTOK + tt * TTW, TTW), ps)
        mk.proj(t["ropekw"].ap()[:, half * 512:(half + 1) * 512], DM,
                MTS4, (0, 1),
                lambda kc, tt: mk.A(xn + kc * NTOK + tt * TTW, TTW),
                evkr, eng="vector")
        if half == 0:
            def fw(e, dkb=dkb):
                e.wait_ge(S["dx"], dkb + 32)
            mk.sec("vector", fw)
        mk.barrier()
        rope_buf(mk, krb, list(range(4)), (0, 1),
                 CST_PERMH, tabs, tabs + NTOK, tmp_off=krb + 4 * NTOK)
        veb = mk.n["ve"]
        def fgkr(g, half=half, veb=veb):
            g.wait_ge(S["ve"], veb)
            for j in range(4):
                for hh in range(2):
                    h = 2 * (half * 4 + j) + hh
                    g.dma_start(out=t["agki_a2"].ap()[h * 128 + 64:h * 128 + 128, :],
                                in_=mk.A(krb + j * NTOK, NTOK, p0=hh * 64, pn=64)
                                ).then_inc(S["dg"], 16)
        mk.sec("gpsimd", fgkr)
        mk.n["dg"] += 16 * 8
        mk.barrier()
    allgather_pair(mk, [(t["agki_a2"], t["agko_a2"]), (t["agvi_a2"], t["agvo_a2"])])
    mk.barrier()

    ccq = mk.n["cc"]
    for qc in range(2):
        dkb2 = mk.n["dx"]
        def fsy2(e):
            e.dma_start(out=mk.A(ATT_QTAB, NTOK), in_=t["l0cosq"].ap()).then_inc(S["dx"], 16)
            e.dma_start(out=mk.A(ATT_QTAB + NTOK, NTOK), in_=t["l0sinq"].ap()).then_inc(S["dx"], 16)
        mk.sec("sync", fsy2)
        mk.n["dx"] += 32
        def fgz(g):
            g.wait_ge(S["gp"], gp_spill)
            g.dma_start(out=mk.A(ATT_EB, 4096), in_=t["qdsp"].ap()).then_inc(S["dg"], 16)
            g.wait_ge(S["dg"], mk.n["dg"] + 16)
            g.nop().then_inc(S["gp"], 1)
        mk.sec("gpsimd", fgz)
        mk.n["dg"] += 16
        mk.n["gp"] += 1
        gp3 = mk.n["gp"]
        def fwv(e):
            e.wait_ge(S["gp"], gp3)
            e.wait_ge(S["dx"], dkb2 + 32)
        mk.sec("tensor", fwv)
        mk.barrier()

        def evq(e, i, j, tt, ps):
            e.tensor_copy(mk.A(ATT_QB + j * 512, TTW), ps)
        mk.proj(t["wqasm"].ap(), 512, MTS16, (qc,),
                lambda kc, tt: mk.A(ATT_EB + kc * NTOK + tt * TTW, TTW),
                evq, eng="vector")
        mk.barrier()
        rope_buf(mk, ATT_QB, list(range(NH)), (qc,), CST_PERMQ,
                 ATT_QTAB, ATT_QTAB + NTOK, tmp_off=ATT_MB, ncols=512)
        wait_cc(mk, ccq)
        mk.barrier()
        attn_core(mk, qc, t["agko_a2"], t["agvo_a2"], DM, ("exp", 128.0 ** -0.5),
                  lambda h: h * 128, lambda h: h * 128)
        znorm_apply(mk, None)
        mk.barrier()
        def evo(e, i, j, tt, ps):
            dst = mk.xap(j, tt * TTW)
            e.tensor_add(dst, dst, ps)
        mk.proj(t["ow"].ap(), DM, MTS16, (qc,),
                lambda kc, tt: mk.A(ATT_OB + kc * 512, 512), evo, eng="vector")
        mk.barrier()

    xn2 = 0
    ln_full(mk, sq_off=16384, rms_wcol=VL0W2, dst_off=xn2)
    moe_router(mk, xn2)
    moe_experts(mk, xn2)

    def fg2(g):
        g.dma_start(out=mk.A(0, NCH * NTOK), in_=t["stash"].ap()).then_inc(S["dg"], 16)
        g.wait_ge(S["dg"], mk.n["dg"] + 16)
        g.nop().then_inc(S["gp"], 1)
    mk.sec("gpsimd", fg2)
    mk.n["dg"] += 16
    mk.n["gp"] += 1
    gp5 = mk.n["gp"]
    def fw3(e):
        e.wait_ge(S["gp"], gp5)
    mk.sec("vector", fw3)
    ln_full(mk, sq_off=16384, add_off=0)


def _ev_copy(mk, dst_off, ncols=NTOK):
    def ev(e, i, j, tt, ps):
        e.tensor_copy(mk.A(dst_off + j * ncols + tt * TTW, TTW), ps)
    return ev


def moe_router(mk, xn2):
    S, t = mk.S, mk.t
    def ev(e, i, j, tt, ps):
        e.activation(t["sm0"][0:7, tt * TTW:(tt + 1) * TTW], ps[0:7, 0:TTW],
                     AF.Sigmoid, bias=mk.vec(VRB0, pn=7))
    mk.proj(t["rw0"].ap(), DM, [(0, 7)], (0, 1),
            lambda kc, tt: mk.A(xn2 + kc * NTOK + tt * TTW, TTW), ev, eng="scalar")
    mk.barrier()

    ptm = 1024
    wtm = 1088
    mmb, veb = mk.n["mm"], mk.n["ve"]

    def fte(e):
        for g in range(8):
            if g >= 2:
                e.wait_ge(S["ve"], veb + g - 1)
            bank = t["ps%d" % (2 + g % 2)]
            e.transpose(bank[0:128, 0:7], t["sm0"][0:7, g * 128:(g + 1) * 128],
                        mk.cst(CST_ID, 7, pn=7))
            e.drain().then_inc(S["mm"], 1)
    mk.sec("tensor", fte)
    mk.n["mm"] += 8

    def fve(e):
        for g in range(8):
            e.wait_ge(S["mm"], mmb + g + 1)
            bank = t["ps%d" % (2 + g % 2)]
            e.tensor_copy(t["sm0"][:, ptm + g * 8: ptm + g * 8 + 7], bank[:, 0:7])
            e.memset(t["sm0"][:, ptm + g * 8 + 7: ptm + g * 8 + 8], -1.0)
            e.drain().then_inc(S["ve"], 1)
    mk.sec("vector", fve)
    mk.n["ve"] += 8

    veb2 = mk.n["ve"]
    def fve2(e):
        e.wait_ge(S["ve"], veb2)
        for g in range(8):
            pg = t["sm0"][:, ptm + g * 8: ptm + g * 8 + 8]
            wg = t["sm0"][:, wtm + g * 8: wtm + g * 8 + 8]
            e.drain()
            e.max(out=t["sm2"][:, 0:8], in_=pg)
            e.drain()
            e.tensor_copy(t["sm2"][:, 8:10], t["sm2"][:, 0:2])
            e.memset(t["sm2"][:, 10:16], -2.0)
            e.drain()
            e.match_replace(out=wg, in_to_replace=t["sm2"][:, 8:16],
                            in_values=pg, imm_value=0.0)
            e.tensor_add(t["sm2"][:, 16:17], t["sm2"][:, 0:1], t["sm2"][:, 1:2])
            e.drain()
            e.reciprocal(t["sm2"][:, 16:17], t["sm2"][:, 16:17])
            e.drain()
            e.tensor_sub(wg, pg, wg)
            e.drain()
            e.tensor_single_scalar(wg, wg, t["sm2"][:, 16:17], OP.mult)
        e.drain().then_inc(S["ve"], 1)
    mk.sec("vector", fve2)
    mk.n["ve"] += 1

    veb3 = mk.n["ve"]
    mmb3 = mk.n["mm"]
    def fte2(e):
        e.wait_ge(S["ve"], veb3)
        for g in range(8):
            bank = t["ps%d" % (2 + (g // 4) % 2)]
            e.transpose(bank[0:8, (g % 4) * 128:(g % 4) * 128 + 128],
                        t["sm0"][:, wtm + g * 8: wtm + g * 8 + 8], mk.cst(CST_ID, 128))
            if g % 4 == 3:
                e.drain().then_inc(S["mm"], 1)
    mk.sec("tensor", fte2)
    mk.n["mm"] += 2

    wfm = 16384
    def fve3(e):
        for half in range(2):
            e.wait_ge(S["mm"], mmb3 + half + 1)
            bank = t["ps%d" % (2 + half % 2)]
            e.tensor_copy(mk.A(wfm + half * 512, 512, pn=8), bank[0:8, 0:512])
        e.drain().then_inc(S["ve"], 1)
    mk.sec("vector", fve3)
    mk.n["ve"] += 1
    mk.barrier()
    return wfm


WFM_OFF = 16384


def moe_experts(mk, xn2):
    S, t = mk.S, mk.t
    hb = 17408
    wB = 25600
    for ei in range(8):
        routed = ei >= 1
        e_idx = ei - 1
        if routed:
            mmb = mk.n["mm"]
            veb = mk.n["ve"]
            def fteb(e, e_idx=e_idx, veb=veb):
                e.wait_ge(S["ve"], veb)
                for tt in range(2):
                    e.matmul(t["ps%d" % (6 + tt)][:, 0:TTW],
                             mk.cst(CST_ESEL + e_idx * 128, 128, pn=7),
                             mk.A(WFM_OFF + tt * TTW, TTW, pn=7),
                             start=True, stop=True)
                    e.drain().then_inc(S["mm"], 1)
            mk.sec("tensor", fteb)
            mk.n["mm"] += 2
            def fveb(e, mmb=mmb):
                for tt in range(2):
                    e.wait_ge(S["mm"], mmb + tt + 1)
                    e.tensor_scalar_mul(mk.A(wB + tt * TTW, TTW),
                                        t["ps%d" % (6 + tt)][:, 0:TTW], 1.0 / 8.0)
                e.drain().then_inc(S["ve"], 1)
            mk.sec("vector", fveb)
            mk.n["ve"] += 1
            mk.barrier()
            Wg = t["rg"].ap()[e_idx * DM:(e_idx + 1) * DM, :]
            Wu = t["ru"].ap()[e_idx * DM:(e_idx + 1) * DM, :]
            Wd = t["rd"].ap()[e_idx * 1024:(e_idx + 1) * 1024, :]
        else:
            Wg, Wu, Wd = t["shg"].ap(), t["shu"].ap(), t["shd"].ap()

        gate_up(mk, xn2, Wg, Wu, hb, wB if routed else None)
        mk.barrier()
        def evd(e, i, j, tt, ps):
            dst = mk.xap(j, tt * TTW)
            e.tensor_add(dst, dst, ps)
        mk.proj(Wd, 1024, MTS16, (0, 1),
                lambda kc, tt: mk.A(hb + kc * NTOK + tt * TTW, TTW), evd,
                eng="vector")
        mk.barrier()


def gate_up(mk, xn2, Wg, Wu, hb, wB):
    S, t = mk.S, mk.t
    nkc = NCH
    mmb, scb, veb = mk.n["mm"], mk.n["sc"], mk.n["ve"]
    ga, gb = mk.n["dwa"], mk.n["dwb"]
    Wgr = rr(Wg, "(kc p) m -> p kc m", p=128)
    Wur = rr(Wu, "(kc p) m -> p kc m", p=128)

    def fsync(e):
        for j in range(8):
            if j >= 1:
                e.wait_ge(S["mm"], mmb + (j - 1) * 4 + 4)
            else:
                for sslot in range(2):
                    if mk.wb_last[sslot]:
                        e.wait_ge(S["mm"], mk.wb_last[sslot])
            e.dma_start(out=mk.wpanel_dst(0, nkc, 128),
                        in_=Wgr[:, :, j * 128:(j + 1) * 128]).then_inc(S["dwa"], 16)
            e.dma_start(out=mk.wpanel_dst(1, nkc, 128),
                        in_=Wur[:, :, j * 128:(j + 1) * 128]).then_inc(S["dwb"], 16)
    mk.sec("sync", fsync)
    mk.n["dwa"] += 16 * 8
    mk.n["dwb"] += 16 * 8
    mk.wb_last[0] = mk.wb_last[1] = mk.n["mm"] + 8 * 2 * 2

    def fte(e):
        i = 0
        for j in range(8):
            e.wait_ge(S["dwa"], ga + 16 * (j + 1))
            e.wait_ge(S["dwb"], gb + 16 * (j + 1))
            for tt in range(2):
                if i >= 1:
                    e.wait_ge(S["ve"], veb + i)
                else:
                    for par in range(2):
                        if mk.ps01_last[par] is not None:
                            e.wait_ge(S[mk.ps01_last[par][0]],
                                      mk.ps01_last[par][1])
                for kc in range(nkc):
                    e.matmul(t["ps0"][:, 0:TTW], mk.wtile(0, kc, 128),
                             mk.A(xn2 + kc * NTOK + tt * TTW, TTW),
                             start=(kc == 0), stop=(kc == nkc - 1))
                e.drain().then_inc(S["mm"], 1)
                ins = None
                for kc in range(nkc):
                    ins = e.matmul(t["ps1"][:, 0:TTW], mk.wtile(1, kc, 128),
                                   mk.A(xn2 + kc * NTOK + tt * TTW, TTW),
                                   start=(kc == 0), stop=(kc == nkc - 1))
                ins.then_inc(S["mm"], 1)
                i += 1
    mk.sec("tensor", fte)
    mk.n["mm"] += 8 * 2 * 2

    def fsc(e):
        for i in range(16):
            j, tt = i // 2, i % 2
            e.wait_ge(S["mm"], mmb + 2 * i + 1)
            dst = mk.A(hb + j * NTOK + tt * TTW, TTW)
            e.activation(dst, t["ps0"][:, 0:TTW], AF.Silu)
            e.drain().then_inc(S["sc"], 1)
    mk.sec("scalar", fsc)
    mk.n["sc"] += 16

    def fve(e):
        for i in range(16):
            j, tt = i // 2, i % 2
            e.wait_ge(S["sc"], scb + i + 1)
            e.wait_ge(S["mm"], mmb + 2 * i + 2)
            dst = mk.A(hb + j * NTOK + tt * TTW, TTW)
            e.tensor_mul(dst, dst, t["ps1"][:, 0:TTW])
            e.drain()
            if wB is None:
                e.tensor_scalar_mul(dst, dst, 1.0 / 8.0)
            else:
                e.tensor_mul(dst, dst, mk.A(wB + tt * TTW, TTW))
            e.drain().then_inc(S["ve"], 1)
    mk.sec("vector", fve)
    mk.n["ve"] += 16
    mk.ps01_last[0] = mk.ps01_last[1] = ("ve", mk.n["ve"])


def p3_l1_attn(mk):
    S, t = mk.S, mk.t
    tabs = 0
    dkb = mk.n["dk"]
    def fsy(e):
        e.dma_start(out=mk.A(tabs, NTOK), in_=t["l1cos"].ap()).then_inc(S["dx"], 16)
        e.dma_start(out=mk.A(tabs + NTOK, NTOK), in_=t["l1sin"].ap()).then_inc(S["dx"], 16)
    mk.sec("sync", fsy)
    mk.n["dx"] += 32

    kb = 2048
    def evk(e, i, j, tt, ps):
        e.activation(mk.A(kb + j * NTOK + tt * TTW, TTW), ps, AF.Identity,
                     bias=mk.vec(VBK1 + j))
    mk.proj(t["wk1"].ap(), DM, MTS4, (0, 1),
            lambda kc, tt: mk.xap(kc, tt * TTW), evk, eng="scalar")
    mk.barrier()
    nrm_factors(mk, kb, 4, None, sq_off=6144)
    def fwk(e):
        e.wait_ge(S["dx"], dkb + 32)
    mk.sec("vector", fwk)
    rope_buf(mk, kb, list(range(4)), (0, 1), CST_PERMI, tabs, tabs + NTOK,
             tmp_off=6144)
    apply_fac(mk, kb, 4)
    veb = mk.n["ve"]
    def fgk(g):
        g.wait_ge(S["ve"], veb)
        for j in range(4):
            g.dma_start(out=t["agki_a3"].ap()[j * 128:(j + 1) * 128, :],
                        in_=mk.A(kb + j * NTOK, NTOK)).then_inc(S["dg"], 16)
    mk.sec("gpsimd", fgk)
    mk.n["dg"] += 16 * 4
    mk.barrier()

    proj_vT(mk, t["wv1"].ap(), DM, MTS4,
            lambda kc, tt: mk.xap(kc, tt * TTW), t["agvi_a3"],
            vt_off=6656, bias_col=VBV1)
    allgather_pair(mk, [(t["agki_a3"], t["agko_a3"]), (t["agvi_a3"], t["agvo_a3"])])
    mk.barrier()

    ccq = mk.n["cc"]
    for qc in range(2):
        def evq(e, i, j, tt, ps):
            e.activation(mk.A(ATT_QB + j * 512, TTW), ps, AF.Identity,
                         bias=mk.vec(VBQ1 + j))
        mk.proj(t["wq1"].ap(), DM, MTS16, (qc,),
                lambda kc, tt: mk.xap(kc, tt * TTW), evq, eng="scalar")
        mk.barrier()
        nrm_factors(mk, ATT_QB, NH, VALP, tts=(qc,), ncols=512, sq_off=ATT_EB)
        rope_buf(mk, ATT_QB, list(range(NH)), (qc,), CST_PERMI,
                 ATT_QTAB, ATT_QTAB + NTOK, tmp_off=ATT_MB, ncols=512)
        apply_fac(mk, ATT_QB, NH, tts=(0,), ncols=512)
        wait_cc(mk, ccq)
        mk.barrier()
        attn_core(mk, qc, t["agko_a3"], t["agvo_a3"], 512, ("exp", 1.0),
                  lambda h: (h // 4) * 128, lambda h: (h // 4) * 128)
        znorm_apply(mk, None)
        mk.barrier()
        def evo(e, i, j, tt, ps):
            dst = mk.xap(j, tt * TTW)
            e.scalar_tensor_tensor(dst, ps, mk.vec(VBO1 + j), dst, OP.add, OP.add)
        mk.proj(t["wo1"].ap(), DM, MTS16, (qc,),
                lambda kc, tt: mk.A(ATT_OB + kc * 512, 512), evo, eng="vector")
        mk.barrier()
    ln_full(mk, sq_off=0)


def nrm_factors(mk, buf, nch, alpha_col, sq_off, tts=(0, 1), ncols=NTOK):
    S, t = mk.S, mk.t
    for tti, tt in enumerate(tts):
        scb, mmb = mk.n["sc"], mk.n["mm"]
        def fsc(e, tti=tti, mmb=mmb):
            for ch in range(nch):
                if ch >= 1:
                    e.wait_ge(S["mm"], mmb + ch)
                e.activation(mk.A(sq_off, 512),
                             mk.A(buf + ch * ncols + tti * TTW, TTW), AF.Square)
                e.drain().then_inc(S["sc"], 1)
        mk.sec("scalar", fsc)
        mk.n["sc"] += nch

        def fte(e, tti=tti, scb=scb):
            for ch in range(nch):
                e.wait_ge(S["sc"], scb + ch + 1)
                e.matmul(t["ps6"][0:16, 0:TTW], mk.rowsel(ch), mk.A(sq_off, 512),
                         start=(ch == 0), stop=(ch == nch - 1))
                e.drain().then_inc(S["mm"], 1)
        mk.sec("tensor", fte)
        mk.n["mm"] += nch

        mmb2 = mk.n["mm"]
        def fve(e, tti=tti, mmb2=mmb2):
            e.wait_ge(S["mm"], mmb2)
            e.tensor_copy(t["sm0"][0:16, tti * 512:(tti + 1) * 512],
                          t["ps6"][0:16, 0:TTW])
            e.drain().then_inc(S["ve"], 1)
        mk.sec("vector", fve)
        mk.n["ve"] += 1

        veb2 = mk.n["ve"]
        def fsc2(e, tti=tti, veb2=veb2):
            e.wait_ge(S["ve"], veb2)
            e.activation(t["sm0"][0:16, tti * 512:(tti + 1) * 512],
                         t["sm0"][0:16, tti * 512:(tti + 1) * 512], AF.Sqrt)
            e.drain().then_inc(S["sc"], 1)
        mk.sec("scalar", fsc2)
        mk.n["sc"] += 1

        scb2 = mk.n["sc"]
        def fve2(e, tti=tti, scb2=scb2):
            e.wait_ge(S["sc"], scb2)
            ap = t["sm0"][0:16, tti * 512:(tti + 1) * 512]
            e.tensor_scalar_add(ap, ap, 1e-6)
            e.drain()
            e.reciprocal(ap, ap)
            e.drain()
            if alpha_col is not None:
                e.tensor_single_scalar(ap, ap, mk.vec(alpha_col, pn=16), OP.mult)
            e.drain().then_inc(S["ve"], 1)
        mk.sec("vector", fve2)
        mk.n["ve"] += 1


def apply_fac(mk, buf, nch, tts=(0, 1), ncols=NTOK):
    S, t = mk.S, mk.t
    for tti in range(len(tts)):
        veb, mmb = mk.n["ve"], mk.n["mm"]
        def fte(e, tti=tti, veb=veb):
            e.wait_ge(S["ve"], veb)
            for ch in range(nch):
                if ch >= 2:
                    e.wait_ge(S["ve"], veb + ch - 1)
                bank = t["ps%d" % (2 + ch % 2)]
                e.matmul(bank[:, 0:TTW], mk.hsel(ch),
                         t["sm0"][0:16, tti * 512:(tti + 1) * 512],
                         start=True, stop=True)
                e.drain().then_inc(S["mm"], 1)
        mk.sec("tensor", fte)
        mk.n["mm"] += nch

        def fve(e, tti=tti, mmb=mmb):
            for ch in range(nch):
                e.wait_ge(S["mm"], mmb + ch + 1)
                bank = t["ps%d" % (2 + ch % 2)]
                ap = mk.A(buf + ch * ncols + tti * TTW, TTW)
                e.tensor_mul(ap, ap, bank[:, 0:TTW])
                e.drain().then_inc(S["ve"], 1)
        mk.sec("vector", fve)
        mk.n["ve"] += nch


def p4_l1_ffn(mk):
    S, t = mk.S, mk.t
    sfm = 0
    stm = 1024
    yb = 2176
    srep = 18560
    sel = 19584
    cB = 20608

    def ev(e, i, j, tt, ps):
        e.activation(t["arena"][0:16, sfm + tt * TTW:sfm + (tt + 1) * TTW],
                     ps[0:16, 0:TTW], AF.Identity, bias=mk.vec(VRB1, pn=16))
    mk.proj(t["rw1"].ap(), DM, [(0, 16)], (0, 1),
            lambda kc, tt: mk.xap(kc, tt * TTW), ev, eng="scalar")
    mk.barrier()

    mmb, veb = mk.n["mm"], mk.n["ve"]
    def fte(e):
        for g in range(8):
            if g >= 2:
                e.wait_ge(S["ve"], veb + g - 1)
            bank = t["ps%d" % (2 + g % 2)]
            e.transpose(bank[0:128, 0:16],
                        t["arena"][0:16, sfm + g * 128:sfm + (g + 1) * 128],
                        mk.cst(CST_ID, 16, pn=16))
            e.drain().then_inc(S["mm"], 1)
    mk.sec("tensor", fte)
    mk.n["mm"] += 8

    def fve(e):
        for g in range(8):
            e.wait_ge(S["mm"], mmb + g + 1)
            bank = t["ps%d" % (2 + g % 2)]
            e.tensor_copy(mk.A(stm + g * 16, 16), bank[:, 0:16])
            e.drain().then_inc(S["ve"], 1)
    mk.sec("vector", fve)
    mk.n["ve"] += 8

    for g in range(8):
        veb_g = mk.n["ve"]
        scb_pre = mk.n["sc"]
        def fva(e, g=g, scb_pre=scb_pre):
            if g >= 1:
                e.wait_ge(S["sc"], scb_pre)
            e.drain()
            ap = mk.A(stm + g * 16, 16)
            e.reduce_max(t["sm2"][:, 0:1], ap, axis=AXX)
            e.drain()
            e.tensor_scalar_mul(t["sm2"][:, 1:2], t["sm2"][:, 0:1], -1.0)
            e.drain().then_inc(S["ve"], 1)
        mk.sec("vector", fva)
        mk.n["ve"] += 1
        veb_g2 = mk.n["ve"]
        scb_g = mk.n["sc"]
        def fsa(e, g=g, veb_g2=veb_g2):
            e.wait_ge(S["ve"], veb_g2)
            ap = mk.A(stm + g * 16, 16)
            e.activation(ap, ap, AF.Exp, bias=t["sm2"][:, 1:2],
                         accum_out=t["sm2"][:, 2:3])
            e.drain().then_inc(S["sc"], 1)
        mk.sec("scalar", fsa)
        mk.n["sc"] += 1
        def fvb(e, g=g, scb_g=scb_g):
            e.wait_ge(S["sc"], scb_g + 1)
            e.drain()
            ap = mk.A(stm + g * 16, 16)
            e.reciprocal(t["sm2"][:, 3:4], t["sm2"][:, 2:3])
            e.drain()
            e.tensor_single_scalar(ap, ap, t["sm2"][:, 3:4], OP.mult)
            e.drain().then_inc(S["ve"], 1)
        mk.sec("vector", fvb)
        mk.n["ve"] += 1

    sfm2 = stm + 128
    veb4 = mk.n["ve"]
    mmb4 = mk.n["mm"]
    def fteb(e):
        e.wait_ge(S["ve"], veb4)
        for g in range(8):
            if g >= 2:
                e.wait_ge(S["ve"], veb4 + g - 1)
            bank = t["ps%d" % (6 + g % 2)]
            e.transpose(bank[0:16, 0:128], mk.A(stm + g * 16, 16),
                        mk.cst(CST_ID, 128))
            e.drain().then_inc(S["mm"], 1)
    mk.sec("tensor", fteb)
    mk.n["mm"] += 8

    def fveb(e):
        for g in range(8):
            e.wait_ge(S["mm"], mmb4 + g + 1)
            bank = t["ps%d" % (6 + g % 2)]
            e.tensor_copy(mk.A(sfm2 + g * 128, 128, pn=16), bank[0:16, 0:128])
            e.drain().then_inc(S["ve"], 1)
    mk.sec("vector", fveb)
    mk.n["ve"] += 8

    veb5b = mk.n["ve"]
    def fgs(g_):
        g_.wait_ge(S["ve"], veb5b)
        g_.dma_start(out=t["agsi"].ap(), in_=mk.A(sfm2, NTOK, pn=16)).then_inc(S["dg"], 16)
    mk.sec("gpsimd", fgs)
    mk.n["dg"] += 16
    dgt = mk.n["dg"]
    def fca(g_):
        g_.wait_ge(S["dg"], dgt)
        g_.collective_compute("AllGather", OP.bypass,
                              replica_groups=[list(range(NCORE))],
                              ins=[t["agsi"].ap().opt()],
                              outs=[t["agso"].ap().opt()]).then_inc(S["cc"], 1)
    mk.sec("gpsimd", fca)
    mk.n["cc"] += 1
    ccs = mk.n["cc"]

    def evy(e, i, j, tt, ps):
        e.activation(mk.A(yb + j * NTOK + tt * TTW, TTW), ps, AF.Identity,
                     bias=mk.vec(VEB + j))
    mk.proj(t["ew"].ap(), DM, MTS16, (0, 1),
            lambda kc, tt: mk.xap(kc, tt * TTW), evy, eng="scalar")
    mk.barrier()

    dkb = mk.n["dx"]
    def fsr(e):
        e.wait_ge(S["cc"], ccs)
        e.dma_start(out=mk.A(srep, NTOK), in_=t["agso"].ap()).then_inc(S["dx"], 16)
    mk.sec("sync", fsr)
    mk.n["dx"] += 16

    def fvi(e):
        e.wait_ge(S["dx"], dkb + 16)
        e.memset(t["sm2"][0:16, 20:21], 0.0)
        e.memset(t["sm2"][0:16, 21:22], 1.0)
        e.memset(t["sm2"][0:16, 22:23], 0.5)
        e.drain().then_inc(S["ve"], 1)
    mk.sec("vector", fvi)
    mk.n["ve"] += 1

    for it in range(38):
        veb_i = mk.n["ve"]
        def fti(e, veb_i=veb_i):
            e.wait_ge(S["ve"], veb_i)
            e.matmul(t["ps7"][:, 0:1], mk.cst(CST_REPR, 128, pn=16),
                     t["sm2"][0:16, 22:23], start=True, stop=True)
            e.drain().then_inc(S["mm"], 1)
        mk.sec("tensor", fti)
        mk.n["mm"] += 1
        mmb_i = mk.n["mm"]
        def fvj(e, mmb_i=mmb_i):
            e.wait_ge(S["mm"], mmb_i)
            e.drain()
            e.tensor_copy(t["sm2"][:, 24:25], t["ps7"][:, 0:1])
            e.drain()
            e.tensor_single_scalar(mk.A(sel, NTOK), mk.A(srep, NTOK),
                                   t["sm2"][:, 24:25], OP.subtract)
            e.drain()
            e.tensor_scalar_max(mk.A(sel, NTOK), mk.A(sel, NTOK), 0.0)
            e.drain()
            e.tensor_scalar_mul(mk.A(sel, NTOK), mk.A(sel, NTOK), 1e30)
            e.drain()
            e.tensor_scalar(mk.A(sel, NTOK), mk.A(sel, NTOK), 1.0, None,
                            OP.min, OP.add, accum_out=t["sm2"][:, 25:26])
            e.drain().then_inc(S["ve"], 1)
        mk.sec("vector", fvj)
        mk.n["ve"] += 1
        veb_j = mk.n["ve"]
        def ftj(e, veb_j=veb_j):
            e.wait_ge(S["ve"], veb_j)
            e.matmul(t["ps6"][0:16, 0:1], mk.cst(CST_FOLDF, 16),
                     t["sm2"][:, 25:26], start=True, stop=True)
            e.drain().then_inc(S["mm"], 1)
        mk.sec("tensor", ftj)
        mk.n["mm"] += 1
        mmb_j = mk.n["mm"]
        def fvk(e, mmb_j=mmb_j):
            e.wait_ge(S["mm"], mmb_j)
            sm = t["sm2"]
            e.drain()
            e.tensor_scalar_sub(sm[0:16, 26:27], t["ps6"][0:16, 0:1], 511.0)
            e.drain()
            e.tensor_scalar_max(sm[0:16, 26:27], sm[0:16, 26:27], 0.0)
            e.drain()
            e.tensor_scalar_min(sm[0:16, 26:27], sm[0:16, 26:27], 1.0)
            e.drain()
            e.tensor_sub(sm[0:16, 27:28], sm[0:16, 22:23], sm[0:16, 20:21])
            e.drain()
            e.tensor_mul(sm[0:16, 27:28], sm[0:16, 27:28], sm[0:16, 26:27])
            e.drain()
            e.tensor_add(sm[0:16, 20:21], sm[0:16, 20:21], sm[0:16, 27:28])
            e.drain()
            e.tensor_sub(sm[0:16, 28:29], sm[0:16, 21:22], sm[0:16, 22:23])
            e.drain()
            e.tensor_mul(sm[0:16, 28:29], sm[0:16, 28:29], sm[0:16, 26:27])
            e.drain()
            e.tensor_add(sm[0:16, 21:22], sm[0:16, 22:23], sm[0:16, 28:29])
            e.drain()
            e.tensor_add(sm[0:16, 22:23], sm[0:16, 20:21], sm[0:16, 21:22])
            e.drain()
            e.tensor_scalar_mul(sm[0:16, 22:23], sm[0:16, 22:23], 0.5)
            e.drain().then_inc(S["ve"], 1)
        mk.sec("vector", fvk)
        mk.n["ve"] += 1

    veb6 = mk.n["ve"]
    def ftf(e):
        e.wait_ge(S["ve"], veb6)
        e.matmul(t["ps7"][:, 0:1], mk.cst(CST_REPR, 128, pn=16),
                 t["sm2"][0:16, 20:21], start=True, stop=True)
        e.drain().then_inc(S["mm"], 1)
    mk.sec("tensor", ftf)
    mk.n["mm"] += 1
    mmb_f = mk.n["mm"]
    def fvf(e):
        e.wait_ge(S["mm"], mmb_f)
        e.drain()
        e.tensor_copy(t["sm2"][:, 24:25], t["ps7"][:, 0:1])
        e.drain()
        e.tensor_single_scalar(mk.A(sel, NTOK), mk.A(srep, NTOK),
                               t["sm2"][:, 24:25], OP.subtract)
        e.drain()
        e.tensor_scalar_max(mk.A(sel, NTOK), mk.A(sel, NTOK), 0.0)
        e.drain()
        e.tensor_scalar_mul(mk.A(sel, NTOK), mk.A(sel, NTOK), 1e30)
        e.drain()
        e.tensor_scalar_min(mk.A(sel, NTOK), mk.A(sel, NTOK), 1.0)
        e.drain()
        e.tensor_mul(mk.A(sel, NTOK), mk.A(sel, NTOK), mk.A(srep, NTOK))
        e.drain().then_inc(S["ve"], 1)
    mk.sec("vector", fvf)
    mk.n["ve"] += 1

    veb7 = mk.n["ve"]
    def ftg(e):
        e.wait_ge(S["ve"], veb7)
        for tt in range(2):
            e.matmul(t["ps%d" % (6 + tt)][0:8, 0:TTW], mk.cst(CST_FOLDH, 8),
                     mk.A(sel + tt * TTW, TTW), start=True, stop=True)
            e.drain().then_inc(S["mm"], 1)
    mk.sec("tensor", ftg)
    mk.n["mm"] += 2
    mmb_g = mk.n["mm"]
    def fvg(e):
        for tt in range(2):
            e.wait_ge(S["mm"], mmb_g - 2 + tt + 1)
            e.tensor_copy(t["sm0"][0:8, tt * 512:(tt + 1) * 512],
                          t["ps%d" % (6 + tt)][0:8, 0:TTW])
        e.drain().then_inc(S["ve"], 1)
    mk.sec("vector", fvg)
    mk.n["ve"] += 1
    veb8 = mk.n["ve"]
    def fth(e):
        e.wait_ge(S["ve"], veb8)
        for tt in range(2):
            e.matmul(t["ps%d" % (2 + tt)][:, 0:TTW], mk.cst(CST_SELC, 128, pn=8),
                     t["sm0"][0:8, tt * 512:(tt + 1) * 512], start=True, stop=True)
            e.drain().then_inc(S["mm"], 1)
    mk.sec("tensor", fth)
    mk.n["mm"] += 2
    mmb_h = mk.n["mm"]
    def fvh(e):
        for tt in range(2):
            e.wait_ge(S["mm"], mmb_h - 2 + tt + 1)
            e.tensor_copy(mk.A(cB + tt * TTW, TTW), t["ps%d" % (2 + tt)][:, 0:TTW])
        e.drain().then_inc(S["ve"], 1)
    mk.sec("vector", fvh)
    mk.n["ve"] += 1

    veb9 = mk.n["ve"]
    def fvz(e):
        e.wait_ge(S["ve"], veb9)
        for kc in range(NCH):
            for tt in range(2):
                dst = mk.xap(kc, tt * TTW)
                tmp = mk.A(sel + tt * TTW, TTW)
                e.drain()
                e.tensor_mul(tmp, mk.A(yb + kc * NTOK + tt * TTW, TTW),
                             mk.A(cB + tt * TTW, TTW))
                e.drain()
                e.tensor_add(dst, dst, tmp)
        e.drain().then_inc(S["ve"], 1)
    mk.sec("vector", fvz)
    mk.n["ve"] += 1
    mk.barrier()
    ln_full(mk, sq_off=sel)



def _host_constants():
    c = {}
    c["ident"] = np.eye(128, dtype=np.float32)
    rs = np.zeros((128, NH * NH), np.float32)
    for h in range(NH):
        rs[:, h * NH + h] = 1.0
    c["rowsel"] = rs
    hs = np.zeros((NH, NH * 128), np.float32)
    for h in range(NH):
        hs[h, h * 128:(h + 1) * 128] = 1.0
    c["hsel"] = hs
    ph = np.zeros((128, 128), np.float32)
    for blk in (0, 64):
        for m in range(32):
            ph[blk + m + 32, blk + m] = -1.0
        for m in range(32, 64):
            ph[blk + m - 32, blk + m] = 1.0
    c["permh"] = ph
    pq = np.zeros((128, 128), np.float32)
    for m in range(64, 96):
        pq[m + 32, m] = -1.0
    for m in range(96, 128):
        pq[m - 32, m] = 1.0
    c["permq"] = pq
    pi = np.zeros((128, 128), np.float32)
    for m in range(0, 128, 2):
        pi[m + 1, m] = -1.0
        pi[m, m + 1] = 1.0
    c["permi"] = pi
    ff = np.zeros((128, NH), np.float32)
    fh = np.zeros((128, NCORE), np.float32)
    for p in range(128):
        ff[p, p % 16] = 1.0
        fh[p, p // 16] = 1.0
    c["foldF"] = ff
    c["foldH"] = fh
    rp = np.zeros((NH, 128), np.float32)
    for m in range(128):
        rp[m % 16, m] = 1.0
    c["repR"] = rp
    es = np.zeros((7, 7 * 128), np.float32)
    for e in range(7):
        es[e, e * 128:(e + 1) * 128] = 1.0
    c["esel"] = es
    return c


def _rope_tables(hf):
    tl = np.arange(NTOK)
    tabs = np.array([QSTARTS[hf][i // 512] + (i % 512) for i in tl], np.float64)
    out = {}
    invf0 = 1.0 / (10000.0 ** (np.arange(0, 64, 2, dtype=np.float64) / 64.0))
    ang0 = np.zeros((128, NTOK), np.float64)
    for r in range(128):
        ang0[r] = tabs * invf0[(r % 64) % 32]
    out["l0cos"] = np.cos(ang0).astype(np.float32)
    out["l0sin"] = np.sin(ang0).astype(np.float32)
    cq = np.ones((128, NTOK), np.float64)
    sq = np.zeros((128, NTOK), np.float64)
    for r in range(64, 128):
        dd = r - 64
        cq[r] = np.cos(tabs * invf0[dd % 32])
        sq[r] = np.sin(tabs * invf0[dd % 32])
    out["l0cosq"] = cq.astype(np.float32)
    out["l0sinq"] = sq.astype(np.float32)
    invf1 = 1.0 / (10000.0 ** (np.arange(0, 128, 2, dtype=np.float64) / 128.0))
    ang1 = np.zeros((128, NTOK), np.float64)
    for r in range(128):
        ang1[r] = tabs * invf1[r // 2]
    out["l1cos"] = np.cos(ang1).astype(np.float32)
    out["l1sin"] = np.sin(ang1).astype(np.float32)
    return out


def _masks(hf):
    m = np.zeros((6, 128, 4, TTW), np.float32)
    for s, (qc, kb) in enumerate(SLOTS):
        q0 = QSTARTS[hf][qc]
        for kc in range(4):
            k0 = kb * 512 + kc * 128
            kk = k0 + np.arange(128)[:, None]
            qq = q0 + np.arange(TTW)[None, :]
            m[s, :, kc, :] = (qq >= kk).astype(np.float32)
    return m.reshape(6 * 128, 4 * TTW)


def _vec_pd(v, nch):
    return np.ascontiguousarray(np.asarray(v).reshape(nch, 128).T.astype(np.float32))


def kernel(**inputs):
    ii = {k: np.asarray(v) for k, v in inputs.items()}
    x = ii["L0_attn_hidden_states"]

    shared = {}
    shared["wq0"] = np.ascontiguousarray(ii["L0_attn_Wq"].T)
    shared["wk0"] = np.ascontiguousarray(ii["L0_attn_Wk"].T)
    shared["wv0"] = np.ascontiguousarray(ii["L0_attn_Wv"].T)
    shared["wo0"] = np.ascontiguousarray(ii["L0_attn_Wo"].T)
    shared["kvdw"] = ii["L0_ffn_kv_d_w"]
    shared["qdw"] = ii["L0_ffn_q_d_w"]
    qu = ii["L0_ffn_q_u_w"]; rq = ii["L0_ffn_rope_q_w"]
    wqasm = np.zeros((512, DM), np.float32)
    for h in range(NH):
        wqasm[:, h * 128:h * 128 + 64] = qu[:, h * 64:(h + 1) * 64]
        wqasm[:, h * 128 + 64:h * 128 + 128] = rq[:, h * 64:(h + 1) * 64]
    shared["wqasm"] = wqasm
    shared["kuw"] = ii["L0_ffn_k_u_w"]
    shared["ropekw"] = ii["L0_ffn_rope_k_w"]
    shared["vuw"] = ii["L0_ffn_v_u_w"]
    shared["ow"] = ii["L0_ffn_o_w"]
    shared["shg"] = ii["L0_ffn_sh_gate"]
    shared["shu"] = ii["L0_ffn_sh_up"]
    shared["shd"] = ii["L0_ffn_sh_down"]
    shared["rg"] = ii["L0_ffn_r_gate"].reshape(7 * DM, 1024)
    shared["ru"] = ii["L0_ffn_r_up"].reshape(7 * DM, 1024)
    shared["rd"] = ii["L0_ffn_r_down"].reshape(7 * 1024, DM)
    shared["rw0"] = ii["L0_ffn_router_w"]
    shared["wq1"] = ii["L1_attn_Wq"]
    shared["wk1"] = ii["L1_attn_Wk"]
    shared["wv1"] = ii["L1_attn_Wv"]
    shared["wo1"] = ii["L1_attn_Wo"]
    shared["rw1"] = ii["L1_ffn_router_w"]
    shared["ew"] = ii["L1_ffn_expert_w"]
    vecs = np.zeros((128, VNVEC), np.float32)
    vecs[:, VGAM] = ii["L0_attn_gamma"]
    vecs[:, VBET] = ii["L0_attn_beta"]
    vecs[:, VL0W1:VL0W1 + NCH] = _vec_pd(ii["L0_ffn_ln1_w"], NCH)
    vecs[:, VL0W2:VL0W2 + NCH] = _vec_pd(ii["L0_ffn_ln2_w"], NCH)
    vecs[:, VBQ1:VBQ1 + NCH] = _vec_pd(ii["L1_attn_bq"], NCH)
    vecs[:, VBK1:VBK1 + 4] = _vec_pd(ii["L1_attn_bk"], 4)
    vecs[:, VBV1:VBV1 + 4] = _vec_pd(ii["L1_attn_bv"], 4)
    vecs[:, VBO1:VBO1 + NCH] = _vec_pd(ii["L1_attn_bo"], NCH)
    vecs[:, VEB:VEB + NCH] = _vec_pd(ii["L1_ffn_expert_b"], NCH)
    vecs[0:NH, VALP] = ii["L1_attn_alpha"]
    vecs[0:7, VRB0] = ii["L0_ffn_router_b"]
    vecs[0:NH, VRB1] = ii["L1_ffn_router_b"]
    shared["vecs"] = vecs
    shared.update(_host_constants())
    shared = {k: np.ascontiguousarray(v, dtype=np.float32) for k, v in shared.items()}

    in_maps = []
    for c in range(NCORE):
        b, hf = c // 2, c % 2
        im = dict(shared)
        im["xs"] = np.ascontiguousarray(
            np.concatenate([x[b, s:s + 512] for s in QSTARTS[hf]], axis=0),
            dtype=np.float32)
        im.update({k: np.ascontiguousarray(v) for k, v in _rope_tables(hf).items()})
        im["masks"] = _masks(hf)
        sc = np.zeros((NCORE, 128), np.float32)
        sc[c, :] = 1.0
        im["selc"] = sc
        in_maps.append(im)

    nc = build()
    res = run_bass_kernel_spmd(nc, in_maps, core_ids=list(range(NCORE)))
    out = np.zeros((4, T, DM), np.float32)
    for c in range(NCORE):
        b, hf = c // 2, c % 2
        y = np.asarray(res.results[c]["yout"])
        for j, s in enumerate(QSTARTS[hf]):
            out[b, s:s + 512] = y[j * 512:(j + 1) * 512]
    return out


# version 03
# speedup vs baseline: 1.0511x; 1.0444x over previous
import numpy as np
from contextlib import ExitStack
import concourse.bass as bass
import concourse.mybir as mybir
from concourse.bass_utils import run_bass_kernel_spmd

F32 = mybir.dt.float32
AF = mybir.ActivationFunctionType
OP = mybir.AluOpType
AXX = mybir.AxisListType.X

NCORE = 8
T = 2048
DM = 2048
NCH = 16
NTOK = 1024
TTW = 512
QSTARTS = [[0, 1536], [512, 1024]]
NKB = [2, 4]
KBSRC = [(0, 0), (1, 0), (1, 512), (0, 512)]
SLOTS = [(0, 0), (0, 1), (1, 0), (1, 1), (1, 2), (1, 3)]
NH = 16
ARENA = 26624

DBG = [None]

CST_ID = 0
CST_PERMH = 128
CST_PERMQ = 256
CST_PERMI = 384
CST_ROWSEL = 512
CST_ONES = 768
CST_FOLDF = 776
CST_FOLDH = 792
CST_SELC = 800
CST_ESEL = 928
CST_VECS = 1824
CST_REPR = 1920
CST_END = 2048

VGAM, VBET = 0, 1
VL0W1, VL0W2 = 2, 18
VBQ1, VBK1, VBV1, VBO1 = 34, 50, 54, 58
VEB = 74
VALP, VRB0, VRB1 = 90, 91, 92
VNVEC = 96

ATT_QTAB = 0
ATT_QB = 2048
ATT_OB = 10240
ATT_EB = 18432
ATT_MB = 22528
ATT_KV = 24576

MTS16 = [(i * 128, 128) for i in range(16)]
MTS8 = [(i * 128, 128) for i in range(8)]
MTS4 = [(i * 128, 128) for i in range(4)]


def rr(ap, s, **kw):
    return ap.rearrange(s, **kw)


class _EPState:
    __slots__ = ("e", "last", "pend", "ninc")

    def __init__(self):
        self.e = None
        self.last = None
        self.pend = False
        self.ninc = 0


class _InsWrap:
    def __init__(self, ins, st):
        self._ins = ins
        self._st = st

    def then_inc(self, sem, n):
        self._st.ninc += 1
        return self._ins.then_inc(sem, n)

    def __getattr__(self, name):
        return getattr(self._ins, name)


class _DrainShim:
    def __init__(self, st):
        self._st = st

    def then_inc(self, sem, n):
        st = self._st
        if st.pend:
            st.pend = False
            if st.last is not None and st.ninc == 0:
                st.ninc = 1
                return st.last.then_inc(sem, n)
            ins = st.e.drain()
            st.last = ins
            st.ninc = 1
            return ins.then_inc(sem, n)
        return st.e.drain().then_inc(sem, n)


class EW:
    def __init__(self, st):
        object.__setattr__(self, "_st", st)

    def _flush(self):
        st = self._st
        if st.pend:
            st.pend = False
            st.e.drain()

    def wait_ge(self, *a, **k):
        self._flush()
        self._st.last = None
        return self._st.e.wait_ge(*a, **k)

    def drain(self):
        self._flush()
        self._st.pend = True
        return _DrainShim(self._st)

    def __getattr__(self, name):
        st = self._st
        f = getattr(st.e, name)
        if not callable(f):
            return f

        def g(*a, **k):
            self._flush()
            ins = f(*a, **k)
            if ins is not None and hasattr(ins, "then_inc"):
                st.last = ins
                st.ninc = 0
                return _InsWrap(ins, st)
            return ins
        return g


class MK:
    def __init__(self, nc, block, tensors, sems):
        self.nc = nc
        self.block = block
        self.t = tensors
        self.S = sems
        self.n = {k: 0 for k in sems}
        self.nbar = 0
        self.wb_last = [0, 0]
        self.ps01_last = [None, None]
        self._ep = {}

    def sec(self, eng, fn):
        st = self._ep.setdefault(eng, _EPState())

        def wrapped(e, st=st, fn=fn):
            st.e = e
            fn(EW(st))
        getattr(self.block, eng)(wrapped)

    def barrier(self):
        self.nbar += 1
        tgt = self.nbar
        engs = ("tensor", "vector", "scalar", "gpsimd", "sync")
        sync_sems = ("dwa", "dwb", "dka", "dkb", "dva", "dvb", "dm", "dx", "dk")
        for eng in engs:
            def f(e, eng=eng):
                if eng == "sync":
                    for sem in sync_sems:
                        if self.n[sem]:
                            e.wait_ge(self.S[sem], self.n[sem])
                    e.nop().then_inc(self.S["bar_sync"], 1)
                elif eng == "gpsimd":
                    if self.n["dg"]:
                        e.wait_ge(self.S["dg"], self.n["dg"])
                    e.drain().then_inc(self.S["bar_gpsimd"], 1)
                else:
                    e.drain().then_inc(self.S["bar_" + eng], 1)
            self.sec(eng, f)
        for eng in engs:
            def f2(e, tgt=tgt):
                for other in engs:
                    e.wait_ge(self.S["bar_" + other], tgt)
            self.sec(eng, f2)

    def A(self, off, cols, p0=0, pn=128):
        return self.t["arena"][p0:p0 + pn, off:off + cols]

    def xap(self, kc, c0, w=TTW):
        return self.t["x"][:, kc * NTOK + c0: kc * NTOK + c0 + w]

    def cst(self, off, cols, p0=0, pn=128):
        return self.t["cst"][p0:p0 + pn, off:off + cols]

    def vec(self, col, pn=128):
        return self.t["cst"][0:pn, CST_VECS + col:CST_VECS + col + 1]

    def hsel(self, h):
        return self.t["sm1"][0:16, h * 128:(h + 1) * 128]

    def onesrow(self):
        return self.t["sm1"][0:1, 0:128]

    def rowsel(self, h):
        return self.cst(CST_ROWSEL + h * NH, NH)

    def wpanel_dst(self, slot, nkc, w):
        return self.t["wb"][:, slot * 2048: slot * 2048 + nkc * w]

    def wtile(self, slot, kc, w):
        return self.t["wb"][:, slot * 2048 + kc * w: slot * 2048 + (kc + 1) * w]

    def proj(self, W_ap, Kdim, mts, tts, rhs_fn, evac, eng="vector",
             dma_out=None, dma_per_tile=0):
        nkc = Kdim // 128
        ntl = len(mts) * len(tts)
        mmb = self.n["mm"]
        lane = ("dwa", "dwb")
        lb = [self.n["dwa"], self.n["dwb"]]
        evs = "ve" if eng == "vector" else "sc"
        evb = self.n[evs]
        dgb = self.n["dg"]
        self._gp_snapshot = self.n["gp"]
        S = self.S
        Wr = rr(W_ap, "(kc p) m -> p kc m", p=128)

        def fsync(e):
            for j, (c0, w) in enumerate(mts):
                if j >= 2:
                    e.wait_ge(S["mm"], mmb + (j - 1) * len(tts))
                else:
                    if self.wb_last[j % 2]:
                        e.wait_ge(S["mm"], self.wb_last[j % 2])
                e.dma_start(out=self.wpanel_dst(j % 2, nkc, w),
                            in_=Wr[:, :, c0:c0 + w]).then_inc(S[lane[j % 2]], 16)
        self.sec("sync", fsync)
        self.n["dwa"] += 16 * ((len(mts) + 1) // 2)
        self.n["dwb"] += 16 * (len(mts) // 2)
        for sslot in range(2):
            js = [j for j in range(len(mts)) if j % 2 == sslot]
            if js:
                self.wb_last[sslot] = mmb + (js[-1] + 1) * len(tts)

        def ftensor(te):
            i = 0
            for j, (c0, w) in enumerate(mts):
                te.wait_ge(S[lane[j % 2]], lb[j % 2] + 16 * (j // 2 + 1))
                for tt in tts:
                    if i >= 2:
                        te.wait_ge(S[evs], evb + i - 1)
                    elif self.ps01_last[i % 2] is not None:
                        te.wait_ge(S[self.ps01_last[i % 2][0]],
                                   self.ps01_last[i % 2][1])
                    ps = self.t["ps%d" % (i % 2)]
                    ins = None
                    for kc in range(nkc):
                        ins = te.matmul(ps[0:w, 0:TTW],
                                        self.wtile(j % 2, kc, w),
                                        rhs_fn(kc, tt),
                                        start=(kc == 0), stop=(kc == nkc - 1))
                    ins.then_inc(S["mm"], 1)
                    i += 1
        self.sec("tensor", ftensor)
        self.n["mm"] += ntl

        def fev(e):
            i = 0
            for j, (c0, w) in enumerate(mts):
                for tt in tts:
                    e.wait_ge(S["mm"], mmb + i + 1)
                    if dma_out is not None and i >= 2:
                        e.wait_ge(S["gp"], self._gp_snapshot + i - 1)
                    evac(e, i, j, tt, self.t["ps%d" % (i % 2)][0:w, 0:TTW])
                    e.drain().then_inc(S[evs], 1)
                    i += 1
        self.sec(eng, fev)
        self.n[evs] += ntl

        for par in range(2):
            iis = [ii2 for ii2 in range(ntl) if ii2 % 2 == par]
            if iis:
                self.ps01_last[par] = (evs, evb + iis[-1] + 1)
        if dma_out is not None:
            gpb = self.n["gp"]
            def fg(g):
                i = 0
                for j, (c0, w) in enumerate(mts):
                    for tt in tts:
                        g.wait_ge(S[evs], evb + i + 1)
                        dma_out(g, i, j, tt)
                        g.wait_ge(S["dg"], dgb + 16 * dma_per_tile * (i + 1))
                        g.nop().then_inc(S["gp"], 1)
                        i += 1
            self.sec("gpsimd", fg)
            self.n["dg"] += 16 * dma_per_tile * ntl
            self.n["gp"] += ntl
        return ntl



def build():
    nc = bass.Bass("TRN2", target_bir_lowering=False, debug=False)
    d = {}

    def din(name, shape):
        d[name] = nc.dram_tensor(name, list(shape), F32, kind="ExternalInput")

    din("xs", (NTOK, DM))
    din("wq0", (DM, DM)); din("wk0", (DM, DM)); din("wv0", (DM, DM)); din("wo0", (DM, DM))
    din("kvdw", (DM, 512)); din("qdw", (DM, 512))
    din("wqasm", (512, DM))
    din("kuw", (512, 1024)); din("ropekw", (DM, 1024)); din("vuw", (512, DM))
    din("ow", (DM, DM))
    din("shg", (DM, 1024)); din("shu", (DM, 1024)); din("shd", (1024, DM))
    din("rg", (7 * DM, 1024)); din("ru", (7 * DM, 1024)); din("rd", (7 * 1024, DM))
    din("rw0", (DM, 7))
    din("wq1", (DM, DM)); din("wk1", (DM, 512)); din("wv1", (DM, 512)); din("wo1", (DM, DM))
    din("rw1", (DM, NH)); din("ew", (DM, DM))
    din("ident", (128, 128))
    din("rowsel", (128, NH * NH))
    din("hsel", (NH, NH * 128))
    din("permh", (128, 128)); din("permq", (128, 128)); din("permi", (128, 128))
    din("l0cos", (128, NTOK)); din("l0sin", (128, NTOK))
    din("l0cosq", (128, NTOK)); din("l0sinq", (128, NTOK))
    din("l1cos", (128, NTOK)); din("l1sin", (128, NTOK))
    din("masks", (6 * 128, 4 * TTW))
    din("foldF", (128, NH)); din("foldH", (128, NCORE)); din("repR", (NH, 128))
    din("selc", (NCORE, 128))
    din("esel", (7, 7 * 128))
    din("vecs", (128, VNVEC))

    d["yout"] = nc.dram_tensor("yout", [NTOK, DM], F32, kind="ExternalOutput")
    d["stash"] = nc.dram_tensor("stash", [128, NCH * NTOK], F32)
    d["qdsp"] = nc.dram_tensor("qdsp", [128, 4 * NTOK], F32)
    for tag, kd in (("a1", DM), ("a2", DM), ("a3", 512)):
        d["agki_" + tag] = nc.dram_tensor("agki_" + tag, [kd, NTOK], F32)
        d["agko_" + tag] = nc.dram_tensor("agko_" + tag, [2 * kd, NTOK], F32)
        d["agvi_" + tag] = nc.dram_tensor("agvi_" + tag, [NTOK, kd], F32)
        d["agvo_" + tag] = nc.dram_tensor("agvo_" + tag, [2 * NTOK, kd], F32)
    d["agsi"] = nc.dram_tensor("agsi", [NH, NTOK], F32)
    d["agso"] = nc.dram_tensor("agso", [NCORE * NH, NTOK], F32)

    with ExitStack() as st:
        tn = {}
        def sb(name, cols):
            tn[name] = st.enter_context(nc.sbuf_tensor(name, [128, cols], F32))
        sb("x", NCH * NTOK)
        sb("arena", ARENA)
        sb("wb", 2 * 2048)
        sb("cst", CST_END)
        sb("sm0", 1536)
        sb("sm1", 2048)
        sb("sm2", 64)
        for i in range(8):
            tn["ps%d" % i] = st.enter_context(
                nc.psum_tensor("ps%d" % i, [128, 512], F32))
        sems = {}
        for s in ("dwa", "dwb", "dka", "dkb", "dva", "dvb", "dm", "dx",
                  "dk", "dg", "gp", "cc", "mm", "sc", "ve",
                  "bar_tensor", "bar_vector", "bar_scalar", "bar_gpsimd",
                  "bar_sync"):
            sems[s] = st.enter_context(nc.semaphore(s))
        with nc.Block() as block:
            tensors = dict(tn)
            tensors.update(d)
            mk = MK(nc, block, tensors, sems)
            emit_all(mk)
    return nc


def emit_all(mk):
    p0_load(mk)
    if DBG[0] == "p0":
        return emit_output(mk)
    p1_l0_attn(mk)
    if DBG[0] == "p1":
        return emit_output(mk)
    p2_l0_ffn(mk)
    if DBG[0] == "p2":
        return emit_output(mk)
    p3_l1_attn(mk)
    if DBG[0] == "p3":
        return emit_output(mk)
    p4_l1_ffn(mk)
    emit_output(mk)


def p0_load(mk):
    S, t = mk.S, mk.t
    dkb = mk.n["dk"]

    def fsync(e):
        for dst_off, src, pn, cols in (
                (CST_ID, "ident", 128, 128), (CST_PERMH, "permh", 128, 128),
                (CST_PERMQ, "permq", 128, 128), (CST_PERMI, "permi", 128, 128),
                (CST_ROWSEL, "rowsel", 128, 256), (CST_FOLDF, "foldF", 128, 16),
                (CST_FOLDH, "foldH", 128, 8), (CST_SELC, "selc", 8, 128),
                (CST_ESEL, "esel", 7, 896), (CST_VECS, "vecs", 128, VNVEC),
                (CST_REPR, "repR", 16, 128)):
            e.dma_start(out=t["cst"][0:pn, dst_off:dst_off + cols],
                        in_=t[src].ap()).then_inc(S["dk"], 16)
        e.dma_start(out=t["sm1"][0:16, 0:2048], in_=t["hsel"].ap()).then_inc(S["dk"], 16)
    mk.sec("sync", fsync)
    mk.n["dk"] += 16 * 12

    def fvec(e):
        e.wait_ge(S["dk"], dkb + 16 * 12)
        e.memset(t["cst"][:, CST_ONES:CST_ONES + 8], 1.0)
        e.drain().then_inc(S["ve"], 1)
    mk.sec("vector", fvec)
    mk.n["ve"] += 1

    stg = 0
    dkb2 = mk.n["dx"]
    mmb = mk.n["mm"]
    veb = mk.n["ve"]
    xr = rr(t["x"].ap(), "p (c t) -> p c t", c=NCH)

    def fsync2(e):
        for g in range(8):
            if g >= 1:
                e.wait_ge(S["mm"], mmb + g * NCH)
            e.dma_start(out=mk.A(stg, 2048),
                        in_=t["xs"].ap()[g * 128:(g + 1) * 128, :]).then_inc(S["dx"], 16)
    mk.sec("sync", fsync2)
    mk.n["dx"] += 16 * 8

    def ftens(e):
        e.wait_ge(S["ve"], veb)
        for g in range(8):
            e.wait_ge(S["dx"], dkb2 + 16 * (g + 1))
            if g >= 1:
                e.wait_ge(S["ve"], veb + 4 * g)
            for dc in range(NCH):
                bank = t["ps%d" % (dc // 4)]
                e.transpose(bank[:, (dc % 4) * 128:(dc % 4) * 128 + 128],
                            mk.A(stg + dc * 128, 128),
                            mk.cst(CST_ID, 128))
            e.drain().then_inc(S["mm"], NCH)
    mk.sec("tensor", ftens)
    mk.n["mm"] += 8 * NCH

    def fvec2(e):
        for g in range(8):
            e.wait_ge(S["mm"], mmb + (g + 1) * NCH)
            for bk in range(4):
                dst = xr[:, bk * 4:(bk + 1) * 4, g * 128:(g + 1) * 128]
                src = rr(t["ps%d" % bk].ap()[:, 0:512], "p (b r) -> p b r", b=4)
                e.tensor_copy(dst, src)
            e.drain().then_inc(S["ve"], 4)
    mk.sec("vector", fvec2)
    mk.n["ve"] += 8 * 4
    mk.barrier()


def emit_output(mk):
    mk.barrier()
    S, t = mk.S, mk.t
    stg = 0
    mmb, veb, dgb = mk.n["mm"], mk.n["ve"], mk.n["dg"]

    def ftens(e):
        for g in range(8):
            if g >= 1:
                e.wait_ge(S["ve"], veb + 4 * g)
            for dc in range(NCH):
                bank = t["ps%d" % (dc // 4)]
                e.transpose(bank[:, (dc % 4) * 128:(dc % 4) * 128 + 128],
                            mk.xap(dc, g * 128, 128),
                            mk.cst(CST_ID, 128))
            e.drain().then_inc(S["mm"], NCH)
    mk.sec("tensor", ftens)
    mk.n["mm"] += 8 * NCH

    gpb = mk.n["gp"]
    def fvec(e):
        for g in range(8):
            e.wait_ge(S["mm"], mmb + (g + 1) * NCH)
            if g >= 1:
                e.wait_ge(S["gp"], gpb + g)
            for bk in range(4):
                e.tensor_copy(mk.A(stg + bk * 512, 512), t["ps%d" % bk][:, 0:512])
            e.drain().then_inc(S["ve"], 4)
    mk.sec("vector", fvec)
    mk.n["ve"] += 8 * 4

    def fg(g_):
        for g in range(8):
            g_.wait_ge(S["ve"], veb + 4 * (g + 1))
            g_.dma_start(out=t["yout"].ap()[g * 128:(g + 1) * 128, :],
                         in_=mk.A(stg, 2048)).then_inc(S["dg"], 16)
            g_.wait_ge(S["dg"], dgb + 16 * (g + 1))
            g_.nop().then_inc(S["gp"], 1)
    mk.sec("gpsimd", fg)
    mk.n["dg"] += 16 * 8
    mk.n["gp"] += 8


def ln_full(mk, sq_off, rms_wcol=None, dst_off=None, add_off=None):
    S, t = mk.S, mk.t
    is_ln = rms_wcol is None
    ve_prev_apply = None
    for tt in range(2):
        if add_off is not None:
            veb0 = mk.n["ve"]
            def fadd(e, tt=tt):
                for kc in range(NCH):
                    dst = mk.xap(kc, tt * TTW)
                    e.tensor_add(dst, dst, mk.A(add_off + kc * NTOK + tt * TTW, TTW))
                e.drain().then_inc(S["ve"], 1)
            mk.sec("vector", fadd)
            mk.n["ve"] += 1

        mmb, scb, veb = mk.n["mm"], mk.n["sc"], mk.n["ve"]

        def fsc(e, tt=tt, veb=veb):
            if add_off is not None:
                e.wait_ge(S["ve"], veb)
            for kc in range(NCH):
                if kc >= 1:
                    e.wait_ge(S["mm"], mmb + kc)
                e.activation(mk.A(sq_off, 512), mk.xap(kc, tt * TTW), AF.Square)
                e.drain().then_inc(S["sc"], 1)
        mk.sec("scalar", fsc)
        mk.n["sc"] += NCH

        def fte(e, tt=tt, ve_prev_apply=ve_prev_apply):
            if ve_prev_apply is not None:
                e.wait_ge(S["ve"], ve_prev_apply)
            for kc in range(NCH):
                e.wait_ge(S["sc"], scb + kc + 1)
                if is_ln:
                    e.matmul(t["ps6"][0:1, 0:TTW], mk.cst(CST_ONES, 1),
                             mk.xap(kc, tt * TTW),
                             start=(kc == 0), stop=(kc == NCH - 1))
                e.matmul(t["ps7"][0:1, 0:TTW], mk.cst(CST_ONES, 1),
                         mk.A(sq_off, 512),
                         start=(kc == 0), stop=(kc == NCH - 1))
                e.drain().then_inc(S["mm"], 1)
        mk.sec("tensor", fte)
        mk.n["mm"] += NCH

        mmb2 = mk.n["mm"]
        def fve(e, mmb2=mmb2):
            e.wait_ge(S["mm"], mmb2)
            if is_ln:
                e.tensor_scalar_mul(t["sm0"][0:1, 0:512], t["ps6"][0:1, 0:TTW], 1.0 / DM)
            e.tensor_scalar_mul(t["sm0"][0:1, 512:1024], t["ps7"][0:1, 0:TTW], 1.0 / DM)
            e.drain()
            if is_ln:
                e.tensor_mul(t["sm0"][0:1, 1024:1536], t["sm0"][0:1, 0:512],
                             t["sm0"][0:1, 0:512])
                e.drain()
                e.tensor_sub(t["sm0"][0:1, 512:1024], t["sm0"][0:1, 512:1024],
                             t["sm0"][0:1, 1024:1536])
                e.drain()
            e.tensor_scalar_add(t["sm0"][0:1, 512:1024], t["sm0"][0:1, 512:1024], 1e-5)
            e.drain()
            e.reciprocal(t["sm0"][0:1, 512:1024], t["sm0"][0:1, 512:1024])
            e.drain().then_inc(S["ve"], 1)
        mk.sec("vector", fve)
        mk.n["ve"] += 1

        veb2 = mk.n["ve"]
        def fsc2(e, veb2=veb2):
            e.wait_ge(S["ve"], veb2)
            e.activation(t["sm0"][0:1, 512:1024], t["sm0"][0:1, 512:1024], AF.Sqrt)
            e.drain().then_inc(S["sc"], 1)
        mk.sec("scalar", fsc2)
        mk.n["sc"] += 1

        scb2 = mk.n["sc"]
        if is_ln:
            def fve2(e, scb2=scb2):
                e.wait_ge(S["sc"], scb2)
                e.tensor_mul(t["sm0"][0:1, 1024:1536], t["sm0"][0:1, 0:512],
                             t["sm0"][0:1, 512:1024])
                e.drain()
                e.tensor_scalar_mul(t["sm0"][0:1, 1024:1536],
                                    t["sm0"][0:1, 1024:1536], -1.0)
                e.drain().then_inc(S["ve"], 1)
            mk.sec("vector", fve2)
            mk.n["ve"] += 1
        veb3 = mk.n["ve"]

        def fte2(e, scb2=scb2, veb3=veb3):
            e.wait_ge(S["sc"], scb2)
            e.wait_ge(S["ve"], veb3)
            e.matmul(t["ps4"][:, 0:TTW], mk.onesrow(), t["sm0"][0:1, 512:1024],
                     start=True, stop=True)
            if is_ln:
                e.matmul(t["ps5"][:, 0:TTW], mk.onesrow(), t["sm0"][0:1, 1024:1536],
                         start=True, stop=True)
            e.drain().then_inc(S["mm"], 1)
        mk.sec("tensor", fte2)
        mk.n["mm"] += 1

        mmb4 = mk.n["mm"]
        def fve3(e, tt=tt, mmb4=mmb4):
            e.wait_ge(S["mm"], mmb4)
            for kc in range(NCH):
                src = mk.xap(kc, tt * TTW)
                dst = src if dst_off is None else mk.A(dst_off + kc * NTOK + tt * TTW, TTW)
                e.tensor_mul(dst, src, t["ps4"][:, 0:TTW])
                if is_ln:
                    e.drain()
                    e.tensor_add(dst, dst, t["ps5"][:, 0:TTW])
            e.drain().then_inc(S["ve"], 1)
        mk.sec("vector", fve3)
        mk.n["ve"] += 1
        ve_prev_apply = mk.n["ve"]

        if not is_ln:
            veb5 = mk.n["ve"]
            def fsc3(e, tt=tt, veb5=veb5):
                e.wait_ge(S["ve"], veb5)
                for kc in range(NCH):
                    dst = mk.A(dst_off + kc * NTOK + tt * TTW, TTW)
                    e.activation(dst, dst, AF.Identity, scale=mk.vec(rms_wcol + kc))
                e.drain().then_inc(S["sc"], 1)
            mk.sec("scalar", fsc3)
            mk.n["sc"] += 1
    mk.barrier()


def fm_ln(mk, buf_off, tts, sq_off, extra_scale=1.0, dma_to=None, ncols=NTOK):
    S, t = mk.S, mk.t
    ntts = len(tts)
    ve_apply = {}
    for h in range(NH):
        for tti, tt in enumerate(tts):
            i = h * ntts + tti
            src = mk.A(buf_off + h * ncols + tti * TTW, TTW)
            stat = [t["ps6"], t["ps7"]] if i % 2 == 0 else [t["ps2"], t["ps3"]]
            bcA = t["ps4"] if i % 2 == 0 else t["ps5"]
            bcB = stat[0]
            scb, mmb, veb = mk.n["sc"], mk.n["mm"], mk.n["ve"]

            def fsc(e, src=src, i=i, mmb=mmb):
                if i >= 1:
                    e.wait_ge(S["mm"], mmb)
                e.activation(mk.A(sq_off, 512), src, AF.Square)
                e.drain().then_inc(S["sc"], 1)
            mk.sec("scalar", fsc)
            mk.n["sc"] += 1

            def fte(e, src=src, stat=stat, scb=scb, i=i):
                e.wait_ge(S["sc"], scb + 1)
                if i >= 2 and (i - 2) in ve_apply:
                    e.wait_ge(S["ve"], ve_apply[i - 2])
                e.matmul(stat[0][0:1, 0:TTW], mk.cst(CST_ONES, 1), src,
                         start=True, stop=True)
                e.matmul(stat[1][0:1, 0:TTW], mk.cst(CST_ONES, 1), mk.A(sq_off, 512),
                         start=True, stop=True)
                e.drain().then_inc(S["mm"], 1)
            mk.sec("tensor", fte)
            mk.n["mm"] += 1

            mmb2 = mk.n["mm"]
            def fve(e, stat=stat, mmb2=mmb2):
                e.wait_ge(S["mm"], mmb2)
                e.tensor_scalar_mul(t["sm0"][0:1, 0:512], stat[0][0:1, 0:TTW], 1.0 / 128)
                e.tensor_scalar_mul(t["sm0"][0:1, 512:1024], stat[1][0:1, 0:TTW], 1.0 / 128)
                e.drain()
                e.tensor_mul(t["sm0"][0:1, 1024:1536], t["sm0"][0:1, 0:512],
                             t["sm0"][0:1, 0:512])
                e.drain()
                e.tensor_sub(t["sm0"][0:1, 512:1024], t["sm0"][0:1, 512:1024],
                             t["sm0"][0:1, 1024:1536])
                e.drain()
                e.tensor_scalar_add(t["sm0"][0:1, 512:1024], t["sm0"][0:1, 512:1024], 1e-5)
                e.drain()
                e.reciprocal(t["sm0"][0:1, 512:1024], t["sm0"][0:1, 512:1024])
                e.drain().then_inc(S["ve"], 1)
            mk.sec("vector", fve)
            mk.n["ve"] += 1

            veb2 = mk.n["ve"]
            def fsc2(e, veb2=veb2):
                e.wait_ge(S["ve"], veb2)
                e.activation(t["sm0"][0:1, 512:1024], t["sm0"][0:1, 512:1024], AF.Sqrt)
                e.drain().then_inc(S["sc"], 1)
            mk.sec("scalar", fsc2)
            mk.n["sc"] += 1

            scb2 = mk.n["sc"]
            def fve2(e, scb2=scb2):
                e.wait_ge(S["sc"], scb2)
                if extra_scale != 1.0:
                    e.tensor_scalar_mul(t["sm0"][0:1, 512:1024],
                                        t["sm0"][0:1, 512:1024], float(extra_scale))
                    e.drain()
                e.tensor_mul(t["sm0"][0:1, 1024:1536], t["sm0"][0:1, 0:512],
                             t["sm0"][0:1, 512:1024])
                e.drain()
                e.tensor_scalar_mul(t["sm0"][0:1, 1024:1536],
                                    t["sm0"][0:1, 1024:1536], -1.0)
                e.drain().then_inc(S["ve"], 1)
            mk.sec("vector", fve2)
            mk.n["ve"] += 1

            veb3 = mk.n["ve"]
            def fte2(e, bcA=bcA, bcB=bcB, veb3=veb3):
                e.wait_ge(S["ve"], veb3)
                e.matmul(bcA[:, 0:TTW], mk.onesrow(), t["sm0"][0:1, 512:1024],
                         start=True, stop=True)
                e.matmul(bcB[:, 0:TTW], mk.onesrow(), t["sm0"][0:1, 1024:1536],
                         start=True, stop=True)
                e.drain().then_inc(S["mm"], 1)
            mk.sec("tensor", fte2)
            mk.n["mm"] += 1

            mmb4 = mk.n["mm"]
            def fve3(e, src=src, bcA=bcA, bcB=bcB, mmb4=mmb4):
                e.wait_ge(S["mm"], mmb4)
                e.tensor_mul(src, src, bcA[:, 0:TTW])
                e.drain()
                e.tensor_add(src, src, bcB[:, 0:TTW])
                e.drain().then_inc(S["ve"], 1)
            mk.sec("vector", fve3)
            mk.n["ve"] += 1
            ve_apply[i] = mk.n["ve"]

            if dma_to is not None:
                veb4 = mk.n["ve"]
                def fgd(g, src=src, h=h, tt=tt, veb4=veb4):
                    g.wait_ge(S["ve"], veb4)
                    g.dma_start(out=dma_to.ap()[h * 128:(h + 1) * 128,
                                                tt * TTW:(tt + 1) * TTW],
                                in_=src).then_inc(S["dg"], 16)
                mk.sec("gpsimd", fgd)
                mk.n["dg"] += 16


def rope_buf(mk, buf_off, chunks, tts, perm_off, cos_off, sin_off, tmp_off,
             ncols=NTOK):
    S, t = mk.S, mk.t
    veb0 = mk.n["ve"]
    mmb0 = mk.n["mm"]
    units = [(ch, tti, tt) for ch in chunks for tti, tt in enumerate(tts)]

    def fte(e):
        for i, (ch, tti, tt) in enumerate(units):
            if i >= 2:
                e.wait_ge(S["ve"], veb0 + 2 * (i - 1))
            else:
                e.wait_ge(S["ve"], veb0)
            src = mk.A(buf_off + ch * ncols + tti * TTW, TTW)
            bank = t["ps%d" % (6 + i % 2)]
            e.matmul(bank[:, 0:TTW], mk.cst(perm_off, 128), src,
                     start=True, stop=True)
            e.drain().then_inc(S["mm"], 1)
    mk.sec("tensor", fte)
    mk.n["mm"] += len(units)

    def fve(e):
        for i, (ch, tti, tt) in enumerate(units):
            e.wait_ge(S["mm"], mmb0 + i + 1)
            src = mk.A(buf_off + ch * ncols + tti * TTW, TTW)
            bank = t["ps%d" % (6 + i % 2)]
            cosap = mk.A(cos_off + tt * TTW, TTW)
            sinap = mk.A(sin_off + tt * TTW, TTW)
            e.drain()
            e.tensor_mul(mk.A(tmp_off, 512), bank[:, 0:TTW], sinap)
            e.tensor_mul(src, src, cosap)
            e.drain()
            e.tensor_add(src, src, mk.A(tmp_off, 512))
            e.drain().then_inc(S["ve"], 2)
    mk.sec("vector", fve)
    mk.n["ve"] += 2 * len(units)


def attn_core(mk, qc, agko, agvo, kdim, score, krow_fn, vcol_fn):
    S, t = mk.S, mk.t
    nkb = NKB[qc]
    NU = nkb * NH
    qb, ob, eb, mb, kv = ATT_QB, ATT_OB, ATT_EB, ATT_MB, ATT_KV

    mmb, scb, veb = mk.n["mm"], mk.n["sc"], mk.n["ve"]
    klane = ("dka", "dkb")
    vlane = ("dva", "dvb")
    kcnt = {0: mk.n["dka"], 1: mk.n["dkb"]}
    vcnt = {0: mk.n["dva"], 1: mk.n["dvb"]}
    dmb = mk.n["dm"]
    rec = {"mask": {}, "k": {}, "v": {}}

    def fsync(e):
        for kbi in range(nkb):
            s = SLOTS.index((qc, kbi))
            if kbi >= 1:
                e.wait_ge(S["ve"], veb + 5 * NH * kbi)
            e.dma_start(out=mk.A(mb, 2048),
                        in_=t["masks"].ap()[s * 128:(s + 1) * 128, :]).then_inc(S["dm"], 16)
            rec["mask"][kbi] = dmb + 16 * (kbi + 1)
            blk, c0 = KBSRC[kbi]
            for h in range(NH):
                u = kbi * NH + h
                if u >= 2:
                    e.wait_ge(S["mm"], mmb + 5 * (u - 1))
                kr = krow_fn(h)
                kcnt[u % 2] += 16
                e.dma_start(out=mk.A(kv + (u % 2) * 512, 512),
                            in_=agko.ap()[blk * kdim + kr:blk * kdim + kr + 128,
                                          c0:c0 + 512]).then_inc(S[klane[u % 2]], 16)
                rec["k"][u] = (klane[u % 2], kcnt[u % 2])
                vc = vcol_fn(h)
                vsrc = rr(agvo.ap(), "(b tk p) d -> p b tk d", p=128, b=2)
                vcnt[u % 2] += 16
                e.dma_start(out=rr(mk.A(kv + 1024 + (u % 2) * 512, 512),
                                   "p (tk d) -> p tk d", tk=4),
                            in_=vsrc[:, blk, c0 // 128:c0 // 128 + 4,
                                     vc:vc + 128]).then_inc(S[vlane[u % 2]], 16)
                rec["v"][u] = (vlane[u % 2], vcnt[u % 2])
    mk.sec("sync", fsync)
    mk.n["dm"] = dmb + 16 * nkb
    mk.n["dka"] = kcnt[0]
    mk.n["dkb"] = kcnt[1]
    mk.n["dva"] = vcnt[0]
    mk.n["dvb"] = vcnt[1]

    def fte(e):
        for kbi in range(nkb):
            for h in range(NH):
                u = kbi * NH + h
                e.wait_ge(S[rec["k"][u][0]], rec["k"][u][1])
                for kc in range(4):
                    g = 4 * u + kc
                    if g >= 2:
                        e.wait_ge(S["sc"], scb + g - 1)
                    sbk = t["ps%d" % (2 + g % 2)]
                    e.matmul(sbk[:, 0:TTW],
                             mk.A(kv + (u % 2) * 512 + kc * 128, 128),
                             mk.A(qb + h * 512, TTW),
                             start=True, stop=True).then_inc(S["mm"], 1)
                e.wait_ge(S["ve"], veb + 5 * u + 4)
                e.wait_ge(S[rec["v"][u][0]], rec["v"][u][1])
                if u >= 2:
                    e.wait_ge(S["ve"], veb + 5 * (u - 1))
                obk = t["ps%d" % (4 + h % 2)]
                ins = None
                for kc in range(4):
                    e.matmul(t["ps6"][0:16, 0:TTW], mk.rowsel(h),
                             mk.A(eb + (u % 2) * 2048 + kc * 512, 512),
                             start=(u == 0 and kc == 0),
                             stop=(u == NU - 1 and kc == 3))
                    ins = e.matmul(obk[:, 0:TTW],
                                   mk.A(kv + 1024 + (u % 2) * 512 + kc * 128, 128),
                                   mk.A(eb + (u % 2) * 2048 + kc * 512, 512),
                                   start=(kc == 0), stop=(kc == 3))
                ins.then_inc(S["mm"], 1)
    mk.sec("tensor", fte)
    mk.n["mm"] += 5 * NU

    def fsc(e):
        for kbi in range(nkb):
            for h in range(NH):
                u = kbi * NH + h
                for kc in range(4):
                    g = 4 * u + kc
                    e.wait_ge(S["mm"], mmb + 5 * u + kc + 1)
                    if u >= 2:
                        e.wait_ge(S["mm"], mmb + 5 * (u - 1))
                    sbk = t["ps%d" % (2 + g % 2)]
                    dst = mk.A(eb + (u % 2) * 2048 + kc * 512, 512)
                    if score[0] == "square":
                        e.activation(dst, sbk[:, 0:TTW], AF.Square)
                    else:
                        e.activation(dst, sbk[:, 0:TTW], AF.Exp, scale=float(score[1]))
                    e.drain().then_inc(S["sc"], 1)
    mk.sec("scalar", fsc)
    mk.n["sc"] += 4 * NU

    def fve(e):
        for kbi in range(nkb):
            for h in range(NH):
                u = kbi * NH + h
                e.wait_ge(S["dm"], rec["mask"][kbi])
                for kc in range(4):
                    e.wait_ge(S["sc"], scb + 4 * u + kc + 1)
                    ap = mk.A(eb + (u % 2) * 2048 + kc * 512, 512)
                    e.tensor_mul(ap, ap, mk.A(mb + kc * 512, 512))
                    e.drain().then_inc(S["ve"], 1)
                e.wait_ge(S["mm"], mmb + 5 * u + 5)
                obk = t["ps%d" % (4 + h % 2)]
                dst = mk.A(ob + h * 512, 512)
                e.drain()
                if kbi == 0:
                    e.tensor_copy(dst, obk[:, 0:TTW])
                else:
                    e.tensor_add(dst, dst, obk[:, 0:TTW])
                e.drain().then_inc(S["ve"], 1)
    mk.sec("vector", fve)
    mk.n["ve"] += 5 * NU


def znorm_apply(mk, z_eps):
    S, t = mk.S, mk.t
    ob = ATT_OB
    mmb = mk.n["mm"]

    def fve(e):
        e.wait_ge(S["mm"], mmb)
        src = t["ps6"][0:16, 0:TTW]
        if z_eps:
            e.tensor_scalar_add(t["sm0"][0:16, 0:512], src, float(z_eps))
            e.drain()
            e.reciprocal(t["sm0"][0:16, 0:512], t["sm0"][0:16, 0:512])
        else:
            e.reciprocal(t["sm0"][0:16, 0:512], src)
        e.drain().then_inc(S["ve"], 1)
    mk.sec("vector", fve)
    mk.n["ve"] += 1

    veb2 = mk.n["ve"]
    mmb2 = mk.n["mm"]
    def fte(e):
        e.wait_ge(S["ve"], veb2)
        for h in range(NH):
            if h >= 2:
                e.wait_ge(S["ve"], veb2 + h - 1)
            bank = t["ps%d" % (2 + h % 2)]
            e.matmul(bank[:, 0:TTW], mk.hsel(h), t["sm0"][0:16, 0:512],
                     start=True, stop=True)
            e.drain().then_inc(S["mm"], 1)
    mk.sec("tensor", fte)
    mk.n["mm"] += NH

    def fve2(e):
        e.drain()
        for h in range(NH):
            e.wait_ge(S["mm"], mmb2 + h + 1)
            bank = t["ps%d" % (2 + h % 2)]
            dst = mk.A(ob + h * 512, 512)
            e.tensor_mul(dst, dst, bank[:, 0:TTW])
            e.drain().then_inc(S["ve"], 1)
    mk.sec("vector", fve2)
    mk.n["ve"] += NH


def proj_vT(mk, W_ap, Kdim, mts, rhs_fn, agvi, vt_off, bias_col=None):
    S, t = mk.S, mk.t
    vt = vt_off
    vs = vt_off + 1024
    nkc = Kdim // 128
    mmb, scb, veb, dgb = (mk.n["mm"], mk.n["sc"], mk.n["ve"], mk.n["dg"])
    lane = ("dwa", "dwb")
    lb = [mk.n["dwa"], mk.n["dwb"]]
    Wr = rr(W_ap, "(kc p) m -> p kc m", p=128)
    ntl = len(mts) * 2

    def fsync(e):
        for j, (c0, w) in enumerate(mts):
            if j >= 2:
                e.wait_ge(S["mm"], mmb + 2 * ((j - 1) * 2))
            else:
                if mk.wb_last[j % 2]:
                    e.wait_ge(S["mm"], mk.wb_last[j % 2])
            e.dma_start(out=mk.wpanel_dst(j % 2, nkc, w),
                        in_=Wr[:, :, c0:c0 + w]).then_inc(S[lane[j % 2]], 16)
    mk.sec("sync", fsync)
    mk.n["dwa"] += 16 * ((len(mts) + 1) // 2)
    mk.n["dwb"] += 16 * (len(mts) // 2)
    for sslot in range(2):
        js = [j for j in range(len(mts)) if j % 2 == sslot]
        if js:
            mk.wb_last[sslot] = mmb + 2 * ntl

    recp = {}
    rect = {}

    def fte(e):
        cnt = mmb
        for j, (c0, w) in enumerate(mts):
            e.wait_ge(S[lane[j % 2]], lb[j % 2] + 16 * (j // 2 + 1))
            for tt in range(2):
                i = j * 2 + tt
                if i >= 2:
                    e.wait_ge(S["sc"], scb + i - 1)
                elif mk.ps01_last[i % 2] is not None:
                    e.wait_ge(S[mk.ps01_last[i % 2][0]], mk.ps01_last[i % 2][1])
                ps = t["ps%d" % (i % 2)]
                ins = None
                for kc in range(nkc):
                    ins = e.matmul(ps[0:w, 0:TTW], mk.wtile(j % 2, kc, w),
                                   rhs_fn(kc, tt), start=(kc == 0),
                                   stop=(kc == nkc - 1))
                ins.then_inc(S["mm"], 1)
                cnt += 1
                recp[i] = cnt
                if i >= 1:
                    it = i - 1
                    e.wait_ge(S["sc"], scb + it + 1)
                    if it >= 2:
                        e.wait_ge(S["ve"], veb + it - 1)
                    bank = t["ps%d" % (2 + it % 2)]
                    for b in range(4):
                        e.transpose(bank[:, b * 128:(b + 1) * 128],
                                    mk.A(vt + (it % 2) * 512 + b * 128, 128),
                                    mk.cst(CST_ID, 128))
                    e.drain().then_inc(S["mm"], 1)
                    cnt += 1
                    rect[it] = cnt
        it = ntl - 1
        e.wait_ge(S["sc"], scb + it + 1)
        e.wait_ge(S["ve"], veb + it - 1)
        bank = t["ps%d" % (2 + it % 2)]
        for b in range(4):
            e.transpose(bank[:, b * 128:(b + 1) * 128],
                        mk.A(vt + (it % 2) * 512 + b * 128, 128),
                        mk.cst(CST_ID, 128))
        e.drain().then_inc(S["mm"], 1)
        cnt += 1
        rect[it] = cnt
    mk.sec("tensor", fte)
    mk.n["mm"] += 2 * ntl
    for par in range(2):
        iis = [ii2 for ii2 in range(ntl) if ii2 % 2 == par]
        if iis:
            mk.ps01_last[par] = ("sc", mk.n["sc"] + iis[-1] + 1)

    def fsc(e):
        for i in range(ntl):
            e.wait_ge(S["mm"], recp[i])
            if i >= 2:
                e.wait_ge(S["mm"], rect[i - 2])
            ps = t["ps%d" % (i % 2)]
            j = i // 2
            if bias_col is not None:
                e.activation(mk.A(vt + (i % 2) * 512, 512), ps[:, 0:TTW],
                             AF.Identity, bias=mk.vec(bias_col + j))
            else:
                e.activation(mk.A(vt + (i % 2) * 512, 512), ps[:, 0:TTW], AF.Identity)
            e.drain().then_inc(S["sc"], 1)
    mk.sec("scalar", fsc)
    mk.n["sc"] += ntl

    gpb = mk.n["gp"]
    def fve(e):
        for i in range(ntl):
            e.wait_ge(S["mm"], rect[i])
            if i >= 2:
                e.wait_ge(S["gp"], gpb + i - 1)
            bank = t["ps%d" % (2 + i % 2)]
            e.tensor_copy(mk.A(vs + (i % 2) * 512, 512), bank[:, 0:512])
            e.drain().then_inc(S["ve"], 1)
    mk.sec("vector", fve)
    mk.n["ve"] += ntl

    def fg(g):
        for i in range(ntl):
            g.wait_ge(S["ve"], veb + i + 1)
            j, tt = i // 2, i % 2
            c0 = mts[j][0]
            dst = rr(agvi.ap(), "(tk p) d -> p tk d", p=128)[
                :, tt * 4:(tt + 1) * 4, c0:c0 + 128]
            src = rr(mk.A(vs + (i % 2) * 512, 512), "p (b d) -> p b d", b=4)
            g.dma_start(out=dst, in_=src).then_inc(S["dg"], 16)
            g.wait_ge(S["dg"], dgb + 16 * (i + 1))
            g.nop().then_inc(S["gp"], 1)
    mk.sec("gpsimd", fg)
    mk.n["dg"] += 16 * ntl
    mk.n["gp"] += ntl


def allgather_pair(mk, pairs):
    S = mk.S
    dgt = mk.n["dg"]
    groups = [[0, 1], [2, 3], [4, 5], [6, 7]]
    def fg(g):
        g.wait_ge(S["dg"], dgt)
        for p in pairs:
            g.collective_compute(
                "AllGather", OP.bypass, replica_groups=groups,
                ins=[p[0].ap().opt()],
                outs=[p[1].ap().opt()],
            ).then_inc(S["cc"], 1)
    mk.sec("gpsimd", fg)
    mk.n["cc"] += len(pairs)


def wait_cc(mk, val):
    S = mk.S
    def f(e):
        e.wait_ge(S["cc"], val)
    mk.sec("sync", f)


def p1_l0_attn(mk):
    S, t = mk.S, mk.t
    kq = 0
    def ev(e, i, j, tt, ps):
        e.activation(mk.A(kq + j * NTOK + tt * TTW, TTW), ps, AF.Identity,
                     scale=mk.vec(VGAM), bias=mk.vec(VBET))
    mk.proj(t["wk0"].ap(), DM, MTS16, (0, 1),
            lambda kc, tt: mk.xap(kc, tt * TTW), ev, eng="scalar")
    mk.barrier()
    fm_ln(mk, kq, (0, 1), sq_off=16384, dma_to=t["agki_a1"])
    mk.barrier()

    proj_vT(mk, t["wv0"].ap(), DM, MTS16,
            lambda kc, tt: mk.xap(kc, tt * TTW), t["agvi_a1"], vt_off=0)
    allgather_pair(mk, [(t["agki_a1"], t["agko_a1"]), (t["agvi_a1"], t["agvo_a1"])])
    mk.barrier()

    ccq = mk.n["cc"]
    for qc in range(2):
        def evq(e, i, j, tt, ps):
            e.activation(mk.A(ATT_QB + j * 512, TTW), ps, AF.Identity,
                         scale=mk.vec(VGAM), bias=mk.vec(VBET))
        mk.proj(t["wq0"].ap(), DM, MTS16, (qc,),
                lambda kc, tt: mk.xap(kc, tt * TTW), evq, eng="scalar")
        mk.barrier()
        fm_ln(mk, ATT_QB, (qc,), sq_off=ATT_EB, extra_scale=128.0 ** -0.5,
              ncols=512)
        wait_cc(mk, ccq)
        mk.barrier()
        attn_core(mk, qc, t["agko_a1"], t["agvo_a1"], DM, ("square",),
                  lambda h: h * 128, lambda h: h * 128)
        znorm_apply(mk, 1e-5)
        mk.barrier()
        def evo(e, i, j, tt, ps):
            dst = mk.xap(j, tt * TTW)
            e.tensor_add(dst, dst, ps)
        mk.proj(t["wo0"].ap(), DM, MTS16, (qc,),
                lambda kc, tt: mk.A(ATT_OB + kc * 512, 512), evo, eng="vector")
        mk.barrier()
    ln_full(mk, sq_off=0)


def p2_l0_ffn(mk):
    S, t = mk.S, mk.t
    dgb = mk.n["dg"]
    def fg(g):
        g.dma_start(out=t["stash"].ap(), in_=t["x"][:, :]).then_inc(S["dg"], 16)
    mk.sec("gpsimd", fg)
    mk.n["dg"] += 16
    mk.barrier()

    xn = 0
    ln_full(mk, sq_off=16384, rms_wcol=VL0W1, dst_off=xn)

    kvd = 16384
    qd = 20480
    mk.proj(t["kvdw"].ap(), DM, MTS4, (0, 1),
            lambda kc, tt: mk.A(xn + kc * NTOK + tt * TTW, TTW),
            _ev_copy(mk, kvd), eng="vector")
    mk.proj(t["qdw"].ap(), DM, MTS4, (0, 1),
            lambda kc, tt: mk.A(xn + kc * NTOK + tt * TTW, TTW),
            _ev_copy(mk, qd), eng="vector")
    mk.barrier()
    def fgq(g):
        g.dma_start(out=t["qdsp"].ap(), in_=mk.A(qd, 4 * NTOK)).then_inc(S["dg"], 16)
        g.wait_ge(S["dg"], mk.n["dg"] + 16)
        g.nop().then_inc(S["gp"], 1)
    mk.sec("gpsimd", fgq)
    mk.n["dg"] += 16
    mk.n["gp"] += 1
    gp_spill = mk.n["gp"]

    kp = 24576
    def evkp(e, i, j, tt, ps):
        e.tensor_copy(mk.A(kp + (i % 2) * 512, 512), ps)
    def dmakp(g, i, j, tt):
        for half in range(2):
            h = 2 * j + half
            src = mk.A(kp + (i % 2) * 512, 512, p0=half * 64, pn=64)
            g.dma_start(out=t["agki_a2"].ap()[h * 128:h * 128 + 64,
                                              tt * TTW:(tt + 1) * TTW],
                        in_=src).then_inc(S["dg"], 16)
    mk.proj(t["kuw"].ap(), 512, MTS8, (0, 1),
            lambda kc, tt: mk.A(kvd + kc * NTOK + tt * TTW, TTW),
            evkp, eng="vector", dma_out=dmakp, dma_per_tile=2)
    mk.barrier()

    proj_vT(mk, t["vuw"].ap(), 512, MTS16,
            lambda kc, tt: mk.A(kvd + kc * NTOK + tt * TTW, TTW),
            t["agvi_a2"], vt_off=24576)
    mk.barrier()

    tabs = 24576
    dkb = mk.n["dx"]
    def fsy(e):
        e.dma_start(out=mk.A(tabs, NTOK), in_=t["l0cos"].ap()).then_inc(S["dx"], 16)
        e.dma_start(out=mk.A(tabs + NTOK, NTOK), in_=t["l0sin"].ap()).then_inc(S["dx"], 16)
    mk.sec("sync", fsy)
    mk.n["dx"] += 32
    krb = 16384
    for half in range(2):
        mts = MTS8[half * 4:(half + 1) * 4]
        def evkr(e, i, j, tt, ps):
            e.tensor_copy(mk.A(krb + j * NTOK + tt * TTW, TTW), ps)
        mk.proj(t["ropekw"].ap()[:, half * 512:(half + 1) * 512], DM,
                MTS4, (0, 1),
                lambda kc, tt: mk.A(xn + kc * NTOK + tt * TTW, TTW),
                evkr, eng="vector")
        if half == 0:
            def fw(e, dkb=dkb):
                e.wait_ge(S["dx"], dkb + 32)
            mk.sec("vector", fw)
        mk.barrier()
        rope_buf(mk, krb, list(range(4)), (0, 1),
                 CST_PERMH, tabs, tabs + NTOK, tmp_off=krb + 4 * NTOK)
        veb = mk.n["ve"]
        def fgkr(g, half=half, veb=veb):
            g.wait_ge(S["ve"], veb)
            for j in range(4):
                for hh in range(2):
                    h = 2 * (half * 4 + j) + hh
                    g.dma_start(out=t["agki_a2"].ap()[h * 128 + 64:h * 128 + 128, :],
                                in_=mk.A(krb + j * NTOK, NTOK, p0=hh * 64, pn=64)
                                ).then_inc(S["dg"], 16)
        mk.sec("gpsimd", fgkr)
        mk.n["dg"] += 16 * 8
        mk.barrier()
    allgather_pair(mk, [(t["agki_a2"], t["agko_a2"]), (t["agvi_a2"], t["agvo_a2"])])
    mk.barrier()

    ccq = mk.n["cc"]
    for qc in range(2):
        dkb2 = mk.n["dx"]
        def fsy2(e):
            e.dma_start(out=mk.A(ATT_QTAB, NTOK), in_=t["l0cosq"].ap()).then_inc(S["dx"], 16)
            e.dma_start(out=mk.A(ATT_QTAB + NTOK, NTOK), in_=t["l0sinq"].ap()).then_inc(S["dx"], 16)
        mk.sec("sync", fsy2)
        mk.n["dx"] += 32
        def fgz(g):
            g.wait_ge(S["gp"], gp_spill)
            g.dma_start(out=mk.A(ATT_EB, 4096), in_=t["qdsp"].ap()).then_inc(S["dg"], 16)
            g.wait_ge(S["dg"], mk.n["dg"] + 16)
            g.nop().then_inc(S["gp"], 1)
        mk.sec("gpsimd", fgz)
        mk.n["dg"] += 16
        mk.n["gp"] += 1
        gp3 = mk.n["gp"]
        def fwv(e):
            e.wait_ge(S["gp"], gp3)
            e.wait_ge(S["dx"], dkb2 + 32)
        mk.sec("tensor", fwv)
        mk.barrier()

        def evq(e, i, j, tt, ps):
            e.tensor_copy(mk.A(ATT_QB + j * 512, TTW), ps)
        mk.proj(t["wqasm"].ap(), 512, MTS16, (qc,),
                lambda kc, tt: mk.A(ATT_EB + kc * NTOK + tt * TTW, TTW),
                evq, eng="vector")
        mk.barrier()
        rope_buf(mk, ATT_QB, list(range(NH)), (qc,), CST_PERMQ,
                 ATT_QTAB, ATT_QTAB + NTOK, tmp_off=ATT_MB, ncols=512)
        wait_cc(mk, ccq)
        mk.barrier()
        attn_core(mk, qc, t["agko_a2"], t["agvo_a2"], DM, ("exp", 128.0 ** -0.5),
                  lambda h: h * 128, lambda h: h * 128)
        znorm_apply(mk, None)
        mk.barrier()
        def evo(e, i, j, tt, ps):
            dst = mk.xap(j, tt * TTW)
            e.tensor_add(dst, dst, ps)
        mk.proj(t["ow"].ap(), DM, MTS16, (qc,),
                lambda kc, tt: mk.A(ATT_OB + kc * 512, 512), evo, eng="vector")
        mk.barrier()

    xn2 = 0
    ln_full(mk, sq_off=16384, rms_wcol=VL0W2, dst_off=xn2)
    moe_router(mk, xn2)
    moe_experts(mk, xn2)

    def fg2(g):
        g.dma_start(out=mk.A(0, NCH * NTOK), in_=t["stash"].ap()).then_inc(S["dg"], 16)
        g.wait_ge(S["dg"], mk.n["dg"] + 16)
        g.nop().then_inc(S["gp"], 1)
    mk.sec("gpsimd", fg2)
    mk.n["dg"] += 16
    mk.n["gp"] += 1
    gp5 = mk.n["gp"]
    def fw3(e):
        e.wait_ge(S["gp"], gp5)
    mk.sec("vector", fw3)
    ln_full(mk, sq_off=16384, add_off=0)


def _ev_copy(mk, dst_off, ncols=NTOK):
    def ev(e, i, j, tt, ps):
        e.tensor_copy(mk.A(dst_off + j * ncols + tt * TTW, TTW), ps)
    return ev


def moe_router(mk, xn2):
    S, t = mk.S, mk.t
    def ev(e, i, j, tt, ps):
        e.activation(t["sm0"][0:7, tt * TTW:(tt + 1) * TTW], ps[0:7, 0:TTW],
                     AF.Sigmoid, bias=mk.vec(VRB0, pn=7))
    mk.proj(t["rw0"].ap(), DM, [(0, 7)], (0, 1),
            lambda kc, tt: mk.A(xn2 + kc * NTOK + tt * TTW, TTW), ev, eng="scalar")
    mk.barrier()

    ptm = 1024
    wtm = 1088
    mmb, veb = mk.n["mm"], mk.n["ve"]

    def fte(e):
        for g in range(8):
            if g >= 2:
                e.wait_ge(S["ve"], veb + g - 1)
            bank = t["ps%d" % (2 + g % 2)]
            e.transpose(bank[0:128, 0:7], t["sm0"][0:7, g * 128:(g + 1) * 128],
                        mk.cst(CST_ID, 7, pn=7))
            e.drain().then_inc(S["mm"], 1)
    mk.sec("tensor", fte)
    mk.n["mm"] += 8

    def fve(e):
        for g in range(8):
            e.wait_ge(S["mm"], mmb + g + 1)
            bank = t["ps%d" % (2 + g % 2)]
            e.tensor_copy(t["sm0"][:, ptm + g * 8: ptm + g * 8 + 7], bank[:, 0:7])
            e.memset(t["sm0"][:, ptm + g * 8 + 7: ptm + g * 8 + 8], -1.0)
            e.drain().then_inc(S["ve"], 1)
    mk.sec("vector", fve)
    mk.n["ve"] += 8

    veb2 = mk.n["ve"]
    def fve2(e):
        e.wait_ge(S["ve"], veb2)
        for g in range(8):
            pg = t["sm0"][:, ptm + g * 8: ptm + g * 8 + 8]
            wg = t["sm0"][:, wtm + g * 8: wtm + g * 8 + 8]
            e.drain()
            e.max(out=t["sm2"][:, 0:8], in_=pg)
            e.drain()
            e.tensor_copy(t["sm2"][:, 8:10], t["sm2"][:, 0:2])
            e.memset(t["sm2"][:, 10:16], -2.0)
            e.drain()
            e.match_replace(out=wg, in_to_replace=t["sm2"][:, 8:16],
                            in_values=pg, imm_value=0.0)
            e.tensor_add(t["sm2"][:, 16:17], t["sm2"][:, 0:1], t["sm2"][:, 1:2])
            e.drain()
            e.reciprocal(t["sm2"][:, 16:17], t["sm2"][:, 16:17])
            e.drain()
            e.tensor_sub(wg, pg, wg)
            e.drain()
            e.tensor_single_scalar(wg, wg, t["sm2"][:, 16:17], OP.mult)
        e.drain().then_inc(S["ve"], 1)
    mk.sec("vector", fve2)
    mk.n["ve"] += 1

    veb3 = mk.n["ve"]
    mmb3 = mk.n["mm"]
    def fte2(e):
        e.wait_ge(S["ve"], veb3)
        for g in range(8):
            bank = t["ps%d" % (2 + (g // 4) % 2)]
            e.transpose(bank[0:8, (g % 4) * 128:(g % 4) * 128 + 128],
                        t["sm0"][:, wtm + g * 8: wtm + g * 8 + 8], mk.cst(CST_ID, 128))
            if g % 4 == 3:
                e.drain().then_inc(S["mm"], 1)
    mk.sec("tensor", fte2)
    mk.n["mm"] += 2

    wfm = 16384
    def fve3(e):
        for half in range(2):
            e.wait_ge(S["mm"], mmb3 + half + 1)
            bank = t["ps%d" % (2 + half % 2)]
            e.tensor_copy(mk.A(wfm + half * 512, 512, pn=8), bank[0:8, 0:512])
        e.drain().then_inc(S["ve"], 1)
    mk.sec("vector", fve3)
    mk.n["ve"] += 1
    mk.barrier()
    return wfm


WFM_OFF = 16384


def moe_experts(mk, xn2):
    S, t = mk.S, mk.t
    hb = 17408
    wB = 25600
    for ei in range(8):
        routed = ei >= 1
        e_idx = ei - 1
        if routed:
            mmb = mk.n["mm"]
            veb = mk.n["ve"]
            def fteb(e, e_idx=e_idx, veb=veb):
                e.wait_ge(S["ve"], veb)
                for tt in range(2):
                    e.matmul(t["ps%d" % (6 + tt)][:, 0:TTW],
                             mk.cst(CST_ESEL + e_idx * 128, 128, pn=7),
                             mk.A(WFM_OFF + tt * TTW, TTW, pn=7),
                             start=True, stop=True)
                    e.drain().then_inc(S["mm"], 1)
            mk.sec("tensor", fteb)
            mk.n["mm"] += 2
            def fveb(e, mmb=mmb):
                for tt in range(2):
                    e.wait_ge(S["mm"], mmb + tt + 1)
                    e.tensor_scalar_mul(mk.A(wB + tt * TTW, TTW),
                                        t["ps%d" % (6 + tt)][:, 0:TTW], 1.0 / 8.0)
                e.drain().then_inc(S["ve"], 1)
            mk.sec("vector", fveb)
            mk.n["ve"] += 1
            mk.barrier()
            Wg = t["rg"].ap()[e_idx * DM:(e_idx + 1) * DM, :]
            Wu = t["ru"].ap()[e_idx * DM:(e_idx + 1) * DM, :]
            Wd = t["rd"].ap()[e_idx * 1024:(e_idx + 1) * 1024, :]
        else:
            Wg, Wu, Wd = t["shg"].ap(), t["shu"].ap(), t["shd"].ap()

        gate_up(mk, xn2, Wg, Wu, hb, wB if routed else None)
        mk.barrier()
        def evd(e, i, j, tt, ps):
            dst = mk.xap(j, tt * TTW)
            e.tensor_add(dst, dst, ps)
        mk.proj(Wd, 1024, MTS16, (0, 1),
                lambda kc, tt: mk.A(hb + kc * NTOK + tt * TTW, TTW), evd,
                eng="vector")
        mk.barrier()


def gate_up(mk, xn2, Wg, Wu, hb, wB):
    S, t = mk.S, mk.t
    nkc = NCH
    mmb, scb, veb = mk.n["mm"], mk.n["sc"], mk.n["ve"]
    ga, gb = mk.n["dwa"], mk.n["dwb"]
    Wgr = rr(Wg, "(kc p) m -> p kc m", p=128)
    Wur = rr(Wu, "(kc p) m -> p kc m", p=128)

    def fsync(e):
        for j in range(8):
            if j >= 1:
                e.wait_ge(S["mm"], mmb + (j - 1) * 4 + 2)
            else:
                for sslot in range(2):
                    if mk.wb_last[sslot]:
                        e.wait_ge(S["mm"], mk.wb_last[sslot])
            e.dma_start(out=mk.wpanel_dst(0, nkc, 128),
                        in_=Wgr[:, :, j * 128:(j + 1) * 128]).then_inc(S["dwa"], 16)
            if j >= 1:
                e.wait_ge(S["mm"], mmb + (j - 1) * 4 + 4)
            e.dma_start(out=mk.wpanel_dst(1, nkc, 128),
                        in_=Wur[:, :, j * 128:(j + 1) * 128]).then_inc(S["dwb"], 16)
    mk.sec("sync", fsync)
    mk.n["dwa"] += 16 * 8
    mk.n["dwb"] += 16 * 8
    mk.wb_last[0] = mk.wb_last[1] = mk.n["mm"] + 8 * 2 * 2

    banks = ("ps0", "ps2", "ps1", "ps3")

    def fte(e):
        for j in range(8):
            e.wait_ge(S["dwa"], ga + 16 * (j + 1))
            for gi in range(4):
                slot, tt = gi // 2, gi % 2
                if gi == 2:
                    e.wait_ge(S["dwb"], gb + 16 * (j + 1))
                if j >= 1:
                    e.wait_ge(S["ve"], veb + (j - 1) * 2 + tt + 1)
                elif gi == 0:
                    for par in range(2):
                        if mk.ps01_last[par] is not None:
                            e.wait_ge(S[mk.ps01_last[par][0]],
                                      mk.ps01_last[par][1])
                ins = None
                for kc in range(nkc):
                    ins = e.matmul(t[banks[gi]][:, 0:TTW], mk.wtile(slot, kc, 128),
                                   mk.A(xn2 + kc * NTOK + tt * TTW, TTW),
                                   start=(kc == 0), stop=(kc == nkc - 1))
                ins.then_inc(S["mm"], 1)
    mk.sec("tensor", fte)
    mk.n["mm"] += 8 * 2 * 2

    def fsc(e):
        for i in range(16):
            j, tt = i // 2, i % 2
            e.wait_ge(S["mm"], mmb + 4 * j + tt + 1)
            dst = mk.A(hb + j * NTOK + tt * TTW, TTW)
            e.activation(dst, t[banks[tt]][:, 0:TTW], AF.Silu)
            e.drain().then_inc(S["sc"], 1)
    mk.sec("scalar", fsc)
    mk.n["sc"] += 16

    def fve(e):
        for i in range(16):
            j, tt = i // 2, i % 2
            e.wait_ge(S["sc"], scb + i + 1)
            e.wait_ge(S["mm"], mmb + 4 * j + 2 + tt + 1)
            dst = mk.A(hb + j * NTOK + tt * TTW, TTW)
            e.tensor_mul(dst, dst, t[banks[2 + tt]][:, 0:TTW])
            e.drain()
            if wB is None:
                e.tensor_scalar_mul(dst, dst, 1.0 / 8.0)
            else:
                e.tensor_mul(dst, dst, mk.A(wB + tt * TTW, TTW))
            e.drain().then_inc(S["ve"], 1)
    mk.sec("vector", fve)
    mk.n["ve"] += 16
    mk.ps01_last[0] = mk.ps01_last[1] = ("ve", mk.n["ve"])


def p3_l1_attn(mk):
    S, t = mk.S, mk.t
    tabs = 0
    dkb = mk.n["dk"]
    def fsy(e):
        e.dma_start(out=mk.A(tabs, NTOK), in_=t["l1cos"].ap()).then_inc(S["dx"], 16)
        e.dma_start(out=mk.A(tabs + NTOK, NTOK), in_=t["l1sin"].ap()).then_inc(S["dx"], 16)
    mk.sec("sync", fsy)
    mk.n["dx"] += 32

    kb = 2048
    def evk(e, i, j, tt, ps):
        e.activation(mk.A(kb + j * NTOK + tt * TTW, TTW), ps, AF.Identity,
                     bias=mk.vec(VBK1 + j))
    mk.proj(t["wk1"].ap(), DM, MTS4, (0, 1),
            lambda kc, tt: mk.xap(kc, tt * TTW), evk, eng="scalar")
    mk.barrier()
    nrm_factors(mk, kb, 4, None, sq_off=6144)
    def fwk(e):
        e.wait_ge(S["dx"], dkb + 32)
    mk.sec("vector", fwk)
    rope_buf(mk, kb, list(range(4)), (0, 1), CST_PERMI, tabs, tabs + NTOK,
             tmp_off=6144)
    apply_fac(mk, kb, 4)
    veb = mk.n["ve"]
    def fgk(g):
        g.wait_ge(S["ve"], veb)
        for j in range(4):
            g.dma_start(out=t["agki_a3"].ap()[j * 128:(j + 1) * 128, :],
                        in_=mk.A(kb + j * NTOK, NTOK)).then_inc(S["dg"], 16)
    mk.sec("gpsimd", fgk)
    mk.n["dg"] += 16 * 4
    mk.barrier()

    proj_vT(mk, t["wv1"].ap(), DM, MTS4,
            lambda kc, tt: mk.xap(kc, tt * TTW), t["agvi_a3"],
            vt_off=6656, bias_col=VBV1)
    allgather_pair(mk, [(t["agki_a3"], t["agko_a3"]), (t["agvi_a3"], t["agvo_a3"])])
    mk.barrier()

    ccq = mk.n["cc"]
    for qc in range(2):
        def evq(e, i, j, tt, ps):
            e.activation(mk.A(ATT_QB + j * 512, TTW), ps, AF.Identity,
                         bias=mk.vec(VBQ1 + j))
        mk.proj(t["wq1"].ap(), DM, MTS16, (qc,),
                lambda kc, tt: mk.xap(kc, tt * TTW), evq, eng="scalar")
        mk.barrier()
        nrm_factors(mk, ATT_QB, NH, VALP, tts=(qc,), ncols=512, sq_off=ATT_EB)
        rope_buf(mk, ATT_QB, list(range(NH)), (qc,), CST_PERMI,
                 ATT_QTAB, ATT_QTAB + NTOK, tmp_off=ATT_MB, ncols=512)
        apply_fac(mk, ATT_QB, NH, tts=(0,), ncols=512)
        wait_cc(mk, ccq)
        mk.barrier()
        attn_core(mk, qc, t["agko_a3"], t["agvo_a3"], 512, ("exp", 1.0),
                  lambda h: (h // 4) * 128, lambda h: (h // 4) * 128)
        znorm_apply(mk, None)
        mk.barrier()
        def evo(e, i, j, tt, ps):
            dst = mk.xap(j, tt * TTW)
            e.scalar_tensor_tensor(dst, ps, mk.vec(VBO1 + j), dst, OP.add, OP.add)
        mk.proj(t["wo1"].ap(), DM, MTS16, (qc,),
                lambda kc, tt: mk.A(ATT_OB + kc * 512, 512), evo, eng="vector")
        mk.barrier()
    ln_full(mk, sq_off=0)


def nrm_factors(mk, buf, nch, alpha_col, sq_off, tts=(0, 1), ncols=NTOK):
    S, t = mk.S, mk.t
    for tti, tt in enumerate(tts):
        scb, mmb = mk.n["sc"], mk.n["mm"]
        def fsc(e, tti=tti, mmb=mmb):
            for ch in range(nch):
                if ch >= 1:
                    e.wait_ge(S["mm"], mmb + ch)
                e.activation(mk.A(sq_off, 512),
                             mk.A(buf + ch * ncols + tti * TTW, TTW), AF.Square)
                e.drain().then_inc(S["sc"], 1)
        mk.sec("scalar", fsc)
        mk.n["sc"] += nch

        def fte(e, tti=tti, scb=scb):
            for ch in range(nch):
                e.wait_ge(S["sc"], scb + ch + 1)
                e.matmul(t["ps6"][0:16, 0:TTW], mk.rowsel(ch), mk.A(sq_off, 512),
                         start=(ch == 0), stop=(ch == nch - 1))
                e.drain().then_inc(S["mm"], 1)
        mk.sec("tensor", fte)
        mk.n["mm"] += nch

        mmb2 = mk.n["mm"]
        def fve(e, tti=tti, mmb2=mmb2):
            e.wait_ge(S["mm"], mmb2)
            e.tensor_copy(t["sm0"][0:16, tti * 512:(tti + 1) * 512],
                          t["ps6"][0:16, 0:TTW])
            e.drain().then_inc(S["ve"], 1)
        mk.sec("vector", fve)
        mk.n["ve"] += 1

        veb2 = mk.n["ve"]
        def fsc2(e, tti=tti, veb2=veb2):
            e.wait_ge(S["ve"], veb2)
            e.activation(t["sm0"][0:16, tti * 512:(tti + 1) * 512],
                         t["sm0"][0:16, tti * 512:(tti + 1) * 512], AF.Sqrt)
            e.drain().then_inc(S["sc"], 1)
        mk.sec("scalar", fsc2)
        mk.n["sc"] += 1

        scb2 = mk.n["sc"]
        def fve2(e, tti=tti, scb2=scb2):
            e.wait_ge(S["sc"], scb2)
            ap = t["sm0"][0:16, tti * 512:(tti + 1) * 512]
            e.tensor_scalar_add(ap, ap, 1e-6)
            e.drain()
            e.reciprocal(ap, ap)
            e.drain()
            if alpha_col is not None:
                e.tensor_single_scalar(ap, ap, mk.vec(alpha_col, pn=16), OP.mult)
            e.drain().then_inc(S["ve"], 1)
        mk.sec("vector", fve2)
        mk.n["ve"] += 1


def apply_fac(mk, buf, nch, tts=(0, 1), ncols=NTOK):
    S, t = mk.S, mk.t
    for tti in range(len(tts)):
        veb, mmb = mk.n["ve"], mk.n["mm"]
        def fte(e, tti=tti, veb=veb):
            e.wait_ge(S["ve"], veb)
            for ch in range(nch):
                if ch >= 2:
                    e.wait_ge(S["ve"], veb + ch - 1)
                bank = t["ps%d" % (2 + ch % 2)]
                e.matmul(bank[:, 0:TTW], mk.hsel(ch),
                         t["sm0"][0:16, tti * 512:(tti + 1) * 512],
                         start=True, stop=True)
                e.drain().then_inc(S["mm"], 1)
        mk.sec("tensor", fte)
        mk.n["mm"] += nch

        def fve(e, tti=tti, mmb=mmb):
            for ch in range(nch):
                e.wait_ge(S["mm"], mmb + ch + 1)
                bank = t["ps%d" % (2 + ch % 2)]
                ap = mk.A(buf + ch * ncols + tti * TTW, TTW)
                e.tensor_mul(ap, ap, bank[:, 0:TTW])
                e.drain().then_inc(S["ve"], 1)
        mk.sec("vector", fve)
        mk.n["ve"] += nch


def p4_l1_ffn(mk):
    S, t = mk.S, mk.t
    sfm = 0
    stm = 1024
    yb = 2176
    srep = 18560
    sel = 19584
    cB = 20608

    def ev(e, i, j, tt, ps):
        e.activation(t["arena"][0:16, sfm + tt * TTW:sfm + (tt + 1) * TTW],
                     ps[0:16, 0:TTW], AF.Identity, bias=mk.vec(VRB1, pn=16))
    mk.proj(t["rw1"].ap(), DM, [(0, 16)], (0, 1),
            lambda kc, tt: mk.xap(kc, tt * TTW), ev, eng="scalar")
    mk.barrier()

    mmb, veb = mk.n["mm"], mk.n["ve"]
    def fte(e):
        for g in range(8):
            if g >= 2:
                e.wait_ge(S["ve"], veb + g - 1)
            bank = t["ps%d" % (2 + g % 2)]
            e.transpose(bank[0:128, 0:16],
                        t["arena"][0:16, sfm + g * 128:sfm + (g + 1) * 128],
                        mk.cst(CST_ID, 16, pn=16))
            e.drain().then_inc(S["mm"], 1)
    mk.sec("tensor", fte)
    mk.n["mm"] += 8

    def fve(e):
        for g in range(8):
            e.wait_ge(S["mm"], mmb + g + 1)
            bank = t["ps%d" % (2 + g % 2)]
            e.tensor_copy(mk.A(stm + g * 16, 16), bank[:, 0:16])
            e.drain().then_inc(S["ve"], 1)
    mk.sec("vector", fve)
    mk.n["ve"] += 8

    for g in range(8):
        veb_g = mk.n["ve"]
        scb_pre = mk.n["sc"]
        def fva(e, g=g, scb_pre=scb_pre):
            if g >= 1:
                e.wait_ge(S["sc"], scb_pre)
            e.drain()
            ap = mk.A(stm + g * 16, 16)
            e.reduce_max(t["sm2"][:, 0:1], ap, axis=AXX)
            e.drain()
            e.tensor_scalar_mul(t["sm2"][:, 1:2], t["sm2"][:, 0:1], -1.0)
            e.drain().then_inc(S["ve"], 1)
        mk.sec("vector", fva)
        mk.n["ve"] += 1
        veb_g2 = mk.n["ve"]
        scb_g = mk.n["sc"]
        def fsa(e, g=g, veb_g2=veb_g2):
            e.wait_ge(S["ve"], veb_g2)
            ap = mk.A(stm + g * 16, 16)
            e.activation(ap, ap, AF.Exp, bias=t["sm2"][:, 1:2],
                         accum_out=t["sm2"][:, 2:3])
            e.drain().then_inc(S["sc"], 1)
        mk.sec("scalar", fsa)
        mk.n["sc"] += 1
        def fvb(e, g=g, scb_g=scb_g):
            e.wait_ge(S["sc"], scb_g + 1)
            e.drain()
            ap = mk.A(stm + g * 16, 16)
            e.reciprocal(t["sm2"][:, 3:4], t["sm2"][:, 2:3])
            e.drain()
            e.tensor_single_scalar(ap, ap, t["sm2"][:, 3:4], OP.mult)
            e.drain().then_inc(S["ve"], 1)
        mk.sec("vector", fvb)
        mk.n["ve"] += 1

    sfm2 = stm + 128
    veb4 = mk.n["ve"]
    mmb4 = mk.n["mm"]
    def fteb(e):
        e.wait_ge(S["ve"], veb4)
        for g in range(8):
            if g >= 2:
                e.wait_ge(S["ve"], veb4 + g - 1)
            bank = t["ps%d" % (6 + g % 2)]
            e.transpose(bank[0:16, 0:128], mk.A(stm + g * 16, 16),
                        mk.cst(CST_ID, 128))
            e.drain().then_inc(S["mm"], 1)
    mk.sec("tensor", fteb)
    mk.n["mm"] += 8

    def fveb(e):
        for g in range(8):
            e.wait_ge(S["mm"], mmb4 + g + 1)
            bank = t["ps%d" % (6 + g % 2)]
            e.tensor_copy(mk.A(sfm2 + g * 128, 128, pn=16), bank[0:16, 0:128])
            e.drain().then_inc(S["ve"], 1)
    mk.sec("vector", fveb)
    mk.n["ve"] += 8

    veb5b = mk.n["ve"]
    def fgs(g_):
        g_.wait_ge(S["ve"], veb5b)
        g_.dma_start(out=t["agsi"].ap(), in_=mk.A(sfm2, NTOK, pn=16)).then_inc(S["dg"], 16)
    mk.sec("gpsimd", fgs)
    mk.n["dg"] += 16
    dgt = mk.n["dg"]
    def fca(g_):
        g_.wait_ge(S["dg"], dgt)
        g_.collective_compute("AllGather", OP.bypass,
                              replica_groups=[list(range(NCORE))],
                              ins=[t["agsi"].ap().opt()],
                              outs=[t["agso"].ap().opt()]).then_inc(S["cc"], 1)
    mk.sec("gpsimd", fca)
    mk.n["cc"] += 1
    ccs = mk.n["cc"]

    def evy(e, i, j, tt, ps):
        e.activation(mk.A(yb + j * NTOK + tt * TTW, TTW), ps, AF.Identity,
                     bias=mk.vec(VEB + j))
    mk.proj(t["ew"].ap(), DM, MTS16, (0, 1),
            lambda kc, tt: mk.xap(kc, tt * TTW), evy, eng="scalar")
    mk.barrier()

    dkb = mk.n["dx"]
    def fsr(e):
        e.wait_ge(S["cc"], ccs)
        e.dma_start(out=mk.A(srep, NTOK), in_=t["agso"].ap()).then_inc(S["dx"], 16)
    mk.sec("sync", fsr)
    mk.n["dx"] += 16

    def fvi(e):
        e.wait_ge(S["dx"], dkb + 16)
        e.memset(t["sm2"][0:16, 20:21], 0.0)
        e.memset(t["sm2"][0:16, 21:22], 1.0)
        e.memset(t["sm2"][0:16, 22:23], 0.5)
        e.drain().then_inc(S["ve"], 1)
    mk.sec("vector", fvi)
    mk.n["ve"] += 1

    for it in range(38):
        veb_i = mk.n["ve"]
        def fti(e, veb_i=veb_i):
            e.wait_ge(S["ve"], veb_i)
            e.matmul(t["ps7"][:, 0:1], mk.cst(CST_REPR, 128, pn=16),
                     t["sm2"][0:16, 22:23], start=True, stop=True)
            e.drain().then_inc(S["mm"], 1)
        mk.sec("tensor", fti)
        mk.n["mm"] += 1
        mmb_i = mk.n["mm"]
        def fvj(e, mmb_i=mmb_i):
            e.wait_ge(S["mm"], mmb_i)
            e.drain()
            e.tensor_copy(t["sm2"][:, 24:25], t["ps7"][:, 0:1])
            e.drain()
            e.tensor_single_scalar(mk.A(sel, NTOK), mk.A(srep, NTOK),
                                   t["sm2"][:, 24:25], OP.subtract)
            e.drain()
            e.tensor_scalar_max(mk.A(sel, NTOK), mk.A(sel, NTOK), 0.0)
            e.drain()
            e.tensor_scalar_mul(mk.A(sel, NTOK), mk.A(sel, NTOK), 1e30)
            e.drain()
            e.tensor_scalar(mk.A(sel, NTOK), mk.A(sel, NTOK), 1.0, None,
                            OP.min, OP.add, accum_out=t["sm2"][:, 25:26])
            e.drain().then_inc(S["ve"], 1)
        mk.sec("vector", fvj)
        mk.n["ve"] += 1
        veb_j = mk.n["ve"]
        def ftj(e, veb_j=veb_j):
            e.wait_ge(S["ve"], veb_j)
            e.matmul(t["ps6"][0:16, 0:1], mk.cst(CST_FOLDF, 16),
                     t["sm2"][:, 25:26], start=True, stop=True)
            e.drain().then_inc(S["mm"], 1)
        mk.sec("tensor", ftj)
        mk.n["mm"] += 1
        mmb_j = mk.n["mm"]
        def fvk(e, mmb_j=mmb_j):
            e.wait_ge(S["mm"], mmb_j)
            sm = t["sm2"]
            e.drain()
            e.tensor_scalar_sub(sm[0:16, 26:27], t["ps6"][0:16, 0:1], 511.0)
            e.drain()
            e.tensor_scalar_max(sm[0:16, 26:27], sm[0:16, 26:27], 0.0)
            e.drain()
            e.tensor_scalar_min(sm[0:16, 26:27], sm[0:16, 26:27], 1.0)
            e.drain()
            e.tensor_sub(sm[0:16, 27:28], sm[0:16, 22:23], sm[0:16, 20:21])
            e.drain()
            e.tensor_mul(sm[0:16, 27:28], sm[0:16, 27:28], sm[0:16, 26:27])
            e.drain()
            e.tensor_add(sm[0:16, 20:21], sm[0:16, 20:21], sm[0:16, 27:28])
            e.drain()
            e.tensor_sub(sm[0:16, 28:29], sm[0:16, 21:22], sm[0:16, 22:23])
            e.drain()
            e.tensor_mul(sm[0:16, 28:29], sm[0:16, 28:29], sm[0:16, 26:27])
            e.drain()
            e.tensor_add(sm[0:16, 21:22], sm[0:16, 22:23], sm[0:16, 28:29])
            e.drain()
            e.tensor_add(sm[0:16, 22:23], sm[0:16, 20:21], sm[0:16, 21:22])
            e.drain()
            e.tensor_scalar_mul(sm[0:16, 22:23], sm[0:16, 22:23], 0.5)
            e.drain().then_inc(S["ve"], 1)
        mk.sec("vector", fvk)
        mk.n["ve"] += 1

    veb6 = mk.n["ve"]
    def ftf(e):
        e.wait_ge(S["ve"], veb6)
        e.matmul(t["ps7"][:, 0:1], mk.cst(CST_REPR, 128, pn=16),
                 t["sm2"][0:16, 20:21], start=True, stop=True)
        e.drain().then_inc(S["mm"], 1)
    mk.sec("tensor", ftf)
    mk.n["mm"] += 1
    mmb_f = mk.n["mm"]
    def fvf(e):
        e.wait_ge(S["mm"], mmb_f)
        e.drain()
        e.tensor_copy(t["sm2"][:, 24:25], t["ps7"][:, 0:1])
        e.drain()
        e.tensor_single_scalar(mk.A(sel, NTOK), mk.A(srep, NTOK),
                               t["sm2"][:, 24:25], OP.subtract)
        e.drain()
        e.tensor_scalar_max(mk.A(sel, NTOK), mk.A(sel, NTOK), 0.0)
        e.drain()
        e.tensor_scalar_mul(mk.A(sel, NTOK), mk.A(sel, NTOK), 1e30)
        e.drain()
        e.tensor_scalar_min(mk.A(sel, NTOK), mk.A(sel, NTOK), 1.0)
        e.drain()
        e.tensor_mul(mk.A(sel, NTOK), mk.A(sel, NTOK), mk.A(srep, NTOK))
        e.drain().then_inc(S["ve"], 1)
    mk.sec("vector", fvf)
    mk.n["ve"] += 1

    veb7 = mk.n["ve"]
    def ftg(e):
        e.wait_ge(S["ve"], veb7)
        for tt in range(2):
            e.matmul(t["ps%d" % (6 + tt)][0:8, 0:TTW], mk.cst(CST_FOLDH, 8),
                     mk.A(sel + tt * TTW, TTW), start=True, stop=True)
            e.drain().then_inc(S["mm"], 1)
    mk.sec("tensor", ftg)
    mk.n["mm"] += 2
    mmb_g = mk.n["mm"]
    def fvg(e):
        for tt in range(2):
            e.wait_ge(S["mm"], mmb_g - 2 + tt + 1)
            e.tensor_copy(t["sm0"][0:8, tt * 512:(tt + 1) * 512],
                          t["ps%d" % (6 + tt)][0:8, 0:TTW])
        e.drain().then_inc(S["ve"], 1)
    mk.sec("vector", fvg)
    mk.n["ve"] += 1
    veb8 = mk.n["ve"]
    def fth(e):
        e.wait_ge(S["ve"], veb8)
        for tt in range(2):
            e.matmul(t["ps%d" % (2 + tt)][:, 0:TTW], mk.cst(CST_SELC, 128, pn=8),
                     t["sm0"][0:8, tt * 512:(tt + 1) * 512], start=True, stop=True)
            e.drain().then_inc(S["mm"], 1)
    mk.sec("tensor", fth)
    mk.n["mm"] += 2
    mmb_h = mk.n["mm"]
    def fvh(e):
        for tt in range(2):
            e.wait_ge(S["mm"], mmb_h - 2 + tt + 1)
            e.tensor_copy(mk.A(cB + tt * TTW, TTW), t["ps%d" % (2 + tt)][:, 0:TTW])
        e.drain().then_inc(S["ve"], 1)
    mk.sec("vector", fvh)
    mk.n["ve"] += 1

    veb9 = mk.n["ve"]
    def fvz(e):
        e.wait_ge(S["ve"], veb9)
        for kc in range(NCH):
            for tt in range(2):
                dst = mk.xap(kc, tt * TTW)
                tmp = mk.A(sel + tt * TTW, TTW)
                e.drain()
                e.tensor_mul(tmp, mk.A(yb + kc * NTOK + tt * TTW, TTW),
                             mk.A(cB + tt * TTW, TTW))
                e.drain()
                e.tensor_add(dst, dst, tmp)
        e.drain().then_inc(S["ve"], 1)
    mk.sec("vector", fvz)
    mk.n["ve"] += 1
    mk.barrier()
    ln_full(mk, sq_off=sel)



def _host_constants():
    c = {}
    c["ident"] = np.eye(128, dtype=np.float32)
    rs = np.zeros((128, NH * NH), np.float32)
    for h in range(NH):
        rs[:, h * NH + h] = 1.0
    c["rowsel"] = rs
    hs = np.zeros((NH, NH * 128), np.float32)
    for h in range(NH):
        hs[h, h * 128:(h + 1) * 128] = 1.0
    c["hsel"] = hs
    ph = np.zeros((128, 128), np.float32)
    for blk in (0, 64):
        for m in range(32):
            ph[blk + m + 32, blk + m] = -1.0
        for m in range(32, 64):
            ph[blk + m - 32, blk + m] = 1.0
    c["permh"] = ph
    pq = np.zeros((128, 128), np.float32)
    for m in range(64, 96):
        pq[m + 32, m] = -1.0
    for m in range(96, 128):
        pq[m - 32, m] = 1.0
    c["permq"] = pq
    pi = np.zeros((128, 128), np.float32)
    for m in range(0, 128, 2):
        pi[m + 1, m] = -1.0
        pi[m, m + 1] = 1.0
    c["permi"] = pi
    ff = np.zeros((128, NH), np.float32)
    fh = np.zeros((128, NCORE), np.float32)
    for p in range(128):
        ff[p, p % 16] = 1.0
        fh[p, p // 16] = 1.0
    c["foldF"] = ff
    c["foldH"] = fh
    rp = np.zeros((NH, 128), np.float32)
    for m in range(128):
        rp[m % 16, m] = 1.0
    c["repR"] = rp
    es = np.zeros((7, 7 * 128), np.float32)
    for e in range(7):
        es[e, e * 128:(e + 1) * 128] = 1.0
    c["esel"] = es
    return c


def _rope_tables(hf):
    tl = np.arange(NTOK)
    tabs = np.array([QSTARTS[hf][i // 512] + (i % 512) for i in tl], np.float64)
    out = {}
    invf0 = 1.0 / (10000.0 ** (np.arange(0, 64, 2, dtype=np.float64) / 64.0))
    ang0 = np.zeros((128, NTOK), np.float64)
    for r in range(128):
        ang0[r] = tabs * invf0[(r % 64) % 32]
    out["l0cos"] = np.cos(ang0).astype(np.float32)
    out["l0sin"] = np.sin(ang0).astype(np.float32)
    cq = np.ones((128, NTOK), np.float64)
    sq = np.zeros((128, NTOK), np.float64)
    for r in range(64, 128):
        dd = r - 64
        cq[r] = np.cos(tabs * invf0[dd % 32])
        sq[r] = np.sin(tabs * invf0[dd % 32])
    out["l0cosq"] = cq.astype(np.float32)
    out["l0sinq"] = sq.astype(np.float32)
    invf1 = 1.0 / (10000.0 ** (np.arange(0, 128, 2, dtype=np.float64) / 128.0))
    ang1 = np.zeros((128, NTOK), np.float64)
    for r in range(128):
        ang1[r] = tabs * invf1[r // 2]
    out["l1cos"] = np.cos(ang1).astype(np.float32)
    out["l1sin"] = np.sin(ang1).astype(np.float32)
    return out


def _masks(hf):
    m = np.zeros((6, 128, 4, TTW), np.float32)
    for s, (qc, kb) in enumerate(SLOTS):
        q0 = QSTARTS[hf][qc]
        for kc in range(4):
            k0 = kb * 512 + kc * 128
            kk = k0 + np.arange(128)[:, None]
            qq = q0 + np.arange(TTW)[None, :]
            m[s, :, kc, :] = (qq >= kk).astype(np.float32)
    return m.reshape(6 * 128, 4 * TTW)


def _vec_pd(v, nch):
    return np.ascontiguousarray(np.asarray(v).reshape(nch, 128).T.astype(np.float32))


def kernel(**inputs):
    ii = {k: np.asarray(v) for k, v in inputs.items()}
    x = ii["L0_attn_hidden_states"]

    shared = {}
    shared["wq0"] = np.ascontiguousarray(ii["L0_attn_Wq"].T)
    shared["wk0"] = np.ascontiguousarray(ii["L0_attn_Wk"].T)
    shared["wv0"] = np.ascontiguousarray(ii["L0_attn_Wv"].T)
    shared["wo0"] = np.ascontiguousarray(ii["L0_attn_Wo"].T)
    shared["kvdw"] = ii["L0_ffn_kv_d_w"]
    shared["qdw"] = ii["L0_ffn_q_d_w"]
    qu = ii["L0_ffn_q_u_w"]; rq = ii["L0_ffn_rope_q_w"]
    wqasm = np.zeros((512, DM), np.float32)
    for h in range(NH):
        wqasm[:, h * 128:h * 128 + 64] = qu[:, h * 64:(h + 1) * 64]
        wqasm[:, h * 128 + 64:h * 128 + 128] = rq[:, h * 64:(h + 1) * 64]
    shared["wqasm"] = wqasm
    shared["kuw"] = ii["L0_ffn_k_u_w"]
    shared["ropekw"] = ii["L0_ffn_rope_k_w"]
    shared["vuw"] = ii["L0_ffn_v_u_w"]
    shared["ow"] = ii["L0_ffn_o_w"]
    shared["shg"] = ii["L0_ffn_sh_gate"]
    shared["shu"] = ii["L0_ffn_sh_up"]
    shared["shd"] = ii["L0_ffn_sh_down"]
    shared["rg"] = ii["L0_ffn_r_gate"].reshape(7 * DM, 1024)
    shared["ru"] = ii["L0_ffn_r_up"].reshape(7 * DM, 1024)
    shared["rd"] = ii["L0_ffn_r_down"].reshape(7 * 1024, DM)
    shared["rw0"] = ii["L0_ffn_router_w"]
    shared["wq1"] = ii["L1_attn_Wq"]
    shared["wk1"] = ii["L1_attn_Wk"]
    shared["wv1"] = ii["L1_attn_Wv"]
    shared["wo1"] = ii["L1_attn_Wo"]
    shared["rw1"] = ii["L1_ffn_router_w"]
    shared["ew"] = ii["L1_ffn_expert_w"]
    vecs = np.zeros((128, VNVEC), np.float32)
    vecs[:, VGAM] = ii["L0_attn_gamma"]
    vecs[:, VBET] = ii["L0_attn_beta"]
    vecs[:, VL0W1:VL0W1 + NCH] = _vec_pd(ii["L0_ffn_ln1_w"], NCH)
    vecs[:, VL0W2:VL0W2 + NCH] = _vec_pd(ii["L0_ffn_ln2_w"], NCH)
    vecs[:, VBQ1:VBQ1 + NCH] = _vec_pd(ii["L1_attn_bq"], NCH)
    vecs[:, VBK1:VBK1 + 4] = _vec_pd(ii["L1_attn_bk"], 4)
    vecs[:, VBV1:VBV1 + 4] = _vec_pd(ii["L1_attn_bv"], 4)
    vecs[:, VBO1:VBO1 + NCH] = _vec_pd(ii["L1_attn_bo"], NCH)
    vecs[:, VEB:VEB + NCH] = _vec_pd(ii["L1_ffn_expert_b"], NCH)
    vecs[0:NH, VALP] = ii["L1_attn_alpha"]
    vecs[0:7, VRB0] = ii["L0_ffn_router_b"]
    vecs[0:NH, VRB1] = ii["L1_ffn_router_b"]
    shared["vecs"] = vecs
    shared.update(_host_constants())
    shared = {k: np.ascontiguousarray(v, dtype=np.float32) for k, v in shared.items()}

    in_maps = []
    for c in range(NCORE):
        b, hf = c // 2, c % 2
        im = dict(shared)
        im["xs"] = np.ascontiguousarray(
            np.concatenate([x[b, s:s + 512] for s in QSTARTS[hf]], axis=0),
            dtype=np.float32)
        im.update({k: np.ascontiguousarray(v) for k, v in _rope_tables(hf).items()})
        im["masks"] = _masks(hf)
        sc = np.zeros((NCORE, 128), np.float32)
        sc[c, :] = 1.0
        im["selc"] = sc
        in_maps.append(im)

    nc = build()
    res = run_bass_kernel_spmd(nc, in_maps, core_ids=list(range(NCORE)))
    out = np.zeros((4, T, DM), np.float32)
    for c in range(NCORE):
        b, hf = c // 2, c % 2
        y = np.asarray(res.results[c]["yout"])
        for j, s in enumerate(QSTARTS[hf]):
            out[b, s:s + 512] = y[j * 512:(j + 1) * 512]
    return out


# version 07 (final)
# speedup vs baseline: 1.0687x; 1.0167x over previous
import numpy as np
from contextlib import ExitStack
import concourse.bass as bass
import concourse.mybir as mybir
from concourse.bass_utils import run_bass_kernel_spmd

F32 = mybir.dt.float32
BF16 = mybir.dt.bfloat16
AF = mybir.ActivationFunctionType
OP = mybir.AluOpType
AXX = mybir.AxisListType.X

NCORE = 8
T = 2048
DM = 2048
NCH = 16
NTOK = 1024
TTW = 512
QSTARTS = [[0, 1536], [512, 1024]]
NKB = [2, 4]
KBSRC = [(0, 0), (1, 0), (1, 512), (0, 512)]
SLOTS = [(0, 0), (0, 1), (1, 0), (1, 1), (1, 2), (1, 3)]
NH = 16
ARENA = 26624

DBG = [None]

CST_ID = 0
CST_PERMH = 128
CST_PERMQ = 256
CST_PERMI = 384
CST_ROWSEL = 512
CST_ONES = 768
CST_FOLDF = 776
CST_FOLDH = 792
CST_SELC = 800
CST_ESEL = 928
CST_VECS = 1824
CST_REPR = 1920
CST_END = 2048

VGAM, VBET = 0, 1
VL0W1, VL0W2 = 2, 18
VBQ1, VBK1, VBV1, VBO1 = 34, 50, 54, 58
VEB = 74
VALP, VRB0, VRB1 = 90, 91, 92
VNVEC = 96

ATT_QTAB = 0
ATT_QB = 2048
ATT_OB = 10240
ATT_EB = 18432
ATT_MB = 22528
ATT_KV = 24576

MTS16 = [(i * 128, 128) for i in range(16)]
MTS8 = [(i * 128, 128) for i in range(8)]
MTS4 = [(i * 128, 128) for i in range(4)]


def rr(ap, s, **kw):
    return ap.rearrange(s, **kw)


class _EPState:
    __slots__ = ("e", "last", "pend", "ninc")

    def __init__(self):
        self.e = None
        self.last = None
        self.pend = False
        self.ninc = 0


class _InsWrap:
    def __init__(self, ins, st):
        self._ins = ins
        self._st = st

    def then_inc(self, sem, n):
        self._st.ninc += 1
        return self._ins.then_inc(sem, n)

    def __getattr__(self, name):
        return getattr(self._ins, name)


class _DrainShim:
    def __init__(self, st):
        self._st = st

    def then_inc(self, sem, n):
        st = self._st
        if st.pend:
            st.pend = False
            if st.last is not None and st.ninc == 0:
                st.ninc = 1
                return st.last.then_inc(sem, n)
            ins = st.e.drain()
            st.last = ins
            st.ninc = 1
            return ins.then_inc(sem, n)
        return st.e.drain().then_inc(sem, n)


class EW:
    def __init__(self, st):
        object.__setattr__(self, "_st", st)

    def _flush(self):
        st = self._st
        if st.pend:
            st.pend = False
            st.e.drain()

    def wait_ge(self, *a, **k):
        self._flush()
        self._st.last = None
        return self._st.e.wait_ge(*a, **k)

    def drain(self):
        self._flush()
        self._st.pend = True
        return _DrainShim(self._st)

    def __getattr__(self, name):
        st = self._st
        f = getattr(st.e, name)
        if not callable(f):
            return f

        def g(*a, **k):
            self._flush()
            ins = f(*a, **k)
            if ins is not None and hasattr(ins, "then_inc"):
                st.last = ins
                st.ninc = 0
                return _InsWrap(ins, st)
            return ins
        return g


class MK:
    def __init__(self, nc, block, tensors, sems):
        self.nc = nc
        self.block = block
        self.t = tensors
        self.S = sems
        self.n = {k: 0 for k in sems}
        self.nbar = 0
        self.wb_last = [0, 0]
        self.ps01_last = [None, None]
        self._ep = {}

    def sec(self, eng, fn):
        st = self._ep.setdefault(eng, _EPState())

        def wrapped(e, st=st, fn=fn):
            st.e = e
            fn(EW(st))
        getattr(self.block, eng)(wrapped)

    def barrier(self):
        self.nbar += 1
        tgt = self.nbar
        engs = ("tensor", "vector", "scalar", "gpsimd", "sync")
        sync_sems = ("dwa", "dwb", "dka", "dkb", "dva", "dvb", "dm", "dx", "dk")
        for eng in engs:
            def f(e, eng=eng):
                if eng == "sync":
                    for sem in sync_sems:
                        if self.n[sem]:
                            e.wait_ge(self.S[sem], self.n[sem])
                    e.nop().then_inc(self.S["bar_sync"], 1)
                elif eng == "gpsimd":
                    if self.n["dg"]:
                        e.wait_ge(self.S["dg"], self.n["dg"])
                    e.drain().then_inc(self.S["bar_gpsimd"], 1)
                else:
                    e.drain().then_inc(self.S["bar_" + eng], 1)
            self.sec(eng, f)
        for eng in engs:
            def f2(e, tgt=tgt):
                for other in engs:
                    e.wait_ge(self.S["bar_" + other], tgt)
            self.sec(eng, f2)

    def A(self, off, cols, p0=0, pn=128):
        return self.t["arena"][p0:p0 + pn, off:off + cols]

    def xap(self, kc, c0, w=TTW):
        return self.t["x"][:, kc * NTOK + c0: kc * NTOK + c0 + w]

    def cst(self, off, cols, p0=0, pn=128):
        return self.t["cst"][p0:p0 + pn, off:off + cols]

    def vec(self, col, pn=128):
        return self.t["cst"][0:pn, CST_VECS + col:CST_VECS + col + 1]

    def hsel(self, h):
        return self.t["sm1"][0:16, h * 128:(h + 1) * 128]

    def onesrow(self):
        return self.t["sm1"][0:1, 0:128]

    def rowsel(self, h):
        return self.cst(CST_ROWSEL + h * NH, NH)

    def wpanel_dst(self, slot, nkc, w):
        return self.t["wb"][:, slot * 2048: slot * 2048 + nkc * w]

    def wtile(self, slot, kc, w):
        return self.t["wb"][:, slot * 2048 + kc * w: slot * 2048 + (kc + 1) * w]

    def proj(self, W_ap, Kdim, mts, tts, rhs_fn, evac, eng="vector",
             dma_out=None, dma_per_tile=0):
        nkc = Kdim // 128
        ntl = len(mts) * len(tts)
        mmb = self.n["mm"]
        lane = ("dwa", "dwb")
        lb = [self.n["dwa"], self.n["dwb"]]
        evs = "ve" if eng == "vector" else "sc"
        evb = self.n[evs]
        dgb = self.n["dg"]
        self._gp_snapshot = self.n["gp"]
        S = self.S
        Wr = rr(W_ap, "(kc p) m -> p kc m", p=128)

        def fsync(e):
            for j, (c0, w) in enumerate(mts):
                if j >= 2:
                    e.wait_ge(S["mm"], mmb + (j - 1) * len(tts))
                else:
                    if self.wb_last[j % 2]:
                        e.wait_ge(S["mm"], self.wb_last[j % 2])
                e.dma_start(out=self.wpanel_dst(j % 2, nkc, w),
                            in_=Wr[:, :, c0:c0 + w]).then_inc(S[lane[j % 2]], 16)
        self.sec("sync", fsync)
        self.n["dwa"] += 16 * ((len(mts) + 1) // 2)
        self.n["dwb"] += 16 * (len(mts) // 2)
        for sslot in range(2):
            js = [j for j in range(len(mts)) if j % 2 == sslot]
            if js:
                self.wb_last[sslot] = mmb + (js[-1] + 1) * len(tts)

        def ftensor(te):
            i = 0
            for j, (c0, w) in enumerate(mts):
                te.wait_ge(S[lane[j % 2]], lb[j % 2] + 16 * (j // 2 + 1))
                for tt in tts:
                    if i >= 2:
                        te.wait_ge(S[evs], evb + i - 1)
                    elif self.ps01_last[i % 2] is not None:
                        te.wait_ge(S[self.ps01_last[i % 2][0]],
                                   self.ps01_last[i % 2][1])
                    ps = self.t["ps%d" % (i % 2)]
                    ins = None
                    for kc in range(nkc):
                        ins = te.matmul(ps[0:w, 0:TTW],
                                        self.wtile(j % 2, kc, w),
                                        rhs_fn(kc, tt),
                                        start=(kc == 0), stop=(kc == nkc - 1))
                    ins.then_inc(S["mm"], 1)
                    i += 1
        self.sec("tensor", ftensor)
        self.n["mm"] += ntl

        def fev(e):
            i = 0
            for j, (c0, w) in enumerate(mts):
                for tt in tts:
                    e.wait_ge(S["mm"], mmb + i + 1)
                    if dma_out is not None and i >= 2:
                        e.wait_ge(S["gp"], self._gp_snapshot + i - 1)
                    evac(e, i, j, tt, self.t["ps%d" % (i % 2)][0:w, 0:TTW])
                    e.drain().then_inc(S[evs], 1)
                    i += 1
        self.sec(eng, fev)
        self.n[evs] += ntl

        for par in range(2):
            iis = [ii2 for ii2 in range(ntl) if ii2 % 2 == par]
            if iis:
                self.ps01_last[par] = (evs, evb + iis[-1] + 1)
        if dma_out is not None:
            gpb = self.n["gp"]
            def fg(g):
                i = 0
                for j, (c0, w) in enumerate(mts):
                    for tt in tts:
                        g.wait_ge(S[evs], evb + i + 1)
                        dma_out(g, i, j, tt)
                        g.wait_ge(S["dg"], dgb + 16 * dma_per_tile * (i + 1))
                        g.nop().then_inc(S["gp"], 1)
                        i += 1
            self.sec("gpsimd", fg)
            self.n["dg"] += 16 * dma_per_tile * ntl
            self.n["gp"] += ntl
        return ntl



def build():
    nc = bass.Bass("TRN2", target_bir_lowering=False, debug=False)
    d = {}

    def din(name, shape):
        d[name] = nc.dram_tensor(name, list(shape), F32, kind="ExternalInput")

    din("xs", (NTOK, DM))
    din("wq0", (DM, DM)); din("wk0", (DM, DM)); din("wv0", (DM, DM)); din("wo0", (DM, DM))
    din("kvdw", (DM, 512)); din("qdw", (DM, 512))
    din("wqasm", (512, DM))
    din("kuw", (512, 1024)); din("ropekw", (DM, 1024)); din("vuw", (512, DM))
    din("ow", (DM, DM))
    din("shg", (DM, 1024)); din("shu", (DM, 1024)); din("shd", (1024, DM))
    din("rg", (7 * DM, 1024)); din("ru", (7 * DM, 1024)); din("rd", (7 * 1024, DM))
    din("rw0", (DM, 7))
    din("wq1", (DM, DM)); din("wk1", (DM, 512)); din("wv1", (DM, 512)); din("wo1", (DM, DM))
    din("rw1", (DM, NH)); din("ew", (DM, DM))
    din("ident", (128, 128))
    din("rowsel", (128, NH * NH))
    din("hsel", (NH, NH * 128))
    din("permh", (128, 128)); din("permq", (128, 128)); din("permi", (128, 128))
    din("l0cos", (128, NTOK)); din("l0sin", (128, NTOK))
    din("l0cosq", (128, NTOK)); din("l0sinq", (128, NTOK))
    din("l1cos", (128, NTOK)); din("l1sin", (128, NTOK))
    din("masks", (6 * 128, 4 * TTW))
    din("foldF", (128, NH)); din("foldH", (128, NCORE)); din("repR", (NH, 128))
    din("selc", (NCORE, 128))
    din("esel", (7, 7 * 128))
    din("vecs", (128, VNVEC))

    d["yout"] = nc.dram_tensor("yout", [NTOK, DM], F32, kind="ExternalOutput")
    d["stash"] = nc.dram_tensor("stash", [128, NCH * NTOK], F32)
    d["qdsp"] = nc.dram_tensor("qdsp", [128, 4 * NTOK], F32)
    for tag, kd in (("a1", DM), ("a2", DM), ("a3", 512)):
        d["agki_" + tag] = nc.dram_tensor("agki_" + tag, [kd, NTOK], F32)
        d["agko_" + tag] = nc.dram_tensor("agko_" + tag, [2 * kd, NTOK], F32)
        d["agvi_" + tag] = nc.dram_tensor("agvi_" + tag, [NTOK, kd], F32)
        d["agvo_" + tag] = nc.dram_tensor("agvo_" + tag, [2 * NTOK, kd], F32)
    d["agsi"] = nc.dram_tensor("agsi", [NH, NTOK], F32)
    d["agso"] = nc.dram_tensor("agso", [NCORE * NH, NTOK], F32)

    with ExitStack() as st:
        tn = {}
        def sb(name, cols):
            tn[name] = st.enter_context(nc.sbuf_tensor(name, [128, cols], F32))
        sb("x", NCH * NTOK)
        sb("arena", ARENA)
        sb("wb", 2 * 2048)
        sb("cst", CST_END)
        sb("sm0", 1536)
        sb("sm1", 2048)
        sb("sm2", 64)
        for i in range(8):
            tn["ps%d" % i] = st.enter_context(
                nc.psum_tensor("ps%d" % i, [128, 512], F32))
        sems = {}
        for s in ("dwa", "dwb", "dka", "dkb", "dva", "dvb", "dm", "dx",
                  "dk", "dg", "gp", "cc", "mm", "sc", "ve",
                  "bar_tensor", "bar_vector", "bar_scalar", "bar_gpsimd",
                  "bar_sync"):
            sems[s] = st.enter_context(nc.semaphore(s))
        with nc.Block() as block:
            tensors = dict(tn)
            tensors.update(d)
            mk = MK(nc, block, tensors, sems)
            emit_all(mk)
    return nc


def emit_all(mk):
    p0_load(mk)
    if DBG[0] == "p0":
        return emit_output(mk)
    p1_l0_attn(mk)
    if DBG[0] == "p1":
        return emit_output(mk)
    p2_l0_ffn(mk)
    if DBG[0] == "p2":
        return emit_output(mk)
    p3_l1_attn(mk)
    if DBG[0] == "p3":
        return emit_output(mk)
    p4_l1_ffn(mk)
    emit_output(mk)


def p0_load(mk):
    S, t = mk.S, mk.t
    dkb = mk.n["dk"]

    def fsync(e):
        for dst_off, src, pn, cols in (
                (CST_ID, "ident", 128, 128), (CST_PERMH, "permh", 128, 128),
                (CST_PERMQ, "permq", 128, 128), (CST_PERMI, "permi", 128, 128),
                (CST_ROWSEL, "rowsel", 128, 256), (CST_FOLDF, "foldF", 128, 16),
                (CST_FOLDH, "foldH", 128, 8), (CST_SELC, "selc", 8, 128),
                (CST_ESEL, "esel", 7, 896), (CST_VECS, "vecs", 128, VNVEC),
                (CST_REPR, "repR", 16, 128)):
            e.dma_start(out=t["cst"][0:pn, dst_off:dst_off + cols],
                        in_=t[src].ap()).then_inc(S["dk"], 16)
        e.dma_start(out=t["sm1"][0:16, 0:2048], in_=t["hsel"].ap()).then_inc(S["dk"], 16)
    mk.sec("sync", fsync)
    mk.n["dk"] += 16 * 12

    def fvec(e):
        e.wait_ge(S["dk"], dkb + 16 * 12)
        e.memset(t["cst"][:, CST_ONES:CST_ONES + 8], 1.0)
        e.drain().then_inc(S["ve"], 1)
    mk.sec("vector", fvec)
    mk.n["ve"] += 1

    stg = 0
    dkb2 = mk.n["dx"]
    mmb = mk.n["mm"]
    veb = mk.n["ve"]
    xr = rr(t["x"].ap(), "p (c t) -> p c t", c=NCH)

    def fsync2(e):
        for g in range(8):
            if g >= 1:
                e.wait_ge(S["mm"], mmb + g * NCH)
            e.dma_start(out=mk.A(stg, 2048),
                        in_=t["xs"].ap()[g * 128:(g + 1) * 128, :]).then_inc(S["dx"], 16)
    mk.sec("sync", fsync2)
    mk.n["dx"] += 16 * 8

    def ftens(e):
        e.wait_ge(S["ve"], veb)
        for g in range(8):
            e.wait_ge(S["dx"], dkb2 + 16 * (g + 1))
            if g >= 1:
                e.wait_ge(S["ve"], veb + 4 * g)
            for dc in range(NCH):
                bank = t["ps%d" % (dc // 4)]
                e.transpose(bank[:, (dc % 4) * 128:(dc % 4) * 128 + 128],
                            mk.A(stg + dc * 128, 128),
                            mk.cst(CST_ID, 128))
            e.drain().then_inc(S["mm"], NCH)
    mk.sec("tensor", ftens)
    mk.n["mm"] += 8 * NCH

    def fvec2(e):
        for g in range(8):
            e.wait_ge(S["mm"], mmb + (g + 1) * NCH)
            for bk in range(4):
                dst = xr[:, bk * 4:(bk + 1) * 4, g * 128:(g + 1) * 128]
                src = rr(t["ps%d" % bk].ap()[:, 0:512], "p (b r) -> p b r", b=4)
                e.tensor_copy(dst, src)
            e.drain().then_inc(S["ve"], 4)
    mk.sec("vector", fvec2)
    mk.n["ve"] += 8 * 4
    mk.barrier()


def emit_output(mk):
    mk.barrier()
    S, t = mk.S, mk.t
    stg = 0
    mmb, veb, dgb = mk.n["mm"], mk.n["ve"], mk.n["dg"]

    def ftens(e):
        for g in range(8):
            if g >= 1:
                e.wait_ge(S["ve"], veb + 4 * g)
            for dc in range(NCH):
                bank = t["ps%d" % (dc // 4)]
                e.transpose(bank[:, (dc % 4) * 128:(dc % 4) * 128 + 128],
                            mk.xap(dc, g * 128, 128),
                            mk.cst(CST_ID, 128))
            e.drain().then_inc(S["mm"], NCH)
    mk.sec("tensor", ftens)
    mk.n["mm"] += 8 * NCH

    gpb = mk.n["gp"]
    def fvec(e):
        for g in range(8):
            e.wait_ge(S["mm"], mmb + (g + 1) * NCH)
            if g >= 1:
                e.wait_ge(S["gp"], gpb + g)
            for bk in range(4):
                e.tensor_copy(mk.A(stg + bk * 512, 512), t["ps%d" % bk][:, 0:512])
            e.drain().then_inc(S["ve"], 4)
    mk.sec("vector", fvec)
    mk.n["ve"] += 8 * 4

    def fg(g_):
        for g in range(8):
            g_.wait_ge(S["ve"], veb + 4 * (g + 1))
            g_.dma_start(out=t["yout"].ap()[g * 128:(g + 1) * 128, :],
                         in_=mk.A(stg, 2048)).then_inc(S["dg"], 16)
            g_.wait_ge(S["dg"], dgb + 16 * (g + 1))
            g_.nop().then_inc(S["gp"], 1)
    mk.sec("gpsimd", fg)
    mk.n["dg"] += 16 * 8
    mk.n["gp"] += 8


def ln_full(mk, sq_off, rms_wcol=None, dst_off=None, add_off=None):
    S, t = mk.S, mk.t
    is_ln = rms_wcol is None
    ve_prev_apply = None
    for tt in range(2):
        if add_off is not None:
            veb0 = mk.n["ve"]
            def fadd(e, tt=tt):
                for kc in range(NCH):
                    dst = mk.xap(kc, tt * TTW)
                    e.tensor_add(dst, dst, mk.A(add_off + kc * NTOK + tt * TTW, TTW))
                e.drain().then_inc(S["ve"], 1)
            mk.sec("vector", fadd)
            mk.n["ve"] += 1

        mmb, scb, veb = mk.n["mm"], mk.n["sc"], mk.n["ve"]

        def fsc(e, tt=tt, veb=veb):
            if add_off is not None:
                e.wait_ge(S["ve"], veb)
            for kc in range(NCH):
                if kc >= 1:
                    e.wait_ge(S["mm"], mmb + kc)
                e.activation(mk.A(sq_off, 512), mk.xap(kc, tt * TTW), AF.Square)
                e.drain().then_inc(S["sc"], 1)
        mk.sec("scalar", fsc)
        mk.n["sc"] += NCH

        def fte(e, tt=tt, ve_prev_apply=ve_prev_apply):
            if ve_prev_apply is not None:
                e.wait_ge(S["ve"], ve_prev_apply)
            for kc in range(NCH):
                e.wait_ge(S["sc"], scb + kc + 1)
                if is_ln:
                    e.matmul(t["ps6"][0:1, 0:TTW], mk.cst(CST_ONES, 1),
                             mk.xap(kc, tt * TTW),
                             start=(kc == 0), stop=(kc == NCH - 1))
                e.matmul(t["ps7"][0:1, 0:TTW], mk.cst(CST_ONES, 1),
                         mk.A(sq_off, 512),
                         start=(kc == 0), stop=(kc == NCH - 1))
                e.drain().then_inc(S["mm"], 1)
        mk.sec("tensor", fte)
        mk.n["mm"] += NCH

        mmb2 = mk.n["mm"]
        def fve(e, mmb2=mmb2):
            e.wait_ge(S["mm"], mmb2)
            if is_ln:
                e.tensor_scalar_mul(t["sm0"][0:1, 0:512], t["ps6"][0:1, 0:TTW], 1.0 / DM)
            e.tensor_scalar_mul(t["sm0"][0:1, 512:1024], t["ps7"][0:1, 0:TTW], 1.0 / DM)
            e.drain()
            if is_ln:
                e.tensor_mul(t["sm0"][0:1, 1024:1536], t["sm0"][0:1, 0:512],
                             t["sm0"][0:1, 0:512])
                e.drain()
                e.tensor_sub(t["sm0"][0:1, 512:1024], t["sm0"][0:1, 512:1024],
                             t["sm0"][0:1, 1024:1536])
                e.drain()
            e.tensor_scalar_add(t["sm0"][0:1, 512:1024], t["sm0"][0:1, 512:1024], 1e-5)
            e.drain()
            e.reciprocal(t["sm0"][0:1, 512:1024], t["sm0"][0:1, 512:1024])
            e.drain().then_inc(S["ve"], 1)
        mk.sec("vector", fve)
        mk.n["ve"] += 1

        veb2 = mk.n["ve"]
        def fsc2(e, veb2=veb2):
            e.wait_ge(S["ve"], veb2)
            e.activation(t["sm0"][0:1, 512:1024], t["sm0"][0:1, 512:1024], AF.Sqrt)
            e.drain().then_inc(S["sc"], 1)
        mk.sec("scalar", fsc2)
        mk.n["sc"] += 1

        scb2 = mk.n["sc"]
        if is_ln:
            def fve2(e, scb2=scb2):
                e.wait_ge(S["sc"], scb2)
                e.tensor_mul(t["sm0"][0:1, 1024:1536], t["sm0"][0:1, 0:512],
                             t["sm0"][0:1, 512:1024])
                e.drain()
                e.tensor_scalar_mul(t["sm0"][0:1, 1024:1536],
                                    t["sm0"][0:1, 1024:1536], -1.0)
                e.drain().then_inc(S["ve"], 1)
            mk.sec("vector", fve2)
            mk.n["ve"] += 1
        veb3 = mk.n["ve"]

        def fte2(e, scb2=scb2, veb3=veb3):
            e.wait_ge(S["sc"], scb2)
            e.wait_ge(S["ve"], veb3)
            e.matmul(t["ps4"][:, 0:TTW], mk.onesrow(), t["sm0"][0:1, 512:1024],
                     start=True, stop=True)
            if is_ln:
                e.matmul(t["ps5"][:, 0:TTW], mk.onesrow(), t["sm0"][0:1, 1024:1536],
                         start=True, stop=True)
            e.drain().then_inc(S["mm"], 1)
        mk.sec("tensor", fte2)
        mk.n["mm"] += 1

        mmb4 = mk.n["mm"]
        def fve3(e, tt=tt, mmb4=mmb4):
            e.wait_ge(S["mm"], mmb4)
            for kc in range(NCH):
                src = mk.xap(kc, tt * TTW)
                dst = src if dst_off is None else mk.A(dst_off + kc * NTOK + tt * TTW, TTW)
                e.tensor_mul(dst, src, t["ps4"][:, 0:TTW])
                if is_ln:
                    e.drain()
                    e.tensor_add(dst, dst, t["ps5"][:, 0:TTW])
            e.drain().then_inc(S["ve"], 1)
        mk.sec("vector", fve3)
        mk.n["ve"] += 1
        ve_prev_apply = mk.n["ve"]

        if not is_ln:
            veb5 = mk.n["ve"]
            def fsc3(e, tt=tt, veb5=veb5):
                e.wait_ge(S["ve"], veb5)
                for kc in range(NCH):
                    dst = mk.A(dst_off + kc * NTOK + tt * TTW, TTW)
                    e.activation(dst, dst, AF.Identity, scale=mk.vec(rms_wcol + kc))
                e.drain().then_inc(S["sc"], 1)
            mk.sec("scalar", fsc3)
            mk.n["sc"] += 1
    mk.barrier()


def fm_ln(mk, buf_off, tts, sq_off, extra_scale=1.0, dma_to=None, ncols=NTOK):
    S, t = mk.S, mk.t
    ntts = len(tts)
    ve_apply = {}
    for h in range(NH):
        for tti, tt in enumerate(tts):
            i = h * ntts + tti
            src = mk.A(buf_off + h * ncols + tti * TTW, TTW)
            stat = [t["ps6"], t["ps7"]] if i % 2 == 0 else [t["ps2"], t["ps3"]]
            bcA = t["ps4"] if i % 2 == 0 else t["ps5"]
            bcB = stat[0]
            scb, mmb, veb = mk.n["sc"], mk.n["mm"], mk.n["ve"]

            def fsc(e, src=src, i=i, mmb=mmb):
                if i >= 1:
                    e.wait_ge(S["mm"], mmb)
                e.activation(mk.A(sq_off, 512), src, AF.Square)
                e.drain().then_inc(S["sc"], 1)
            mk.sec("scalar", fsc)
            mk.n["sc"] += 1

            def fte(e, src=src, stat=stat, scb=scb, i=i):
                e.wait_ge(S["sc"], scb + 1)
                if i >= 2 and (i - 2) in ve_apply:
                    e.wait_ge(S["ve"], ve_apply[i - 2])
                e.matmul(stat[0][0:1, 0:TTW], mk.cst(CST_ONES, 1), src,
                         start=True, stop=True)
                e.matmul(stat[1][0:1, 0:TTW], mk.cst(CST_ONES, 1), mk.A(sq_off, 512),
                         start=True, stop=True)
                e.drain().then_inc(S["mm"], 1)
            mk.sec("tensor", fte)
            mk.n["mm"] += 1

            mmb2 = mk.n["mm"]
            def fve(e, stat=stat, mmb2=mmb2):
                e.wait_ge(S["mm"], mmb2)
                e.tensor_scalar_mul(t["sm0"][0:1, 0:512], stat[0][0:1, 0:TTW], 1.0 / 128)
                e.tensor_scalar_mul(t["sm0"][0:1, 512:1024], stat[1][0:1, 0:TTW], 1.0 / 128)
                e.drain()
                e.tensor_mul(t["sm0"][0:1, 1024:1536], t["sm0"][0:1, 0:512],
                             t["sm0"][0:1, 0:512])
                e.drain()
                e.tensor_sub(t["sm0"][0:1, 512:1024], t["sm0"][0:1, 512:1024],
                             t["sm0"][0:1, 1024:1536])
                e.drain()
                e.tensor_scalar_add(t["sm0"][0:1, 512:1024], t["sm0"][0:1, 512:1024], 1e-5)
                e.drain()
                e.reciprocal(t["sm0"][0:1, 512:1024], t["sm0"][0:1, 512:1024])
                e.drain().then_inc(S["ve"], 1)
            mk.sec("vector", fve)
            mk.n["ve"] += 1

            veb2 = mk.n["ve"]
            def fsc2(e, veb2=veb2):
                e.wait_ge(S["ve"], veb2)
                e.activation(t["sm0"][0:1, 512:1024], t["sm0"][0:1, 512:1024], AF.Sqrt)
                e.drain().then_inc(S["sc"], 1)
            mk.sec("scalar", fsc2)
            mk.n["sc"] += 1

            scb2 = mk.n["sc"]
            def fve2(e, scb2=scb2):
                e.wait_ge(S["sc"], scb2)
                if extra_scale != 1.0:
                    e.tensor_scalar_mul(t["sm0"][0:1, 512:1024],
                                        t["sm0"][0:1, 512:1024], float(extra_scale))
                    e.drain()
                e.tensor_mul(t["sm0"][0:1, 1024:1536], t["sm0"][0:1, 0:512],
                             t["sm0"][0:1, 512:1024])
                e.drain()
                e.tensor_scalar_mul(t["sm0"][0:1, 1024:1536],
                                    t["sm0"][0:1, 1024:1536], -1.0)
                e.drain().then_inc(S["ve"], 1)
            mk.sec("vector", fve2)
            mk.n["ve"] += 1

            veb3 = mk.n["ve"]
            def fte2(e, bcA=bcA, bcB=bcB, veb3=veb3):
                e.wait_ge(S["ve"], veb3)
                e.matmul(bcA[:, 0:TTW], mk.onesrow(), t["sm0"][0:1, 512:1024],
                         start=True, stop=True)
                e.matmul(bcB[:, 0:TTW], mk.onesrow(), t["sm0"][0:1, 1024:1536],
                         start=True, stop=True)
                e.drain().then_inc(S["mm"], 1)
            mk.sec("tensor", fte2)
            mk.n["mm"] += 1

            mmb4 = mk.n["mm"]
            def fve3(e, src=src, bcA=bcA, bcB=bcB, mmb4=mmb4):
                e.wait_ge(S["mm"], mmb4)
                e.tensor_mul(src, src, bcA[:, 0:TTW])
                e.drain()
                e.tensor_add(src, src, bcB[:, 0:TTW])
                e.drain().then_inc(S["ve"], 1)
            mk.sec("vector", fve3)
            mk.n["ve"] += 1
            ve_apply[i] = mk.n["ve"]

            if dma_to is not None:
                veb4 = mk.n["ve"]
                def fgd(g, src=src, h=h, tt=tt, veb4=veb4):
                    g.wait_ge(S["ve"], veb4)
                    g.dma_start(out=dma_to.ap()[h * 128:(h + 1) * 128,
                                                tt * TTW:(tt + 1) * TTW],
                                in_=src).then_inc(S["dg"], 16)
                mk.sec("gpsimd", fgd)
                mk.n["dg"] += 16


def rope_buf(mk, buf_off, chunks, tts, perm_off, cos_off, sin_off, tmp_off,
             ncols=NTOK):
    S, t = mk.S, mk.t
    veb0 = mk.n["ve"]
    mmb0 = mk.n["mm"]
    units = [(ch, tti, tt) for ch in chunks for tti, tt in enumerate(tts)]

    def fte(e):
        for i, (ch, tti, tt) in enumerate(units):
            if i >= 2:
                e.wait_ge(S["ve"], veb0 + 2 * (i - 1))
            else:
                e.wait_ge(S["ve"], veb0)
            src = mk.A(buf_off + ch * ncols + tti * TTW, TTW)
            bank = t["ps%d" % (6 + i % 2)]
            e.matmul(bank[:, 0:TTW], mk.cst(perm_off, 128), src,
                     start=True, stop=True)
            e.drain().then_inc(S["mm"], 1)
    mk.sec("tensor", fte)
    mk.n["mm"] += len(units)

    def fve(e):
        for i, (ch, tti, tt) in enumerate(units):
            e.wait_ge(S["mm"], mmb0 + i + 1)
            src = mk.A(buf_off + ch * ncols + tti * TTW, TTW)
            bank = t["ps%d" % (6 + i % 2)]
            cosap = mk.A(cos_off + tt * TTW, TTW)
            sinap = mk.A(sin_off + tt * TTW, TTW)
            e.drain()
            e.tensor_mul(mk.A(tmp_off, 512), bank[:, 0:TTW], sinap)
            e.tensor_mul(src, src, cosap)
            e.drain()
            e.tensor_add(src, src, mk.A(tmp_off, 512))
            e.drain().then_inc(S["ve"], 2)
    mk.sec("vector", fve)
    mk.n["ve"] += 2 * len(units)


def attn_core(mk, qc, agko, agvo, kdim, score, krow_fn, vcol_fn):
    S, t = mk.S, mk.t
    nkb = NKB[qc]
    NU = nkb * NH
    qb, ob, eb, mb, kv = ATT_QB, ATT_OB, ATT_EB, ATT_MB, ATT_KV

    mmb, scb, veb = mk.n["mm"], mk.n["sc"], mk.n["ve"]
    klane = ("dka", "dkb")
    vlane = ("dva", "dvb")
    kcnt = {0: mk.n["dka"], 1: mk.n["dkb"]}
    vcnt = {0: mk.n["dva"], 1: mk.n["dvb"]}
    dmb = mk.n["dm"]
    rec = {"mask": {}, "k": {}, "v": {}}

    def fsync(e):
        for kbi in range(nkb):
            s = SLOTS.index((qc, kbi))
            if kbi >= 1:
                e.wait_ge(S["ve"], veb + 5 * NH * kbi)
            e.dma_start(out=mk.A(mb, 2048),
                        in_=t["masks"].ap()[s * 128:(s + 1) * 128, :]).then_inc(S["dm"], 16)
            rec["mask"][kbi] = dmb + 16 * (kbi + 1)
            blk, c0 = KBSRC[kbi]
            for h in range(NH):
                u = kbi * NH + h
                if u >= 2:
                    e.wait_ge(S["mm"], mmb + 5 * (u - 1))
                kr = krow_fn(h)
                kcnt[u % 2] += 16
                e.dma_start(out=mk.A(kv + (u % 2) * 512, 512),
                            in_=agko.ap()[blk * kdim + kr:blk * kdim + kr + 128,
                                          c0:c0 + 512]).then_inc(S[klane[u % 2]], 16)
                rec["k"][u] = (klane[u % 2], kcnt[u % 2])
                vc = vcol_fn(h)
                vsrc = rr(agvo.ap(), "(b tk p) d -> p b tk d", p=128, b=2)
                vcnt[u % 2] += 16
                e.dma_start(out=rr(mk.A(kv + 1024 + (u % 2) * 512, 512),
                                   "p (tk d) -> p tk d", tk=4),
                            in_=vsrc[:, blk, c0 // 128:c0 // 128 + 4,
                                     vc:vc + 128]).then_inc(S[vlane[u % 2]], 16)
                rec["v"][u] = (vlane[u % 2], vcnt[u % 2])
    mk.sec("sync", fsync)
    mk.n["dm"] = dmb + 16 * nkb
    mk.n["dka"] = kcnt[0]
    mk.n["dkb"] = kcnt[1]
    mk.n["dva"] = vcnt[0]
    mk.n["dvb"] = vcnt[1]

    def fte(e):
        for kbi in range(nkb):
            for h in range(NH):
                u = kbi * NH + h
                e.wait_ge(S[rec["k"][u][0]], rec["k"][u][1])
                for kc in range(4):
                    g = 4 * u + kc
                    if g >= 2:
                        e.wait_ge(S["sc"], scb + g - 1)
                    sbk = t["ps%d" % (2 + g % 2)]
                    e.matmul(sbk[:, 0:TTW],
                             mk.A(kv + (u % 2) * 512 + kc * 128, 128),
                             mk.A(qb + h * 512, TTW),
                             start=True, stop=True).then_inc(S["mm"], 1)
                e.wait_ge(S["ve"], veb + 5 * u + 4)
                e.wait_ge(S[rec["v"][u][0]], rec["v"][u][1])
                if u >= 2:
                    e.wait_ge(S["ve"], veb + 5 * (u - 1))
                obk = t["ps%d" % (4 + h % 2)]
                ins = None
                for kc in range(4):
                    e.matmul(t["ps6"][0:16, 0:TTW], mk.rowsel(h),
                             mk.A(eb + (u % 2) * 2048 + kc * 512, 512),
                             start=(u == 0 and kc == 0),
                             stop=(u == NU - 1 and kc == 3))
                    ins = e.matmul(obk[:, 0:TTW],
                                   mk.A(kv + 1024 + (u % 2) * 512 + kc * 128, 128),
                                   mk.A(eb + (u % 2) * 2048 + kc * 512, 512),
                                   start=(kc == 0), stop=(kc == 3))
                ins.then_inc(S["mm"], 1)
    mk.sec("tensor", fte)
    mk.n["mm"] += 5 * NU

    def fsc(e):
        for kbi in range(nkb):
            for h in range(NH):
                u = kbi * NH + h
                for kc in range(4):
                    g = 4 * u + kc
                    e.wait_ge(S["mm"], mmb + 5 * u + kc + 1)
                    if u >= 2:
                        e.wait_ge(S["mm"], mmb + 5 * (u - 1))
                    sbk = t["ps%d" % (2 + g % 2)]
                    dst = mk.A(eb + (u % 2) * 2048 + kc * 512, 512)
                    if score[0] == "square":
                        e.activation(dst, sbk[:, 0:TTW], AF.Square)
                    else:
                        e.activation(dst, sbk[:, 0:TTW], AF.Exp, scale=float(score[1]))
                    e.drain().then_inc(S["sc"], 1)
    mk.sec("scalar", fsc)
    mk.n["sc"] += 4 * NU

    def fve(e):
        for kbi in range(nkb):
            for h in range(NH):
                u = kbi * NH + h
                e.wait_ge(S["dm"], rec["mask"][kbi])
                for kc in range(4):
                    e.wait_ge(S["sc"], scb + 4 * u + kc + 1)
                    ap = mk.A(eb + (u % 2) * 2048 + kc * 512, 512)
                    e.tensor_mul(ap, ap, mk.A(mb + kc * 512, 512))
                    e.drain().then_inc(S["ve"], 1)
                e.wait_ge(S["mm"], mmb + 5 * u + 5)
                obk = t["ps%d" % (4 + h % 2)]
                dst = mk.A(ob + h * 512, 512)
                e.drain()
                if kbi == 0:
                    e.tensor_copy(dst, obk[:, 0:TTW])
                else:
                    e.tensor_add(dst, dst, obk[:, 0:TTW])
                e.drain().then_inc(S["ve"], 1)
    mk.sec("vector", fve)
    mk.n["ve"] += 5 * NU


def znorm_apply(mk, z_eps):
    S, t = mk.S, mk.t
    ob = ATT_OB
    mmb = mk.n["mm"]

    def fve(e):
        e.wait_ge(S["mm"], mmb)
        src = t["ps6"][0:16, 0:TTW]
        if z_eps:
            e.tensor_scalar_add(t["sm0"][0:16, 0:512], src, float(z_eps))
            e.drain()
            e.reciprocal(t["sm0"][0:16, 0:512], t["sm0"][0:16, 0:512])
        else:
            e.reciprocal(t["sm0"][0:16, 0:512], src)
        e.drain().then_inc(S["ve"], 1)
    mk.sec("vector", fve)
    mk.n["ve"] += 1

    veb2 = mk.n["ve"]
    mmb2 = mk.n["mm"]
    def fte(e):
        e.wait_ge(S["ve"], veb2)
        for h in range(NH):
            if h >= 2:
                e.wait_ge(S["ve"], veb2 + h - 1)
            bank = t["ps%d" % (2 + h % 2)]
            e.matmul(bank[:, 0:TTW], mk.hsel(h), t["sm0"][0:16, 0:512],
                     start=True, stop=True)
            e.drain().then_inc(S["mm"], 1)
    mk.sec("tensor", fte)
    mk.n["mm"] += NH

    def fve2(e):
        e.drain()
        for h in range(NH):
            e.wait_ge(S["mm"], mmb2 + h + 1)
            bank = t["ps%d" % (2 + h % 2)]
            dst = mk.A(ob + h * 512, 512)
            e.tensor_mul(dst, dst, bank[:, 0:TTW])
            e.drain().then_inc(S["ve"], 1)
    mk.sec("vector", fve2)
    mk.n["ve"] += NH


def proj_vT(mk, W_ap, Kdim, mts, rhs_fn, agvi, vt_off, bias_col=None):
    S, t = mk.S, mk.t
    vt = vt_off
    vs = vt_off + 1024
    nkc = Kdim // 128
    mmb, scb, veb, dgb = (mk.n["mm"], mk.n["sc"], mk.n["ve"], mk.n["dg"])
    lane = ("dwa", "dwb")
    lb = [mk.n["dwa"], mk.n["dwb"]]
    Wr = rr(W_ap, "(kc p) m -> p kc m", p=128)
    ntl = len(mts) * 2

    def fsync(e):
        for j, (c0, w) in enumerate(mts):
            if j >= 2:
                e.wait_ge(S["mm"], mmb + 2 * ((j - 1) * 2))
            else:
                if mk.wb_last[j % 2]:
                    e.wait_ge(S["mm"], mk.wb_last[j % 2])
            e.dma_start(out=mk.wpanel_dst(j % 2, nkc, w),
                        in_=Wr[:, :, c0:c0 + w]).then_inc(S[lane[j % 2]], 16)
    mk.sec("sync", fsync)
    mk.n["dwa"] += 16 * ((len(mts) + 1) // 2)
    mk.n["dwb"] += 16 * (len(mts) // 2)
    for sslot in range(2):
        js = [j for j in range(len(mts)) if j % 2 == sslot]
        if js:
            mk.wb_last[sslot] = mmb + 2 * ntl

    recp = {}
    rect = {}

    def fte(e):
        cnt = mmb
        for j, (c0, w) in enumerate(mts):
            e.wait_ge(S[lane[j % 2]], lb[j % 2] + 16 * (j // 2 + 1))
            for tt in range(2):
                i = j * 2 + tt
                if i >= 2:
                    e.wait_ge(S["sc"], scb + i - 1)
                elif mk.ps01_last[i % 2] is not None:
                    e.wait_ge(S[mk.ps01_last[i % 2][0]], mk.ps01_last[i % 2][1])
                ps = t["ps%d" % (i % 2)]
                ins = None
                for kc in range(nkc):
                    ins = e.matmul(ps[0:w, 0:TTW], mk.wtile(j % 2, kc, w),
                                   rhs_fn(kc, tt), start=(kc == 0),
                                   stop=(kc == nkc - 1))
                ins.then_inc(S["mm"], 1)
                cnt += 1
                recp[i] = cnt
                if i >= 1:
                    it = i - 1
                    e.wait_ge(S["sc"], scb + it + 1)
                    if it >= 2:
                        e.wait_ge(S["ve"], veb + it - 1)
                    bank = t["ps%d" % (2 + it % 2)]
                    for b in range(4):
                        e.transpose(bank[:, b * 128:(b + 1) * 128],
                                    mk.A(vt + (it % 2) * 512 + b * 128, 128),
                                    mk.cst(CST_ID, 128))
                    e.drain().then_inc(S["mm"], 1)
                    cnt += 1
                    rect[it] = cnt
        it = ntl - 1
        e.wait_ge(S["sc"], scb + it + 1)
        e.wait_ge(S["ve"], veb + it - 1)
        bank = t["ps%d" % (2 + it % 2)]
        for b in range(4):
            e.transpose(bank[:, b * 128:(b + 1) * 128],
                        mk.A(vt + (it % 2) * 512 + b * 128, 128),
                        mk.cst(CST_ID, 128))
        e.drain().then_inc(S["mm"], 1)
        cnt += 1
        rect[it] = cnt
    mk.sec("tensor", fte)
    mk.n["mm"] += 2 * ntl
    for par in range(2):
        iis = [ii2 for ii2 in range(ntl) if ii2 % 2 == par]
        if iis:
            mk.ps01_last[par] = ("sc", mk.n["sc"] + iis[-1] + 1)

    def fsc(e):
        for i in range(ntl):
            e.wait_ge(S["mm"], recp[i])
            if i >= 2:
                e.wait_ge(S["mm"], rect[i - 2])
            ps = t["ps%d" % (i % 2)]
            j = i // 2
            if bias_col is not None:
                e.activation(mk.A(vt + (i % 2) * 512, 512), ps[:, 0:TTW],
                             AF.Identity, bias=mk.vec(bias_col + j))
            else:
                e.activation(mk.A(vt + (i % 2) * 512, 512), ps[:, 0:TTW], AF.Identity)
            e.drain().then_inc(S["sc"], 1)
    mk.sec("scalar", fsc)
    mk.n["sc"] += ntl

    gpb = mk.n["gp"]
    def fve(e):
        for i in range(ntl):
            e.wait_ge(S["mm"], rect[i])
            if i >= 2:
                e.wait_ge(S["gp"], gpb + i - 1)
            bank = t["ps%d" % (2 + i % 2)]
            e.tensor_copy(mk.A(vs + (i % 2) * 512, 512), bank[:, 0:512])
            e.drain().then_inc(S["ve"], 1)
    mk.sec("vector", fve)
    mk.n["ve"] += ntl

    def fg(g):
        for i in range(ntl):
            g.wait_ge(S["ve"], veb + i + 1)
            j, tt = i // 2, i % 2
            c0 = mts[j][0]
            dst = rr(agvi.ap(), "(tk p) d -> p tk d", p=128)[
                :, tt * 4:(tt + 1) * 4, c0:c0 + 128]
            src = rr(mk.A(vs + (i % 2) * 512, 512), "p (b d) -> p b d", b=4)
            g.dma_start(out=dst, in_=src).then_inc(S["dg"], 16)
            g.wait_ge(S["dg"], dgb + 16 * (i + 1))
            g.nop().then_inc(S["gp"], 1)
    mk.sec("gpsimd", fg)
    mk.n["dg"] += 16 * ntl
    mk.n["gp"] += ntl


def allgather_pair(mk, pairs):
    S = mk.S
    dgt = mk.n["dg"]
    groups = [[0, 1], [2, 3], [4, 5], [6, 7]]
    def fg(g):
        g.wait_ge(S["dg"], dgt)
        for p in pairs:
            g.collective_compute(
                "AllGather", OP.bypass, replica_groups=groups,
                ins=[p[0].ap().opt()],
                outs=[p[1].ap().opt()],
            ).then_inc(S["cc"], 1)
    mk.sec("gpsimd", fg)
    mk.n["cc"] += len(pairs)


def wait_cc(mk, val):
    S = mk.S
    def f(e):
        e.wait_ge(S["cc"], val)
    mk.sec("sync", f)


def p1_l0_attn(mk):
    S, t = mk.S, mk.t
    kq = 0
    def ev(e, i, j, tt, ps):
        e.activation(mk.A(kq + j * NTOK + tt * TTW, TTW), ps, AF.Identity,
                     scale=mk.vec(VGAM), bias=mk.vec(VBET))
    mk.proj(t["wk0"].ap(), DM, MTS16, (0, 1),
            lambda kc, tt: mk.xap(kc, tt * TTW), ev, eng="scalar")
    mk.barrier()
    fm_ln(mk, kq, (0, 1), sq_off=16384, dma_to=t["agki_a1"])
    mk.barrier()

    proj_vT(mk, t["wv0"].ap(), DM, MTS16,
            lambda kc, tt: mk.xap(kc, tt * TTW), t["agvi_a1"], vt_off=0)
    allgather_pair(mk, [(t["agki_a1"], t["agko_a1"]), (t["agvi_a1"], t["agvo_a1"])])
    mk.barrier()

    ccq = mk.n["cc"]
    for qc in range(2):
        def evq(e, i, j, tt, ps):
            e.activation(mk.A(ATT_QB + j * 512, TTW), ps, AF.Identity,
                         scale=mk.vec(VGAM), bias=mk.vec(VBET))
        mk.proj(t["wq0"].ap(), DM, MTS16, (qc,),
                lambda kc, tt: mk.xap(kc, tt * TTW), evq, eng="scalar")
        mk.barrier()
        fm_ln(mk, ATT_QB, (qc,), sq_off=ATT_EB, extra_scale=128.0 ** -0.5,
              ncols=512)
        wait_cc(mk, ccq)
        mk.barrier()
        attn_core(mk, qc, t["agko_a1"], t["agvo_a1"], DM, ("square",),
                  lambda h: h * 128, lambda h: h * 128)
        znorm_apply(mk, 1e-5)
        mk.barrier()
        def evo(e, i, j, tt, ps):
            dst = mk.xap(j, tt * TTW)
            e.tensor_add(dst, dst, ps)
        mk.proj(t["wo0"].ap(), DM, MTS16, (qc,),
                lambda kc, tt: mk.A(ATT_OB + kc * 512, 512), evo, eng="vector")
        mk.barrier()
    ln_full(mk, sq_off=0)


def p2_l0_ffn(mk):
    S, t = mk.S, mk.t
    dgb = mk.n["dg"]
    def fg(g):
        g.dma_start(out=t["stash"].ap(), in_=t["x"][:, :]).then_inc(S["dg"], 16)
    mk.sec("gpsimd", fg)
    mk.n["dg"] += 16
    mk.barrier()

    xn = 0
    ln_full(mk, sq_off=16384, rms_wcol=VL0W1, dst_off=xn)

    kvd = 16384
    qd = 20480
    mk.proj(t["kvdw"].ap(), DM, MTS4, (0, 1),
            lambda kc, tt: mk.A(xn + kc * NTOK + tt * TTW, TTW),
            _ev_copy(mk, kvd), eng="vector")
    mk.proj(t["qdw"].ap(), DM, MTS4, (0, 1),
            lambda kc, tt: mk.A(xn + kc * NTOK + tt * TTW, TTW),
            _ev_copy(mk, qd), eng="vector")
    mk.barrier()
    def fgq(g):
        g.dma_start(out=t["qdsp"].ap(), in_=mk.A(qd, 4 * NTOK)).then_inc(S["dg"], 16)
        g.wait_ge(S["dg"], mk.n["dg"] + 16)
        g.nop().then_inc(S["gp"], 1)
    mk.sec("gpsimd", fgq)
    mk.n["dg"] += 16
    mk.n["gp"] += 1
    gp_spill = mk.n["gp"]

    kp = 24576
    def evkp(e, i, j, tt, ps):
        e.tensor_copy(mk.A(kp + (i % 2) * 512, 512), ps)
    def dmakp(g, i, j, tt):
        for half in range(2):
            h = 2 * j + half
            src = mk.A(kp + (i % 2) * 512, 512, p0=half * 64, pn=64)
            g.dma_start(out=t["agki_a2"].ap()[h * 128:h * 128 + 64,
                                              tt * TTW:(tt + 1) * TTW],
                        in_=src).then_inc(S["dg"], 16)
    mk.proj(t["kuw"].ap(), 512, MTS8, (0, 1),
            lambda kc, tt: mk.A(kvd + kc * NTOK + tt * TTW, TTW),
            evkp, eng="vector", dma_out=dmakp, dma_per_tile=2)
    mk.barrier()

    proj_vT(mk, t["vuw"].ap(), 512, MTS16,
            lambda kc, tt: mk.A(kvd + kc * NTOK + tt * TTW, TTW),
            t["agvi_a2"], vt_off=24576)
    mk.barrier()

    tabs = 24576
    dkb = mk.n["dx"]
    def fsy(e):
        e.dma_start(out=mk.A(tabs, NTOK), in_=t["l0cos"].ap()).then_inc(S["dx"], 16)
        e.dma_start(out=mk.A(tabs + NTOK, NTOK), in_=t["l0sin"].ap()).then_inc(S["dx"], 16)
    mk.sec("sync", fsy)
    mk.n["dx"] += 32
    krb = 16384
    for half in range(2):
        mts = MTS8[half * 4:(half + 1) * 4]
        def evkr(e, i, j, tt, ps):
            e.tensor_copy(mk.A(krb + j * NTOK + tt * TTW, TTW), ps)
        mk.proj(t["ropekw"].ap()[:, half * 512:(half + 1) * 512], DM,
                MTS4, (0, 1),
                lambda kc, tt: mk.A(xn + kc * NTOK + tt * TTW, TTW),
                evkr, eng="vector")
        if half == 0:
            def fw(e, dkb=dkb):
                e.wait_ge(S["dx"], dkb + 32)
            mk.sec("vector", fw)
        mk.barrier()
        rope_buf(mk, krb, list(range(4)), (0, 1),
                 CST_PERMH, tabs, tabs + NTOK, tmp_off=krb + 4 * NTOK)
        veb = mk.n["ve"]
        def fgkr(g, half=half, veb=veb):
            g.wait_ge(S["ve"], veb)
            for j in range(4):
                for hh in range(2):
                    h = 2 * (half * 4 + j) + hh
                    g.dma_start(out=t["agki_a2"].ap()[h * 128 + 64:h * 128 + 128, :],
                                in_=mk.A(krb + j * NTOK, NTOK, p0=hh * 64, pn=64)
                                ).then_inc(S["dg"], 16)
        mk.sec("gpsimd", fgkr)
        mk.n["dg"] += 16 * 8
        mk.barrier()
    allgather_pair(mk, [(t["agki_a2"], t["agko_a2"]), (t["agvi_a2"], t["agvo_a2"])])
    mk.barrier()

    ccq = mk.n["cc"]
    for qc in range(2):
        dkb2 = mk.n["dx"]
        def fsy2(e):
            e.dma_start(out=mk.A(ATT_QTAB, NTOK), in_=t["l0cosq"].ap()).then_inc(S["dx"], 16)
            e.dma_start(out=mk.A(ATT_QTAB + NTOK, NTOK), in_=t["l0sinq"].ap()).then_inc(S["dx"], 16)
        mk.sec("sync", fsy2)
        mk.n["dx"] += 32
        def fgz(g):
            g.wait_ge(S["gp"], gp_spill)
            g.dma_start(out=mk.A(ATT_EB, 4096), in_=t["qdsp"].ap()).then_inc(S["dg"], 16)
            g.wait_ge(S["dg"], mk.n["dg"] + 16)
            g.nop().then_inc(S["gp"], 1)
        mk.sec("gpsimd", fgz)
        mk.n["dg"] += 16
        mk.n["gp"] += 1
        gp3 = mk.n["gp"]
        def fwv(e):
            e.wait_ge(S["gp"], gp3)
            e.wait_ge(S["dx"], dkb2 + 32)
        mk.sec("tensor", fwv)
        mk.barrier()

        def evq(e, i, j, tt, ps):
            e.tensor_copy(mk.A(ATT_QB + j * 512, TTW), ps)
        mk.proj(t["wqasm"].ap(), 512, MTS16, (qc,),
                lambda kc, tt: mk.A(ATT_EB + kc * NTOK + tt * TTW, TTW),
                evq, eng="vector")
        mk.barrier()
        rope_buf(mk, ATT_QB, list(range(NH)), (qc,), CST_PERMQ,
                 ATT_QTAB, ATT_QTAB + NTOK, tmp_off=ATT_MB, ncols=512)
        wait_cc(mk, ccq)
        mk.barrier()
        attn_core(mk, qc, t["agko_a2"], t["agvo_a2"], DM, ("exp", 128.0 ** -0.5),
                  lambda h: h * 128, lambda h: h * 128)
        znorm_apply(mk, None)
        mk.barrier()
        def evo(e, i, j, tt, ps):
            dst = mk.xap(j, tt * TTW)
            e.tensor_add(dst, dst, ps)
        mk.proj(t["ow"].ap(), DM, MTS16, (qc,),
                lambda kc, tt: mk.A(ATT_OB + kc * 512, 512), evo, eng="vector")
        mk.barrier()

    xn2 = 0
    ln_full(mk, sq_off=16384, rms_wcol=VL0W2, dst_off=xn2)
    moe_router(mk, xn2)
    split_pair_xn2(mk)
    moe_experts(mk, xn2)

    def fg2(g):
        g.dma_start(out=mk.A(0, NCH * NTOK), in_=t["stash"].ap()).then_inc(S["dg"], 16)
        g.wait_ge(S["dg"], mk.n["dg"] + 16)
        g.nop().then_inc(S["gp"], 1)
    mk.sec("gpsimd", fg2)
    mk.n["dg"] += 16
    mk.n["gp"] += 1
    gp5 = mk.n["gp"]
    def fw3(e):
        e.wait_ge(S["gp"], gp5)
    mk.sec("vector", fw3)
    ln_full(mk, sq_off=16384, add_off=0)


def _ev_copy(mk, dst_off, ncols=NTOK):
    def ev(e, i, j, tt, ps):
        e.tensor_copy(mk.A(dst_off + j * ncols + tt * TTW, TTW), ps)
    return ev


XHI = 0
XLO = 16384


def split_pair_xn2(mk):
    S, t = mk.S, mk.t
    sav = 17408
    dka0, dkb0 = mk.n["dka"], mk.n["dkb"]

    def src(kk):
        if kk == 0:
            return t["sm0"][:, 0:NTOK]
        if kk < 8:
            return mk.A(kk * NTOK, NTOK)
        return mk.A(sav + (kk - 8) * NTOK, NTOK)

    def fve(e):
        abf = t["arena"].bitcast(BF16)
        e.tensor_copy(mk.A(sav, 8 * NTOK), mk.A(8 * NTOK, 8 * NTOK))
        ins = e.tensor_copy(t["sm0"][:, 0:NTOK], mk.A(0, NTOK))
        ins.then_inc(S["dkb"], 1)
        e.drain()
        for r in range(8):
            e.wait_ge(S["dka"], dka0 + r + 1)
            ins = None
            for kk in (2 * r, 2 * r + 1):
                ins = e.tensor_tensor(
                    abf[:, XLO + kk * NTOK: XLO + (kk + 1) * NTOK],
                    src(kk),
                    abf[:, XHI + kk * NTOK: XHI + (kk + 1) * NTOK],
                    OP.subtract)
            ins.then_inc(S["dkb"], 1)
    mk.sec("vector", fve)
    mk.n["dkb"] += 9

    def fsc(e):
        abf = t["arena"].bitcast(BF16)
        for r in range(8):
            e.wait_ge(S["dkb"], dkb0 + r + 1)
            e.drain()
            ins = None
            for kk in (2 * r, 2 * r + 1):
                ins = e.activation(abf[:, XHI + kk * NTOK: XHI + (kk + 1) * NTOK],
                                   src(kk), AF.Copy)
            ins.then_inc(S["dka"], 1)
    mk.sec("scalar", fsc)
    mk.n["dka"] += 8
    mk.barrier()


def moe_router(mk, xn2):
    S, t = mk.S, mk.t
    def ev(e, i, j, tt, ps):
        e.activation(t["sm0"][0:7, tt * TTW:(tt + 1) * TTW], ps[0:7, 0:TTW],
                     AF.Sigmoid, bias=mk.vec(VRB0, pn=7))
    mk.proj(t["rw0"].ap(), DM, [(0, 7)], (0, 1),
            lambda kc, tt: mk.A(xn2 + kc * NTOK + tt * TTW, TTW), ev, eng="scalar")
    mk.barrier()

    ptm = 1024
    wtm = 1088
    mmb, veb = mk.n["mm"], mk.n["ve"]

    def fte(e):
        for g in range(8):
            if g >= 2:
                e.wait_ge(S["ve"], veb + g - 1)
            bank = t["ps%d" % (2 + g % 2)]
            e.transpose(bank[0:128, 0:7], t["sm0"][0:7, g * 128:(g + 1) * 128],
                        mk.cst(CST_ID, 7, pn=7))
            e.drain().then_inc(S["mm"], 1)
    mk.sec("tensor", fte)
    mk.n["mm"] += 8

    def fve(e):
        for g in range(8):
            e.wait_ge(S["mm"], mmb + g + 1)
            bank = t["ps%d" % (2 + g % 2)]
            e.tensor_copy(t["sm0"][:, ptm + g * 8: ptm + g * 8 + 7], bank[:, 0:7])
            e.memset(t["sm0"][:, ptm + g * 8 + 7: ptm + g * 8 + 8], -1.0)
            e.drain().then_inc(S["ve"], 1)
    mk.sec("vector", fve)
    mk.n["ve"] += 8

    veb2 = mk.n["ve"]
    def fve2(e):
        e.wait_ge(S["ve"], veb2)
        for g in range(8):
            pg = t["sm0"][:, ptm + g * 8: ptm + g * 8 + 8]
            wg = t["sm0"][:, wtm + g * 8: wtm + g * 8 + 8]
            e.drain()
            e.max(out=t["sm2"][:, 0:8], in_=pg)
            e.drain()
            e.tensor_copy(t["sm2"][:, 8:10], t["sm2"][:, 0:2])
            e.memset(t["sm2"][:, 10:16], -2.0)
            e.drain()
            e.match_replace(out=wg, in_to_replace=t["sm2"][:, 8:16],
                            in_values=pg, imm_value=0.0)
            e.tensor_add(t["sm2"][:, 16:17], t["sm2"][:, 0:1], t["sm2"][:, 1:2])
            e.drain()
            e.reciprocal(t["sm2"][:, 16:17], t["sm2"][:, 16:17])
            e.drain()
            e.tensor_sub(wg, pg, wg)
            e.drain()
            e.tensor_single_scalar(wg, wg, t["sm2"][:, 16:17], OP.mult)
        e.drain().then_inc(S["ve"], 1)
    mk.sec("vector", fve2)
    mk.n["ve"] += 1

    veb3 = mk.n["ve"]
    mmb3 = mk.n["mm"]
    def fte2(e):
        e.wait_ge(S["ve"], veb3)
        for g in range(8):
            bank = t["ps%d" % (2 + (g // 4) % 2)]
            e.transpose(bank[0:8, (g % 4) * 128:(g % 4) * 128 + 128],
                        t["sm0"][:, wtm + g * 8: wtm + g * 8 + 8], mk.cst(CST_ID, 128))
            if g % 4 == 3:
                e.drain().then_inc(S["mm"], 1)
    mk.sec("tensor", fte2)
    mk.n["mm"] += 2

    wfm = 16384
    def fve3(e):
        for half in range(2):
            e.wait_ge(S["mm"], mmb3 + half + 1)
            bank = t["ps%d" % (2 + half % 2)]
            e.tensor_copy(mk.A(wfm + half * 512, 512, pn=8), bank[0:8, 0:512])
        e.drain().then_inc(S["ve"], 1)
    mk.sec("vector", fve3)
    mk.n["ve"] += 1
    mk.barrier()
    return wfm


WFM_OFF = 16384


def moe_experts(mk, xn2):
    S, t = mk.S, mk.t
    hb = 17408
    wB = 25600
    for ei in range(8):
        routed = ei >= 1
        e_idx = ei - 1
        if routed:
            mmb = mk.n["mm"]
            veb = mk.n["ve"]
            def fteb(e, e_idx=e_idx, veb=veb):
                e.wait_ge(S["ve"], veb)
                for tt in range(2):
                    e.matmul(t["ps%d" % (6 + tt)][:, 0:TTW],
                             mk.cst(CST_ESEL + e_idx * 128, 128, pn=7),
                             mk.A(WFM_OFF + tt * TTW, TTW, pn=7),
                             start=True, stop=True)
                    e.drain().then_inc(S["mm"], 1)
            mk.sec("tensor", fteb)
            mk.n["mm"] += 2
            def fveb(e, mmb=mmb):
                for tt in range(2):
                    e.wait_ge(S["mm"], mmb + tt + 1)
                    e.tensor_scalar_mul(mk.A(wB + tt * TTW, TTW),
                                        t["ps%d" % (6 + tt)][:, 0:TTW], 1.0 / 8.0)
                e.drain().then_inc(S["ve"], 1)
            mk.sec("vector", fveb)
            mk.n["ve"] += 1
            mk.barrier()
            Wg = t["rg"].ap()[e_idx * DM:(e_idx + 1) * DM, :]
            Wu = t["ru"].ap()[e_idx * DM:(e_idx + 1) * DM, :]
            Wd = t["rd"].ap()[e_idx * 1024:(e_idx + 1) * 1024, :]
        else:
            Wg, Wu, Wd = t["shg"].ap(), t["shu"].ap(), t["shd"].ap()

        gate_up(mk, xn2, Wg, Wu, hb, wB if routed else None)
        mk.barrier()
        def evd(e, i, j, tt, ps):
            dst = mk.xap(j, tt * TTW)
            e.tensor_add(dst, dst, ps)
        mk.proj(Wd, 1024, MTS16, (0, 1),
                lambda kc, tt: mk.A(hb + kc * NTOK + tt * TTW, TTW), evd,
                eng="vector")
        mk.barrier()


GU_PIECES = ((0, 6), (6, 12), (12, 16))


def gate_up(mk, xn2, Wg, Wu, hb, wB):
    S, t = mk.S, mk.t
    nkc = NCH
    mmb, scb, veb = mk.n["mm"], mk.n["sc"], mk.n["ve"]
    da0, ka0, kb0 = mk.n["dwa"], mk.n["dka"], mk.n["dkb"]
    wbl = list(mk.wb_last)
    Wr_ = (rr(Wg, "(kc p) m -> p kc m", p=128),
           rr(Wu, "(kc p) m -> p kc m", p=128))

    db0 = mk.n["dwb"]
    lanes = ("dwa", "dwb")

    def fsync(e):
        for j in range(8):
            for w in range(2):
                for p, (k0, k1) in enumerate(GU_PIECES):
                    pi = (j * 2 + w) * 3 + p
                    if pi >= 2:
                        e.wait_ge(S["dkb"], kb0 + pi - 1)
                    e.dma_start(out=t["sm0"][:, (pi % 2) * 768:
                                             (pi % 2) * 768 + (k1 - k0) * 128],
                                in_=Wr_[w][:, k0:k1, j * 128:(j + 1) * 128]
                                ).then_inc(S[lanes[pi % 2]], 16)
    mk.sec("sync", fsync)
    mk.n["dwa"] += 16 * 24
    mk.n["dwb"] += 16 * 24

    def his(e, j):
        wbf = t["wb"].bitcast(BF16)
        for w in range(2):
            for p, (k0, k1) in enumerate(GU_PIECES):
                pi = (j * 2 + w) * 3 + p
                e.wait_ge(S[lanes[pi % 2]],
                          (da0 if pi % 2 == 0 else db0) + 16 * (pi // 2 + 1))
                if p == 0:
                    if j >= 1:
                        e.wait_ge(S["mm"], mmb + 4 * (j - 1) + 2 * (w + 1))
                    else:
                        for sslot in range(2):
                            if wbl[sslot]:
                                e.wait_ge(S["mm"], wbl[sslot])
                ins = e.activation(wbf[:, w * 4096 + k0 * 128: w * 4096 + k1 * 128],
                                   t["sm0"][:, (pi % 2) * 768:
                                            (pi % 2) * 768 + (k1 - k0) * 128],
                                   AF.Copy)
                ins.then_inc(S["dka"], 1)

    def los(e, j):
        wbf = t["wb"].bitcast(BF16)
        for w in range(2):
            for p, (k0, k1) in enumerate(GU_PIECES):
                pi = (j * 2 + w) * 3 + p
                e.wait_ge(S["dka"], ka0 + pi + 1)
                ins = e.tensor_tensor(
                    wbf[:, w * 4096 + 2048 + k0 * 128: w * 4096 + 2048 + k1 * 128],
                    t["sm0"][:, (pi % 2) * 768: (pi % 2) * 768 + (k1 - k0) * 128],
                    wbf[:, w * 4096 + k0 * 128: w * 4096 + k1 * 128],
                    OP.subtract)
                ins.then_inc(S["dkb"], 1)

    banks = ("ps0", "ps2", "ps1", "ps3")

    def fte(e):
        abf = t["arena"].bitcast(BF16)
        wbf = t["wb"].bitcast(BF16)
        for j in range(8):
            for gi in range(4):
                w, tt = gi // 2, gi % 2
                if gi % 2 == 0:
                    e.wait_ge(S["dkb"], kb0 + 6 * j + 3 * (w + 1))
                if j >= 1:
                    e.wait_ge(S["ve"], veb + (j - 1) * 2 + tt + 1)
                elif gi == 0:
                    for par in range(2):
                        if mk.ps01_last[par] is not None:
                            e.wait_ge(S[mk.ps01_last[par][0]],
                                      mk.ps01_last[par][1])
                ins = None
                for kc in range(nkc):
                    whi = wbf[:, w * 4096 + kc * 128: w * 4096 + (kc + 1) * 128]
                    wlo = wbf[:, w * 4096 + 2048 + kc * 128:
                              w * 4096 + 2048 + (kc + 1) * 128]
                    base = kc * NTOK + tt * TTW
                    xhi = abf[:, XHI + base: XHI + base + TTW]
                    xlo = abf[:, XLO + base: XLO + base + TTW]
                    ins = e.matmul(t[banks[gi]][:, 0:TTW], whi, xhi,
                                   start=(kc == 0), stop=False)
                    ins = e.matmul(t[banks[gi]][:, 0:TTW], whi, xlo,
                                   start=False, stop=False)
                    ins = e.matmul(t[banks[gi]][:, 0:TTW], wlo, xhi,
                                   start=False, stop=(kc == nkc - 1))
                ins.then_inc(S["mm"], 1)
    mk.sec("tensor", fte)
    mk.n["mm"] += 8 * 2 * 2
    mk.wb_last[0] = mk.wb_last[1] = mk.n["mm"]

    def fsc(e):
        his(e, 0)
        for i in range(16):
            j, tt = i // 2, i % 2
            e.wait_ge(S["mm"], mmb + 4 * j + tt + 1)
            dst = mk.A(hb + j * NTOK + tt * TTW, TTW)
            ins = e.activation(dst, t[banks[tt]][:, 0:TTW], AF.Silu)
            ins.then_inc(S["sc"], 1)
            if tt == 1 and j < 7:
                his(e, j + 1)
    mk.sec("scalar", fsc)
    mk.n["sc"] += 16
    mk.n["dka"] += 48

    def fve(e):
        los(e, 0)
        for i in range(16):
            j, tt = i // 2, i % 2
            e.wait_ge(S["sc"], scb + i + 1)
            e.wait_ge(S["mm"], mmb + 4 * j + 2 + tt + 1)
            dst = mk.A(hb + j * NTOK + tt * TTW, TTW)
            e.tensor_mul(dst, dst, t[banks[2 + tt]][:, 0:TTW])
            e.drain()
            if wB is None:
                e.tensor_scalar_mul(dst, dst, 1.0 / 8.0)
            else:
                e.tensor_mul(dst, dst, mk.A(wB + tt * TTW, TTW))
            e.drain().then_inc(S["ve"], 1)
            if tt == 1 and j < 7:
                los(e, j + 1)
    mk.sec("vector", fve)
    mk.n["ve"] += 16
    mk.n["dkb"] += 48
    mk.ps01_last[0] = mk.ps01_last[1] = ("ve", mk.n["ve"])


def p3_l1_attn(mk):
    S, t = mk.S, mk.t
    tabs = 0
    dkb = mk.n["dk"]
    def fsy(e):
        e.dma_start(out=mk.A(tabs, NTOK), in_=t["l1cos"].ap()).then_inc(S["dx"], 16)
        e.dma_start(out=mk.A(tabs + NTOK, NTOK), in_=t["l1sin"].ap()).then_inc(S["dx"], 16)
    mk.sec("sync", fsy)
    mk.n["dx"] += 32

    kb = 2048
    def evk(e, i, j, tt, ps):
        e.activation(mk.A(kb + j * NTOK + tt * TTW, TTW), ps, AF.Identity,
                     bias=mk.vec(VBK1 + j))
    mk.proj(t["wk1"].ap(), DM, MTS4, (0, 1),
            lambda kc, tt: mk.xap(kc, tt * TTW), evk, eng="scalar")
    mk.barrier()
    nrm_factors(mk, kb, 4, None, sq_off=6144)
    def fwk(e):
        e.wait_ge(S["dx"], dkb + 32)
    mk.sec("vector", fwk)
    rope_buf(mk, kb, list(range(4)), (0, 1), CST_PERMI, tabs, tabs + NTOK,
             tmp_off=6144)
    apply_fac(mk, kb, 4)
    veb = mk.n["ve"]
    def fgk(g):
        g.wait_ge(S["ve"], veb)
        for j in range(4):
            g.dma_start(out=t["agki_a3"].ap()[j * 128:(j + 1) * 128, :],
                        in_=mk.A(kb + j * NTOK, NTOK)).then_inc(S["dg"], 16)
    mk.sec("gpsimd", fgk)
    mk.n["dg"] += 16 * 4
    mk.barrier()

    proj_vT(mk, t["wv1"].ap(), DM, MTS4,
            lambda kc, tt: mk.xap(kc, tt * TTW), t["agvi_a3"],
            vt_off=6656, bias_col=VBV1)
    allgather_pair(mk, [(t["agki_a3"], t["agko_a3"]), (t["agvi_a3"], t["agvo_a3"])])
    mk.barrier()

    ccq = mk.n["cc"]
    for qc in range(2):
        def evq(e, i, j, tt, ps):
            e.activation(mk.A(ATT_QB + j * 512, TTW), ps, AF.Identity,
                         bias=mk.vec(VBQ1 + j))
        mk.proj(t["wq1"].ap(), DM, MTS16, (qc,),
                lambda kc, tt: mk.xap(kc, tt * TTW), evq, eng="scalar")
        mk.barrier()
        nrm_factors(mk, ATT_QB, NH, VALP, tts=(qc,), ncols=512, sq_off=ATT_EB)
        rope_buf(mk, ATT_QB, list(range(NH)), (qc,), CST_PERMI,
                 ATT_QTAB, ATT_QTAB + NTOK, tmp_off=ATT_MB, ncols=512)
        apply_fac(mk, ATT_QB, NH, tts=(0,), ncols=512)
        wait_cc(mk, ccq)
        mk.barrier()
        attn_core(mk, qc, t["agko_a3"], t["agvo_a3"], 512, ("exp", 1.0),
                  lambda h: (h // 4) * 128, lambda h: (h // 4) * 128)
        znorm_apply(mk, None)
        mk.barrier()
        def evo(e, i, j, tt, ps):
            dst = mk.xap(j, tt * TTW)
            e.scalar_tensor_tensor(dst, ps, mk.vec(VBO1 + j), dst, OP.add, OP.add)
        mk.proj(t["wo1"].ap(), DM, MTS16, (qc,),
                lambda kc, tt: mk.A(ATT_OB + kc * 512, 512), evo, eng="vector")
        mk.barrier()
    ln_full(mk, sq_off=0)


def nrm_factors(mk, buf, nch, alpha_col, sq_off, tts=(0, 1), ncols=NTOK):
    S, t = mk.S, mk.t
    for tti, tt in enumerate(tts):
        scb, mmb = mk.n["sc"], mk.n["mm"]
        def fsc(e, tti=tti, mmb=mmb):
            for ch in range(nch):
                if ch >= 1:
                    e.wait_ge(S["mm"], mmb + ch)
                e.activation(mk.A(sq_off, 512),
                             mk.A(buf + ch * ncols + tti * TTW, TTW), AF.Square)
                e.drain().then_inc(S["sc"], 1)
        mk.sec("scalar", fsc)
        mk.n["sc"] += nch

        def fte(e, tti=tti, scb=scb):
            for ch in range(nch):
                e.wait_ge(S["sc"], scb + ch + 1)
                e.matmul(t["ps6"][0:16, 0:TTW], mk.rowsel(ch), mk.A(sq_off, 512),
                         start=(ch == 0), stop=(ch == nch - 1))
                e.drain().then_inc(S["mm"], 1)
        mk.sec("tensor", fte)
        mk.n["mm"] += nch

        mmb2 = mk.n["mm"]
        def fve(e, tti=tti, mmb2=mmb2):
            e.wait_ge(S["mm"], mmb2)
            e.tensor_copy(t["sm0"][0:16, tti * 512:(tti + 1) * 512],
                          t["ps6"][0:16, 0:TTW])
            e.drain().then_inc(S["ve"], 1)
        mk.sec("vector", fve)
        mk.n["ve"] += 1

        veb2 = mk.n["ve"]
        def fsc2(e, tti=tti, veb2=veb2):
            e.wait_ge(S["ve"], veb2)
            e.activation(t["sm0"][0:16, tti * 512:(tti + 1) * 512],
                         t["sm0"][0:16, tti * 512:(tti + 1) * 512], AF.Sqrt)
            e.drain().then_inc(S["sc"], 1)
        mk.sec("scalar", fsc2)
        mk.n["sc"] += 1

        scb2 = mk.n["sc"]
        def fve2(e, tti=tti, scb2=scb2):
            e.wait_ge(S["sc"], scb2)
            ap = t["sm0"][0:16, tti * 512:(tti + 1) * 512]
            e.tensor_scalar_add(ap, ap, 1e-6)
            e.drain()
            e.reciprocal(ap, ap)
            e.drain()
            if alpha_col is not None:
                e.tensor_single_scalar(ap, ap, mk.vec(alpha_col, pn=16), OP.mult)
            e.drain().then_inc(S["ve"], 1)
        mk.sec("vector", fve2)
        mk.n["ve"] += 1


def apply_fac(mk, buf, nch, tts=(0, 1), ncols=NTOK):
    S, t = mk.S, mk.t
    for tti in range(len(tts)):
        veb, mmb = mk.n["ve"], mk.n["mm"]
        def fte(e, tti=tti, veb=veb):
            e.wait_ge(S["ve"], veb)
            for ch in range(nch):
                if ch >= 2:
                    e.wait_ge(S["ve"], veb + ch - 1)
                bank = t["ps%d" % (2 + ch % 2)]
                e.matmul(bank[:, 0:TTW], mk.hsel(ch),
                         t["sm0"][0:16, tti * 512:(tti + 1) * 512],
                         start=True, stop=True)
                e.drain().then_inc(S["mm"], 1)
        mk.sec("tensor", fte)
        mk.n["mm"] += nch

        def fve(e, tti=tti, mmb=mmb):
            for ch in range(nch):
                e.wait_ge(S["mm"], mmb + ch + 1)
                bank = t["ps%d" % (2 + ch % 2)]
                ap = mk.A(buf + ch * ncols + tti * TTW, TTW)
                e.tensor_mul(ap, ap, bank[:, 0:TTW])
                e.drain().then_inc(S["ve"], 1)
        mk.sec("vector", fve)
        mk.n["ve"] += nch


def p4_l1_ffn(mk):
    S, t = mk.S, mk.t
    sfm = 0
    stm = 1024
    yb = 2176
    srep = 18560
    sel = 19584
    cB = 20608

    def ev(e, i, j, tt, ps):
        e.activation(t["arena"][0:16, sfm + tt * TTW:sfm + (tt + 1) * TTW],
                     ps[0:16, 0:TTW], AF.Identity, bias=mk.vec(VRB1, pn=16))
    mk.proj(t["rw1"].ap(), DM, [(0, 16)], (0, 1),
            lambda kc, tt: mk.xap(kc, tt * TTW), ev, eng="scalar")
    mk.barrier()

    mmb, veb = mk.n["mm"], mk.n["ve"]
    def fte(e):
        for g in range(8):
            if g >= 2:
                e.wait_ge(S["ve"], veb + g - 1)
            bank = t["ps%d" % (2 + g % 2)]
            e.transpose(bank[0:128, 0:16],
                        t["arena"][0:16, sfm + g * 128:sfm + (g + 1) * 128],
                        mk.cst(CST_ID, 16, pn=16))
            e.drain().then_inc(S["mm"], 1)
    mk.sec("tensor", fte)
    mk.n["mm"] += 8

    def fve(e):
        for g in range(8):
            e.wait_ge(S["mm"], mmb + g + 1)
            bank = t["ps%d" % (2 + g % 2)]
            e.tensor_copy(mk.A(stm + g * 16, 16), bank[:, 0:16])
            e.drain().then_inc(S["ve"], 1)
    mk.sec("vector", fve)
    mk.n["ve"] += 8

    for g in range(8):
        veb_g = mk.n["ve"]
        scb_pre = mk.n["sc"]
        def fva(e, g=g, scb_pre=scb_pre):
            if g >= 1:
                e.wait_ge(S["sc"], scb_pre)
            e.drain()
            ap = mk.A(stm + g * 16, 16)
            e.reduce_max(t["sm2"][:, 0:1], ap, axis=AXX)
            e.drain()
            e.tensor_scalar_mul(t["sm2"][:, 1:2], t["sm2"][:, 0:1], -1.0)
            e.drain().then_inc(S["ve"], 1)
        mk.sec("vector", fva)
        mk.n["ve"] += 1
        veb_g2 = mk.n["ve"]
        scb_g = mk.n["sc"]
        def fsa(e, g=g, veb_g2=veb_g2):
            e.wait_ge(S["ve"], veb_g2)
            ap = mk.A(stm + g * 16, 16)
            e.activation(ap, ap, AF.Exp, bias=t["sm2"][:, 1:2],
                         accum_out=t["sm2"][:, 2:3])
            e.drain().then_inc(S["sc"], 1)
        mk.sec("scalar", fsa)
        mk.n["sc"] += 1
        def fvb(e, g=g, scb_g=scb_g):
            e.wait_ge(S["sc"], scb_g + 1)
            e.drain()
            ap = mk.A(stm + g * 16, 16)
            e.reciprocal(t["sm2"][:, 3:4], t["sm2"][:, 2:3])
            e.drain()
            e.tensor_single_scalar(ap, ap, t["sm2"][:, 3:4], OP.mult)
            e.drain().then_inc(S["ve"], 1)
        mk.sec("vector", fvb)
        mk.n["ve"] += 1

    sfm2 = stm + 128
    veb4 = mk.n["ve"]
    mmb4 = mk.n["mm"]
    def fteb(e):
        e.wait_ge(S["ve"], veb4)
        for g in range(8):
            if g >= 2:
                e.wait_ge(S["ve"], veb4 + g - 1)
            bank = t["ps%d" % (6 + g % 2)]
            e.transpose(bank[0:16, 0:128], mk.A(stm + g * 16, 16),
                        mk.cst(CST_ID, 128))
            e.drain().then_inc(S["mm"], 1)
    mk.sec("tensor", fteb)
    mk.n["mm"] += 8

    def fveb(e):
        for g in range(8):
            e.wait_ge(S["mm"], mmb4 + g + 1)
            bank = t["ps%d" % (6 + g % 2)]
            e.tensor_copy(mk.A(sfm2 + g * 128, 128, pn=16), bank[0:16, 0:128])
            e.drain().then_inc(S["ve"], 1)
    mk.sec("vector", fveb)
    mk.n["ve"] += 8

    veb5b = mk.n["ve"]
    def fgs(g_):
        g_.wait_ge(S["ve"], veb5b)
        g_.dma_start(out=t["agsi"].ap(), in_=mk.A(sfm2, NTOK, pn=16)).then_inc(S["dg"], 16)
    mk.sec("gpsimd", fgs)
    mk.n["dg"] += 16
    dgt = mk.n["dg"]
    def fca(g_):
        g_.wait_ge(S["dg"], dgt)
        g_.collective_compute("AllGather", OP.bypass,
                              replica_groups=[list(range(NCORE))],
                              ins=[t["agsi"].ap().opt()],
                              outs=[t["agso"].ap().opt()]).then_inc(S["cc"], 1)
    mk.sec("gpsimd", fca)
    mk.n["cc"] += 1
    ccs = mk.n["cc"]

    def evy(e, i, j, tt, ps):
        e.activation(mk.A(yb + j * NTOK + tt * TTW, TTW), ps, AF.Identity,
                     bias=mk.vec(VEB + j))
    mk.proj(t["ew"].ap(), DM, MTS16, (0, 1),
            lambda kc, tt: mk.xap(kc, tt * TTW), evy, eng="scalar")
    mk.barrier()

    dkb = mk.n["dx"]
    def fsr(e):
        e.wait_ge(S["cc"], ccs)
        e.dma_start(out=mk.A(srep, NTOK), in_=t["agso"].ap()).then_inc(S["dx"], 16)
    mk.sec("sync", fsr)
    mk.n["dx"] += 16

    def fvi(e):
        e.wait_ge(S["dx"], dkb + 16)
        e.memset(t["sm2"][0:16, 20:21], 0.0)
        e.memset(t["sm2"][0:16, 21:22], 1.0)
        e.memset(t["sm2"][0:16, 22:23], 0.5)
        e.drain().then_inc(S["ve"], 1)
    mk.sec("vector", fvi)
    mk.n["ve"] += 1

    for it in range(38):
        veb_i = mk.n["ve"]
        def fti(e, veb_i=veb_i):
            e.wait_ge(S["ve"], veb_i)
            e.matmul(t["ps7"][:, 0:1], mk.cst(CST_REPR, 128, pn=16),
                     t["sm2"][0:16, 22:23], start=True, stop=True)
            e.drain().then_inc(S["mm"], 1)
        mk.sec("tensor", fti)
        mk.n["mm"] += 1
        mmb_i = mk.n["mm"]
        def fvj(e, mmb_i=mmb_i):
            e.wait_ge(S["mm"], mmb_i)
            e.drain()
            e.tensor_copy(t["sm2"][:, 24:25], t["ps7"][:, 0:1])
            e.drain()
            e.tensor_single_scalar(mk.A(sel, NTOK), mk.A(srep, NTOK),
                                   t["sm2"][:, 24:25], OP.subtract)
            e.drain()
            e.tensor_scalar_max(mk.A(sel, NTOK), mk.A(sel, NTOK), 0.0)
            e.drain()
            e.tensor_scalar_mul(mk.A(sel, NTOK), mk.A(sel, NTOK), 1e30)
            e.drain()
            e.tensor_scalar(mk.A(sel, NTOK), mk.A(sel, NTOK), 1.0, None,
                            OP.min, OP.add, accum_out=t["sm2"][:, 25:26])
            e.drain().then_inc(S["ve"], 1)
        mk.sec("vector", fvj)
        mk.n["ve"] += 1
        veb_j = mk.n["ve"]
        def ftj(e, veb_j=veb_j):
            e.wait_ge(S["ve"], veb_j)
            e.matmul(t["ps6"][0:16, 0:1], mk.cst(CST_FOLDF, 16),
                     t["sm2"][:, 25:26], start=True, stop=True)
            e.drain().then_inc(S["mm"], 1)
        mk.sec("tensor", ftj)
        mk.n["mm"] += 1
        mmb_j = mk.n["mm"]
        def fvk(e, mmb_j=mmb_j):
            e.wait_ge(S["mm"], mmb_j)
            sm = t["sm2"]
            e.drain()
            e.tensor_scalar_sub(sm[0:16, 26:27], t["ps6"][0:16, 0:1], 511.0)
            e.drain()
            e.tensor_scalar_max(sm[0:16, 26:27], sm[0:16, 26:27], 0.0)
            e.drain()
            e.tensor_scalar_min(sm[0:16, 26:27], sm[0:16, 26:27], 1.0)
            e.drain()
            e.tensor_sub(sm[0:16, 27:28], sm[0:16, 22:23], sm[0:16, 20:21])
            e.drain()
            e.tensor_mul(sm[0:16, 27:28], sm[0:16, 27:28], sm[0:16, 26:27])
            e.drain()
            e.tensor_add(sm[0:16, 20:21], sm[0:16, 20:21], sm[0:16, 27:28])
            e.drain()
            e.tensor_sub(sm[0:16, 28:29], sm[0:16, 21:22], sm[0:16, 22:23])
            e.drain()
            e.tensor_mul(sm[0:16, 28:29], sm[0:16, 28:29], sm[0:16, 26:27])
            e.drain()
            e.tensor_add(sm[0:16, 21:22], sm[0:16, 22:23], sm[0:16, 28:29])
            e.drain()
            e.tensor_add(sm[0:16, 22:23], sm[0:16, 20:21], sm[0:16, 21:22])
            e.drain()
            e.tensor_scalar_mul(sm[0:16, 22:23], sm[0:16, 22:23], 0.5)
            e.drain().then_inc(S["ve"], 1)
        mk.sec("vector", fvk)
        mk.n["ve"] += 1

    veb6 = mk.n["ve"]
    def ftf(e):
        e.wait_ge(S["ve"], veb6)
        e.matmul(t["ps7"][:, 0:1], mk.cst(CST_REPR, 128, pn=16),
                 t["sm2"][0:16, 20:21], start=True, stop=True)
        e.drain().then_inc(S["mm"], 1)
    mk.sec("tensor", ftf)
    mk.n["mm"] += 1
    mmb_f = mk.n["mm"]
    def fvf(e):
        e.wait_ge(S["mm"], mmb_f)
        e.drain()
        e.tensor_copy(t["sm2"][:, 24:25], t["ps7"][:, 0:1])
        e.drain()
        e.tensor_single_scalar(mk.A(sel, NTOK), mk.A(srep, NTOK),
                               t["sm2"][:, 24:25], OP.subtract)
        e.drain()
        e.tensor_scalar_max(mk.A(sel, NTOK), mk.A(sel, NTOK), 0.0)
        e.drain()
        e.tensor_scalar_mul(mk.A(sel, NTOK), mk.A(sel, NTOK), 1e30)
        e.drain()
        e.tensor_scalar_min(mk.A(sel, NTOK), mk.A(sel, NTOK), 1.0)
        e.drain()
        e.tensor_mul(mk.A(sel, NTOK), mk.A(sel, NTOK), mk.A(srep, NTOK))
        e.drain().then_inc(S["ve"], 1)
    mk.sec("vector", fvf)
    mk.n["ve"] += 1

    veb7 = mk.n["ve"]
    def ftg(e):
        e.wait_ge(S["ve"], veb7)
        for tt in range(2):
            e.matmul(t["ps%d" % (6 + tt)][0:8, 0:TTW], mk.cst(CST_FOLDH, 8),
                     mk.A(sel + tt * TTW, TTW), start=True, stop=True)
            e.drain().then_inc(S["mm"], 1)
    mk.sec("tensor", ftg)
    mk.n["mm"] += 2
    mmb_g = mk.n["mm"]
    def fvg(e):
        for tt in range(2):
            e.wait_ge(S["mm"], mmb_g - 2 + tt + 1)
            e.tensor_copy(t["sm0"][0:8, tt * 512:(tt + 1) * 512],
                          t["ps%d" % (6 + tt)][0:8, 0:TTW])
        e.drain().then_inc(S["ve"], 1)
    mk.sec("vector", fvg)
    mk.n["ve"] += 1
    veb8 = mk.n["ve"]
    def fth(e):
        e.wait_ge(S["ve"], veb8)
        for tt in range(2):
            e.matmul(t["ps%d" % (2 + tt)][:, 0:TTW], mk.cst(CST_SELC, 128, pn=8),
                     t["sm0"][0:8, tt * 512:(tt + 1) * 512], start=True, stop=True)
            e.drain().then_inc(S["mm"], 1)
    mk.sec("tensor", fth)
    mk.n["mm"] += 2
    mmb_h = mk.n["mm"]
    def fvh(e):
        for tt in range(2):
            e.wait_ge(S["mm"], mmb_h - 2 + tt + 1)
            e.tensor_copy(mk.A(cB + tt * TTW, TTW), t["ps%d" % (2 + tt)][:, 0:TTW])
        e.drain().then_inc(S["ve"], 1)
    mk.sec("vector", fvh)
    mk.n["ve"] += 1

    veb9 = mk.n["ve"]
    def fvz(e):
        e.wait_ge(S["ve"], veb9)
        for kc in range(NCH):
            for tt in range(2):
                dst = mk.xap(kc, tt * TTW)
                tmp = mk.A(sel + tt * TTW, TTW)
                e.drain()
                e.tensor_mul(tmp, mk.A(yb + kc * NTOK + tt * TTW, TTW),
                             mk.A(cB + tt * TTW, TTW))
                e.drain()
                e.tensor_add(dst, dst, tmp)
        e.drain().then_inc(S["ve"], 1)
    mk.sec("vector", fvz)
    mk.n["ve"] += 1
    mk.barrier()
    ln_full(mk, sq_off=sel)



def _host_constants():
    c = {}
    c["ident"] = np.eye(128, dtype=np.float32)
    rs = np.zeros((128, NH * NH), np.float32)
    for h in range(NH):
        rs[:, h * NH + h] = 1.0
    c["rowsel"] = rs
    hs = np.zeros((NH, NH * 128), np.float32)
    for h in range(NH):
        hs[h, h * 128:(h + 1) * 128] = 1.0
    c["hsel"] = hs
    ph = np.zeros((128, 128), np.float32)
    for blk in (0, 64):
        for m in range(32):
            ph[blk + m + 32, blk + m] = -1.0
        for m in range(32, 64):
            ph[blk + m - 32, blk + m] = 1.0
    c["permh"] = ph
    pq = np.zeros((128, 128), np.float32)
    for m in range(64, 96):
        pq[m + 32, m] = -1.0
    for m in range(96, 128):
        pq[m - 32, m] = 1.0
    c["permq"] = pq
    pi = np.zeros((128, 128), np.float32)
    for m in range(0, 128, 2):
        pi[m + 1, m] = -1.0
        pi[m, m + 1] = 1.0
    c["permi"] = pi
    ff = np.zeros((128, NH), np.float32)
    fh = np.zeros((128, NCORE), np.float32)
    for p in range(128):
        ff[p, p % 16] = 1.0
        fh[p, p // 16] = 1.0
    c["foldF"] = ff
    c["foldH"] = fh
    rp = np.zeros((NH, 128), np.float32)
    for m in range(128):
        rp[m % 16, m] = 1.0
    c["repR"] = rp
    es = np.zeros((7, 7 * 128), np.float32)
    for e in range(7):
        es[e, e * 128:(e + 1) * 128] = 1.0
    c["esel"] = es
    return c


def _rope_tables(hf):
    tl = np.arange(NTOK)
    tabs = np.array([QSTARTS[hf][i // 512] + (i % 512) for i in tl], np.float64)
    out = {}
    invf0 = 1.0 / (10000.0 ** (np.arange(0, 64, 2, dtype=np.float64) / 64.0))
    ang0 = np.zeros((128, NTOK), np.float64)
    for r in range(128):
        ang0[r] = tabs * invf0[(r % 64) % 32]
    out["l0cos"] = np.cos(ang0).astype(np.float32)
    out["l0sin"] = np.sin(ang0).astype(np.float32)
    cq = np.ones((128, NTOK), np.float64)
    sq = np.zeros((128, NTOK), np.float64)
    for r in range(64, 128):
        dd = r - 64
        cq[r] = np.cos(tabs * invf0[dd % 32])
        sq[r] = np.sin(tabs * invf0[dd % 32])
    out["l0cosq"] = cq.astype(np.float32)
    out["l0sinq"] = sq.astype(np.float32)
    invf1 = 1.0 / (10000.0 ** (np.arange(0, 128, 2, dtype=np.float64) / 128.0))
    ang1 = np.zeros((128, NTOK), np.float64)
    for r in range(128):
        ang1[r] = tabs * invf1[r // 2]
    out["l1cos"] = np.cos(ang1).astype(np.float32)
    out["l1sin"] = np.sin(ang1).astype(np.float32)
    return out


def _masks(hf):
    m = np.zeros((6, 128, 4, TTW), np.float32)
    for s, (qc, kb) in enumerate(SLOTS):
        q0 = QSTARTS[hf][qc]
        for kc in range(4):
            k0 = kb * 512 + kc * 128
            kk = k0 + np.arange(128)[:, None]
            qq = q0 + np.arange(TTW)[None, :]
            m[s, :, kc, :] = (qq >= kk).astype(np.float32)
    return m.reshape(6 * 128, 4 * TTW)


def _vec_pd(v, nch):
    return np.ascontiguousarray(np.asarray(v).reshape(nch, 128).T.astype(np.float32))


def kernel(**inputs):
    ii = {k: np.asarray(v) for k, v in inputs.items()}
    x = ii["L0_attn_hidden_states"]

    shared = {}
    shared["wq0"] = np.ascontiguousarray(ii["L0_attn_Wq"].T)
    shared["wk0"] = np.ascontiguousarray(ii["L0_attn_Wk"].T)
    shared["wv0"] = np.ascontiguousarray(ii["L0_attn_Wv"].T)
    shared["wo0"] = np.ascontiguousarray(ii["L0_attn_Wo"].T)
    shared["kvdw"] = ii["L0_ffn_kv_d_w"]
    shared["qdw"] = ii["L0_ffn_q_d_w"]
    qu = ii["L0_ffn_q_u_w"]; rq = ii["L0_ffn_rope_q_w"]
    wqasm = np.zeros((512, DM), np.float32)
    for h in range(NH):
        wqasm[:, h * 128:h * 128 + 64] = qu[:, h * 64:(h + 1) * 64]
        wqasm[:, h * 128 + 64:h * 128 + 128] = rq[:, h * 64:(h + 1) * 64]
    shared["wqasm"] = wqasm
    shared["kuw"] = ii["L0_ffn_k_u_w"]
    shared["ropekw"] = ii["L0_ffn_rope_k_w"]
    shared["vuw"] = ii["L0_ffn_v_u_w"]
    shared["ow"] = ii["L0_ffn_o_w"]
    shared["shg"] = ii["L0_ffn_sh_gate"]
    shared["shu"] = ii["L0_ffn_sh_up"]
    shared["shd"] = ii["L0_ffn_sh_down"]
    shared["rg"] = ii["L0_ffn_r_gate"].reshape(7 * DM, 1024)
    shared["ru"] = ii["L0_ffn_r_up"].reshape(7 * DM, 1024)
    shared["rd"] = ii["L0_ffn_r_down"].reshape(7 * 1024, DM)
    shared["rw0"] = ii["L0_ffn_router_w"]
    shared["wq1"] = ii["L1_attn_Wq"]
    shared["wk1"] = ii["L1_attn_Wk"]
    shared["wv1"] = ii["L1_attn_Wv"]
    shared["wo1"] = ii["L1_attn_Wo"]
    shared["rw1"] = ii["L1_ffn_router_w"]
    shared["ew"] = ii["L1_ffn_expert_w"]
    vecs = np.zeros((128, VNVEC), np.float32)
    vecs[:, VGAM] = ii["L0_attn_gamma"]
    vecs[:, VBET] = ii["L0_attn_beta"]
    vecs[:, VL0W1:VL0W1 + NCH] = _vec_pd(ii["L0_ffn_ln1_w"], NCH)
    vecs[:, VL0W2:VL0W2 + NCH] = _vec_pd(ii["L0_ffn_ln2_w"], NCH)
    vecs[:, VBQ1:VBQ1 + NCH] = _vec_pd(ii["L1_attn_bq"], NCH)
    vecs[:, VBK1:VBK1 + 4] = _vec_pd(ii["L1_attn_bk"], 4)
    vecs[:, VBV1:VBV1 + 4] = _vec_pd(ii["L1_attn_bv"], 4)
    vecs[:, VBO1:VBO1 + NCH] = _vec_pd(ii["L1_attn_bo"], NCH)
    vecs[:, VEB:VEB + NCH] = _vec_pd(ii["L1_ffn_expert_b"], NCH)
    vecs[0:NH, VALP] = ii["L1_attn_alpha"]
    vecs[0:7, VRB0] = ii["L0_ffn_router_b"]
    vecs[0:NH, VRB1] = ii["L1_ffn_router_b"]
    shared["vecs"] = vecs
    shared.update(_host_constants())
    shared = {k: np.ascontiguousarray(v, dtype=np.float32) for k, v in shared.items()}

    in_maps = []
    for c in range(NCORE):
        b, hf = c // 2, c % 2
        im = dict(shared)
        im["xs"] = np.ascontiguousarray(
            np.concatenate([x[b, s:s + 512] for s in QSTARTS[hf]], axis=0),
            dtype=np.float32)
        im.update({k: np.ascontiguousarray(v) for k, v in _rope_tables(hf).items()})
        im["masks"] = _masks(hf)
        sc = np.zeros((NCORE, 128), np.float32)
        sc[c, :] = 1.0
        im["selc"] = sc
        in_maps.append(im)

    nc = build()
    res = run_bass_kernel_spmd(nc, in_maps, core_ids=list(range(NCORE)))
    out = np.zeros((4, T, DM), np.float32)
    for c in range(NCORE):
        b, hf = c // 2, c % 2
        y = np.asarray(res.results[c]["yout"])
        for j, s in enumerate(QSTARTS[hf]):
            out[b, s:s + 512] = y[j * 512:(j + 1) * 512]
    return out
